# version 16
# baseline (speedup 1.0000x reference)
import numpy as np
import ml_dtypes
import concourse.bacc as bacc
import concourse.mybir as mybir
from concourse.bass_utils import run_bass_kernel_spmd

F32 = mybir.dt.float32
BF16 = mybir.dt.bfloat16
ALU = mybir.AluOpType
AF = mybir.ActivationFunctionType
AX = mybir.AxisListType

MAGIC = 12582912.0
SCALE = float(1.0 / np.sqrt(128.0))
EPS_LN = 1e-5
INV127 = float(np.float32(1.0) / np.float32(127.0))
EPS_Q = 1e-08
PI = float(np.pi)
TWO_PI = float(2.0 * np.pi)
LN1E4 = float(np.log(10000.0))
NEGF = -3.0e38
POSF = 3.0e38

C = 2048
TT = 2048
NG = 4
GT = TT // NG
NTI = GT // 128
I_FFN = 5632
NI = I_FFN // 128
NH, HD, NL, HT = 16, 128, 64, 64
N1B = 2 * I_FFN // 128


def _register_rint_op():
    import concourse.dve_ops as dve_ops
    from concourse.dve_op import DveOp
    from concourse.dve_spec import Spec, Src0, Src1, C0, lower
    from concourse.dve_table_gen import dve_ver_for
    for op in dve_ops.OPS:
        if op.name == "RINT_SCALE_TT":
            return op
    name = "RINT_SCALE_TT"
    spec = Spec(body=(Src0 * Src1 + C0) - C0)
    ver = dve_ver_for("TRN2")
    compiled = lower(spec, ver=ver, perf_en=False, name=name)
    op = DveOp(name, spec, subdim=False, uops_sha={ver: compiled.sha(ver)},
               reference=lambda in0, in1, s0, s1, imm2: np.rint(in0 * in1))
    row = len(dve_ops.OPS) + dve_ops._CUSTOM_DVE_ROW_BASE
    assert row <= 31, "no free custom-DVE opcode row"
    dve_ops.OPS.append(op)
    dve_ops._SUB_OPCODE_FOR_NAME[name] = row
    dve_ops.CUSTOM_DVE_SPECS[name] = spec
    dve_ops.CUSTOM_DVE_REFS[name] = op.ref
    return op


RINT = _register_rint_op()


class P:
    def __init__(self, nc):
        self.nc = nc
        self.steps = []
        self.count = {}

    def sem(self, name):
        s = self.nc.alloc_semaphore(name)
        self.count[s] = 0
        return s

    def step(self, eng, fn, waits=(), inc=None):
        incs = []
        if inc is not None:
            if not isinstance(inc, (list, tuple)):
                inc = [(inc, 1)]
            for s, n in inc:
                self.count[s] += n
                incs.append((s, n))

        def run(e, fn=fn, waits=tuple(waits), incs=tuple(incs)):
            for s, v in waits:
                if v > 0:
                    e.wait_ge(s, v)
            last = fn(e)
            if incs:
                if last is None:
                    last = e.drain() if hasattr(e, "drain") else e.nop()
                for s, n in incs:
                    last.then_inc(s, n)
        self.steps.append((eng, run))
        return self.count[incs[0][0]] if incs else None

    def v(self, s):
        return self.count[s]

    def emit(self, block):
        steps = self.steps

        def mk(name):
            def f(eng):
                for e, run in steps:
                    if e == name:
                        run(eng)
            return f
        block.sync(mk("sync"))
        block.scalar(mk("scalar"))
        block.vector(mk("vector"))
        block.tensor(mk("tensor"))
        block.gpsimd(mk("gpsimd"))


class Slots:
    def __init__(self, p, name, n):
        self.p, self.n = p, n
        self.ld = [p.sem(f"{name}_ld{j}") for j in range(n)]
        self.fr = [p.sem(f"{name}_fr{j}") for j in range(n)]
        self.il = 0
        self.iu = 0

    def load(self, queue, dst_fn, src_ap, extra_waits=()):
        j = self.il % self.n
        r = self.il // self.n
        waits = list(extra_waits)
        if r > 0:
            waits.append((self.fr[j], r))
            waits.append((self.ld[j], 16 * r))
        self.p.step(queue,
                    lambda e, j=j, src_ap=src_ap, dst_fn=dst_fn:
                        e.dma_start(out=dst_fn(j), in_=src_ap),
                    waits=waits, inc=[(self.ld[j], 16)])
        self.il += 1
        return j

    def wait_of_next(self):
        j = self.iu % self.n
        return (self.ld[j], 16 * (self.iu // self.n + 1))

    def slot(self):
        return self.iu % self.n

    def done(self, eng):
        j = self.iu % self.n
        self.p.step(eng, lambda e: None, inc=[(self.fr[j], 1)])
        self.iu += 1


def build_program():
    nc = bacc.Bacc("TRN2", target_bir_lowering=False)

    x_in = nc.dram_tensor("x", [TT, C], F32, kind="ExternalInput").ap()
    xh_in = nc.dram_tensor("xh", [HT, C], F32, kind="ExternalInput").ap()
    lat_in = nc.dram_tensor("lat", [NL, C], F32, kind="ExternalInput").ap()
    wk_in = nc.dram_tensor("wk", [C, C], F32, kind="ExternalInput").ap()
    wv_in = nc.dram_tensor("wv", [C, C], F32, kind="ExternalInput").ap()
    wg_in = nc.dram_tensor("wg", [C, NH * NL], F32, kind="ExternalInput").ap()
    wp_in = nc.dram_tensor("wp", [C, C], F32, kind="ExternalInput").ap()
    w1_in = nc.dram_tensor("w1", [2 * I_FFN, C], F32, kind="ExternalInput").ap()
    w2_in = nc.dram_tensor("w2", [C, I_FFN], F32, kind="ExternalInput").ap()
    out_hbm = nc.dram_tensor("out", [TT, C], F32, kind="ExternalOutput").ap()

    w1qT_hbm = nc.dram_tensor("w1qT", [N1B, 128, C], BF16, kind="Internal").ap()
    w2qT_hbm = nc.dram_tensor("w2qT", [16, 128, I_FFN], BF16, kind="Internal").ap()
    wpS_hbm = nc.dram_tensor("wpS", [16, 128, C], BF16, kind="Internal").ap()
    wgS_hbm = nc.dram_tensor("wgS", [C, NH * NL], BF16, kind="Internal").ap()

    id_bf_h = nc.inline_tensor(np.eye(128, dtype=ml_dtypes.bfloat16), "id_bf_h").ap()
    id_f32_h = nc.inline_tensor(np.eye(128, dtype=np.float32), "id_f32_h").ap()
    maskv = np.where(np.arange(HT)[None, :] <= np.arange(NL)[:, None],
                     np.float32(0.0), np.float32(-1000000000.0)).astype(np.float32)
    mask_h = nc.inline_tensor(maskv, "cmask").ap()
    tpart_h = nc.inline_tensor(
        np.broadcast_to(np.arange(HT, dtype=np.float32)[:, None],
                        (HT, HT)).copy(), "tpart_h").ap()
    fidx_h = nc.inline_tensor(
        np.broadcast_to(np.arange(HT, dtype=np.float32)[None, :],
                        (HT, HT)).copy(), "fidx_h").ap()
    ones_h = nc.inline_tensor(np.ones((1, 128), dtype=np.float32), "ones_h").ap()

    p = P(nc)

    id_bf = nc.alloc_sbuf_tensor("id_bf", [128, 128], BF16)
    id_f32 = nc.alloc_sbuf_tensor("id_f32", [128, 128], F32)
    attn_sb = nc.alloc_sbuf_tensor("attn_sb", [128, 8 * HD], BF16)
    xqT = nc.alloc_sbuf_tensor("xqT", [128, 16 * GT], BF16)
    wtd = nc.alloc_sbuf_tensor("wtd", [128, 16 * GT], BF16)
    colp = nc.alloc_sbuf_tensor("colp", [128, 24], F32)
    bn6 = nc.alloc_sbuf_tensor("bn6", [128, 24], F32)
    mv2 = nc.alloc_sbuf_tensor("mv2", [128, 2], F32)
    ones_sb = nc.alloc_sbuf_tensor("ones_sb", [1, 128], F32)
    sw1_all = nc.alloc_sbuf_tensor("sw1_all", [128, 2 * NI], F32)
    sw2_all = nc.alloc_sbuf_tensor("sw2_all", [128, 16], F32)

    ps_f32 = nc.alloc_psum_tensor("ps_f32", [128, 3072], F32)
    ps_bf = nc.alloc_psum_tensor("ps_bf", [128, 2048], BF16)

    def psf(bank, w=512, off=0):
        return ps_f32[:, bank * 512 + off: bank * 512 + off + w]

    def psb(bank, w=1024):
        return ps_bf[:, bank * 1024: bank * 1024 + w]

    sP = p.sem("sP")
    sV = p.sem("sV")
    sS = p.sem("sS")
    sG = p.sem("sG")
    sQs = p.sem("sQs")
    sQid = p.sem("sQid")
    sOut = p.sem("sOut")
    sW1w = [p.sem(f"sW1w{i}") for i in range(4)]
    sW2w = p.sem("sW2w")
    sWpw = p.sem("sWpw")
    sWgw = p.sem("sWgw")
    sXg = p.sem("sXg")
    sWgsL = p.sem("sWgsL")

    psb_guard = {0: (None, 0), 1: (None, 0)}
    wtr_guard = {0: None, 1: None}

    def bcast(col_ap, n):
        return col_ap.broadcast_to([col_ap.shape[0], n])

    ARENA_B = (nc.sbuf_top - nc.sbuf_base - 64) // 32 * 32
    arena_h = nc.sbuf_tensor("arena", [128, ARENA_B], mybir.dt.uint8).__enter__()
    AB = nc.lookup_mloc(arena_h).addr

    def _bytes(shape, dt):
        n = 1
        for d in shape[1:]:
            n *= d
        return n * mybir.dt.size(dt)

    def sb_at(off, name, shape, dt):
        assert off % 32 == 0 and off + _bytes(shape, dt) <= ARENA_B, \
            (name, off, _bytes(shape, dt), ARENA_B)
        return nc.alloc_sbuf_tensor_at(name, shape, dt, offset=AB + off)

    _cur = [0]

    def sbuf(name, shape, dt, side="left"):
        off = (_cur[0] + 31) // 32 * 32
        _cur[0] = off + _bytes(shape, dt)
        return sb_at(off, name, shape, dt)

    def scope_mark():
        return _cur[0]

    def scope_close(mark):
        _cur[0] = mark

    def tr8(src_fn, nblk, n_rows, bank, waits):
        w = list(waits) + [psb_guard[bank]]

        def f(e, nblk=nblk, n_rows=n_rows, bank=bank, src_fn=src_fn):
            for b in range(nblk):
                e.transpose(psb(bank)[0:128, b * 128: b * 128 + n_rows],
                            src_fn(b), id_bf[0:n_rows, 0:n_rows])
            return None
        return p.step("tensor", f, waits=w, inc=sP)

    def tr8_evac(dst_ap, nblk, n_rows, bank, pe_t):
        def f(e, dst_ap=dst_ap, nblk=nblk, n_rows=n_rows, bank=bank):
            srcs = psb(bank)[:].rearrange("p (b w) -> p b w", w=128)[
                :, 0:nblk, 0:n_rows]
            e.tensor_copy(dst_ap, srcs)
            e.drain()
            return None
        t = p.step("vector", f, waits=[(sP, pe_t)], inc=sV)
        psb_guard[bank] = (sV, t)
        return t

    def tr8_evac_s(dst_ap, nblk, n_rows, bank, pe_t):
        def f(e, dst_ap=dst_ap, nblk=nblk, n_rows=n_rows, bank=bank):
            srcs = psb(bank)[:].rearrange("p (b w) -> p b w", w=128)[
                :, 0:nblk, 0:n_rows]
            e.activation(out=dst_ap, in_=srcs, func=AF.Copy)
            return None
        t = p.step("scalar", f, waits=[(sP, pe_t)], inc=sS)
        psb_guard[bank] = (sS, t)
        return t

    p.step("scalar", lambda e: e.dma_start(out=id_bf[:], in_=id_bf_h[:, :]),
           inc=[(sQid, 16)])
    p.step("scalar", lambda e: e.dma_start(out=id_f32[:], in_=id_f32_h[:, :]),
           inc=[(sQid, 16)])
    qs_consts = p.v(sQid)

    m_p0 = scope_mark()
    xh_sb = sbuf("xh_sb", [HT, C], F32)
    lat_sb = sbuf("lat_sb", [NL, C], F32)
    mask_sb = sbuf("mask_sb", [NL, HT], F32)
    xnh = sbuf("xnh", [HT, C], F32)
    xnh_bf = sbuf("xnh_bf", [HT, C], BF16)
    xnhT = sbuf("xnhT", [128, 16 * HT], BF16)
    lat_bf = sbuf("lat_bf", [NL, C], BF16)
    latT = sbuf("latT", [128, 16 * NL], BF16)
    kvf = sbuf("kvf", [128, 2 * C], F32)
    kvb = sbuf("kvb", [128, 2 * C], BF16)
    k_tm = sbuf("k_tm", [HT, C], F32)
    v_tm = sbuf("v_tm", [HT, C], F32)
    k_bf = sbuf("k_bf", [HT, C], BF16)
    v_bf = sbuf("v_bf", [HT, C], BF16)
    kT = sbuf("kT", [128, 16 * HT], BF16)
    rt_t = sbuf("rt_t", [HT, HT], F32)
    rt_i = sbuf("rt_i", [HT, HT], F32)
    rt_k = sbuf("rt_k", [HT, HT], F32)
    cos_t = sbuf("cos_t", [HT, HT], F32)
    sin_t = sbuf("sin_t", [HT, HT], F32)
    re1 = sbuf("re1", [HT, C // 2], F32)
    re2 = sbuf("re2", [HT, C // 2], F32)
    re3 = sbuf("re3", [HT, C // 2], F32)
    sc_sb = sbuf("sc_sb", [NL, 2 * HT], F32)
    w_bf = sbuf("w_bf", [NL, 2 * HT], BF16)
    wT_bf = sbuf("wT_bf", [HT, 2 * NL], BF16)
    wpf = sbuf("wpf", [128, 2 * C], F32)
    wqb = sbuf("wqb", [128, 2 * C], BF16)
    wtr = sbuf("wtr", [128, 2 * C], BF16)
    wpfp = sbuf("wpfp", [128, 2 * C], F32)
    wtrp = sbuf("wtrp", [128, 2 * C], BF16)
    wgf = sbuf("wgf", [128, 2 * NH * NL], F32)
    wgb = sbuf("wgb", [128, 2 * NH * NL], BF16)

    def f(e):
        e.memset(colp[0:1, 23:24], 0.25)
        e.drain()
        return None
    v_seed = p.step("vector", f, inc=sV)

    def f(e):
        for fn in (AF.Copy, AF.Identity, AF.Sqrt, AF.Exp, AF.Sin, AF.Silu):
            e.activation(out=colp[0:1, 22:23], in_=colp[0:1, 23:24], func=fn)
            e.drain()
        return None
    p.step("scalar", f, waits=[(sV, v_seed)], inc=sS)

    p.step("scalar", lambda e: e.dma_start(out=xh_sb[:], in_=xh_in[:, :]),
           inc=[(sQs, 16)])
    p.step("scalar", lambda e: e.dma_start(out=lat_sb[:], in_=lat_in[:, :]),
           inc=[(sQs, 16)])
    p.step("scalar", lambda e: e.dma_start(out=mask_sb[:], in_=mask_h[:, :]),
           inc=[(sQs, 16)])
    qs_head = p.v(sQs)

    wst = Slots(p, "wst", 2)
    wgst = Slots(p, "wgst", 2)
    wpst = Slots(p, "wpst", 2)
    wg_slot_guard = {0: 0, 1: 0}
    wp_slot_guard = {0: 0, 1: 0}
    GW = NH * NL

    def emit_wg_chunk(i):
        wgst.load("sync", lambda j: wgf[:, j * GW: (j + 1) * GW],
                  wg_in[i * 128:(i + 1) * 128, :])
        wsem, wval = wgst.wait_of_next()
        j = wgst.slot()

        def f(e, j=j):
            e.tensor_scalar(out=wgb[:, j * GW:(j + 1) * GW],
                            in0=wgf[:, j * GW:(j + 1) * GW],
                            scalar1=SCALE, scalar2=None, op0=ALU.mult)
            e.drain()
            return None
        v_c = p.step("vector", f,
                     waits=[(wsem, wval), (sWgw, wg_slot_guard[j])], inc=sV)
        wgst.done("vector")
        p.step("gpsimd",
               lambda e, i=i, j=j: e.dma_start(
                   out=wgS_hbm[i * 128:(i + 1) * 128, :],
                   in_=wgb[:, j * GW:(j + 1) * GW]),
               waits=[(sV, v_c), (sWgw, p.v(sWgw))], inc=[(sWgw, 16)])
        wg_slot_guard[j] = p.v(sWgw)

    def emit_wp_chunk(ic):
        wpst.load("sync", lambda j: wpfp[:, j * C:(j + 1) * C],
                  wp_in[ic * 128:(ic + 1) * 128, :])
        wsem, wval = wpst.wait_of_next()
        j = wpst.slot()
        s_c = p.step("scalar",
                     lambda e, j=j: e.activation(
                         out=wtrp[:, j * C:(j + 1) * C],
                         in_=wpfp[:, j * C:(j + 1) * C], func=AF.Copy),
                     waits=[(wsem, wval), (sWpw, wp_slot_guard[j])], inc=sS)
        wpst.done("scalar")
        p.step("gpsimd",
               lambda e, ic=ic, j=j: e.dma_start(
                   out=wpS_hbm[ic, :, :],
                   in_=wtrp[:, j * C:(j + 1) * C]),
               waits=[(sS, s_c), (sWpw, p.v(sWpw))], inc=[(sWpw, 16)])
        wp_slot_guard[j] = p.v(sWpw)


    trslot = [0]
    state = {"tr": {0: 0, 1: 0}}

    def prep_chunk(src_row_ap, W, dst_img_row, done_sem, cpar,
                   b_wpf, b_wqb, b_wtr, scol=None, extra_load_waits=()):
        base = 8 + 4 * cpar
        wst.load("sync", lambda j: b_wpf[:, j * W: j * W + W], src_row_ap,
                 extra_waits=extra_load_waits)
        wsem, wval = wst.wait_of_next()
        j = wst.slot()

        def f(e, j=j, W=W, base=base, cpar=cpar, b_wpf=b_wpf, b_wqb=b_wqb,
              scol=scol):
            e.tensor_reduce(out=colp[:, base:base + 1],
                            in_=b_wpf[:, j * W: j * W + W],
                            axis=AX.X, op=ALU.max, apply_absolute_value=True)
            e.drain()
            e.reciprocal(colp[:, base + 2:base + 3], colp[:, base:base + 1])
            e.drain()
            e.tensor_scalar(out=colp[:, base + 2:base + 3],
                            in0=colp[:, base + 2:base + 3], scalar1=127.0,
                            scalar2=1e8, op0=ALU.mult, op1=ALU.min)
            e.drain()
            e.reciprocal(scol, colp[:, base + 2:base + 3])
            e.drain()
            e._custom_dve(RINT, out=b_wqb[:, cpar * W:cpar * W + W],
                          in0=b_wpf[:, j * W: j * W + W],
                          in1=bcast(colp[:, base + 2:base + 3], W), s0=MAGIC)
            e.drain()
            return None
        v_q = p.step("vector", f,
                     waits=[(wsem, wval), (sP, state["tr"][cpar])], inc=sV)
        wst.done("vector")

        ts = trslot[0]
        trslot[0] ^= 1
        nsub = W // 128
        last_s = None
        pe_t = None
        for lo in range(0, nsub, 8):
            hi = min(lo + 8, nsub)
            bank = (lo // 8) % 2
            extra = [(sV, v_q), (sQid, qs_consts)]
            if wtr_guard[ts] is not None:
                extra.append(wtr_guard[ts])
            pe_t = tr8(lambda b, lo=lo, cpar=cpar, W=W, b_wqb=b_wqb:
                       b_wqb[:, cpar * W + (lo + b) * 128:
                             cpar * W + (lo + b + 1) * 128],
                       hi - lo, 128, bank, extra)
            dst = b_wtr[:, ts * W + lo * 128: ts * W + hi * 128].rearrange(
                "p (b w) -> p b w", w=128)
            last_s = tr8_evac_s(dst, hi - lo, 128, bank, pe_t)
        state["tr"][cpar] = pe_t

        wv_t = p.step("gpsimd",
                      lambda e, ts=ts, W=W, dst_img_row=dst_img_row, b_wtr=b_wtr:
                          e.dma_start(out=dst_img_row,
                                      in_=b_wtr[:, ts * W: ts * W + W]),
                      waits=[(sS, last_s), (done_sem, p.v(done_sem))],
                      inc=[(done_sem, 16)])
        wtr_guard[ts] = (done_sem, wv_t)

    w1_order = []
    for k in range(NI):
        w1_order.append(k)
        w1_order.append(NI + k)
    for idx, blk in enumerate(w1_order):
        quarter = idx // 22
        prep_chunk(w1_in[blk * 128:(blk + 1) * 128, :], C,
                   w1qT_hbm[idx, :, :], sW1w[quarter], idx % 2,
                   wpf, wqb, wtr, scol=sw1_all[:, idx:idx + 1])
        if idx % 5 == 1 and idx // 5 < 16:
            emit_wg_chunk(idx // 5)
        if idx % 5 == 3 and idx // 5 < 16:
            emit_wp_chunk(idx // 5)
    w1_written = [p.v(s) for s in sW1w]
    wg_written = p.v(sWgw)
    wp_written = p.v(sWpw)

    def f(e):
        for q in range(4):
            e.bn_stats(bn6[0:HT, q * 6:(q + 1) * 6],
                       xh_sb[:, q * 512:(q + 1) * 512])
        e.drain()
        e.bn_aggr(mv2[0:HT, :], bn6[0:HT, :])
        e.drain()
        return None
    v_hstats = p.step("vector", f, waits=[(sQs, qs_head)], inc=sV)

    def f(e):
        e.tensor_scalar(out=colp[0:HT, 0:1], in0=mv2[0:HT, 1:2],
                        scalar1=EPS_LN, scalar2=None, op0=ALU.add)
        e.drain()
        return None
    v_heps = p.step("vector", f, waits=[(sV, v_hstats)], inc=sV)
    s_hstd = p.step("scalar",
                    lambda e: e.activation(out=colp[0:HT, 0:1], in_=colp[0:HT, 0:1],
                                           func=AF.Sqrt),
                    waits=[(sV, v_heps)], inc=sS)

    def f(e):
        e.reciprocal(colp[0:HT, 1:2], colp[0:HT, 0:1])
        e.drain()
        e.tensor_scalar(out=colp[0:HT, 2:3], in0=mv2[0:HT, 0:1],
                        scalar1=colp[0:HT, 1:2], scalar2=-1.0,
                        op0=ALU.mult, op1=ALU.mult)
        e.drain()
        return None
    v_hcol = p.step("vector", f, waits=[(sS, s_hstd)], inc=sV)

    s_xnh = p.step("scalar",
                   lambda e: e.activation(out=xnh[:], in_=xh_sb[:], func=AF.Identity,
                                          bias=colp[0:HT, 2:3],
                                          scale=colp[0:HT, 1:2]),
                   waits=[(sV, v_hcol)], inc=sS)

    def f(e):
        e.tensor_copy(xnh_bf[:], xnh[:])
        e.tensor_scalar(out=lat_bf[:], in0=lat_sb[:], scalar1=SCALE,
                        scalar2=None, op0=ALU.mult)
        e.drain()
        return None
    v_hbf = p.step("vector", f, waits=[(sS, s_xnh), (sQs, qs_head)], inc=sV)

    def tr16_into(src_sb, n_rows, dst_sb, dstw, wait_v):
        last = None
        for half in range(2):
            bank = half
            pe_t = tr8(lambda b, half=half, src_sb=src_sb, n_rows=n_rows:
                       src_sb[0:n_rows,
                              (half * 8 + b) * 128:(half * 8 + b + 1) * 128],
                       8, n_rows, bank, [(sV, wait_v), (sQid, qs_consts)])
            dst = dst_sb[:].rearrange("p (i w) -> p i w", w=dstw)[
                :, half * 8:(half + 1) * 8, 0:n_rows]
            last = tr8_evac(dst, 8, n_rows, bank, pe_t)
        return last

    v_xnhT = tr16_into(xnh_bf, HT, xnhT, HT, v_hbf)
    v_latT = tr16_into(lat_bf, NL, latT, NL, v_hbf)

    kvslots = Slots(p, "kv", 2)

    kvb_guard = {0: 0, 1: 0}

    def kv_proj(w_in, dst, wait_first):
        pe_t = None
        for hc in range(16):
            kvslots.load("sync", lambda j: kvf[:, j * C:(j + 1) * C],
                         w_in[hc * 128:(hc + 1) * 128, :])
            wsem, wval = kvslots.wait_of_next()
            j = kvslots.slot()
            s_c = p.step("scalar",
                         lambda e, j=j: e.activation(
                             out=kvb[:, j * C:(j + 1) * C],
                             in_=kvf[:, j * C:(j + 1) * C], func=AF.Copy),
                         waits=[(wsem, wval), (sP, kvb_guard[j])], inc=sS)
            kvslots.done("scalar")

            def f(e, hc=hc, j=j):
                for oc in range(4):
                    e.matmul(psf(oc)[0:HT, :],
                             xnhT[:, hc * HT:(hc + 1) * HT],
                             kvb[:, j * C + oc * 512: j * C + (oc + 1) * 512],
                             start=(hc == 0), stop=(hc == 15),
                             skip_group_check=True)
                return None
            pe_t = p.step("tensor", f, waits=[(sS, s_c), (sV, wait_first)],
                          inc=sP)
            kvb_guard[j] = pe_t

        def f(e, dst=dst):
            e.tensor_copy(dst[:], ps_f32[0:HT, 0:2048])
            e.drain()
            return None
        return p.step("vector", f, waits=[(sP, pe_t)], inc=sV)

    v_k = kv_proj(wk_in, k_tm, v_xnhT)
    v_v = kv_proj(wv_in, v_tm, v_k)

    p.step("scalar", lambda e: e.dma_start(out=rt_t[:], in_=tpart_h[:, :]),
           inc=[(sQid, 16)])
    p.step("scalar", lambda e: e.dma_start(out=rt_i[:], in_=fidx_h[:, :]),
           inc=[(sQid, 16)])
    p.step("scalar", lambda e: e.dma_start(out=ones_sb[:], in_=ones_h[:, :]),
           inc=[(sQid, 16)])
    qs_iota = p.v(sQid)

    def f(e):
        e.tensor_scalar(out=rt_i[:], in0=rt_i[:],
                        scalar1=float(-2.0 * LN1E4 / 128.0), scalar2=None,
                        op0=ALU.mult)
        e.drain()
        return None
    v_invfs = p.step("vector", f, waits=[(sQid, qs_iota)], inc=sV)
    s_invf = p.step("scalar",
                    lambda e: e.activation(out=rt_i[:], in_=rt_i[:], func=AF.Exp),
                    waits=[(sV, v_invfs)], inc=sS)

    def f(e):
        e.tensor_tensor(out=rt_t[:], in0=rt_t[:], in1=rt_i[:], op=ALU.mult)
        e.memset(rt_k[:], float(1.0 / TWO_PI))
        e.drain()
        e._custom_dve(RINT, out=rt_i[:], in0=rt_t[:], in1=rt_k[:], s0=MAGIC)
        e.drain()
        e.scalar_tensor_tensor(out=sin_t[:], in0=rt_i[:], scalar=-TWO_PI,
                               in1=rt_t[:], op0=ALU.mult, op1=ALU.add)
        e.drain()
        e.tensor_scalar(out=sin_t[:], in0=sin_t[:], scalar1=PI, scalar2=-PI,
                        op0=ALU.min, op1=ALU.max)
        e.tensor_scalar(out=rt_t[:], in0=rt_t[:], scalar1=float(PI / 2),
                        scalar2=None, op0=ALU.add)
        e.drain()
        e._custom_dve(RINT, out=rt_i[:], in0=rt_t[:], in1=rt_k[:], s0=MAGIC)
        e.drain()
        e.scalar_tensor_tensor(out=cos_t[:], in0=rt_i[:], scalar=-TWO_PI,
                               in1=rt_t[:], op0=ALU.mult, op1=ALU.add)
        e.drain()
        e.tensor_scalar(out=cos_t[:], in0=cos_t[:], scalar1=PI, scalar2=-PI,
                        op0=ALU.min, op1=ALU.max)
        e.drain()
        return None
    v_ropearg = p.step("vector", f, waits=[(sS, s_invf)], inc=sV)

    def f(e):
        e.activation(out=sin_t[:], in_=sin_t[:], func=AF.Sin)
        e.drain()
        e.activation(out=cos_t[:], in_=cos_t[:], func=AF.Sin)
        e.drain()
        return None
    s_tab = p.step("scalar", f, waits=[(sV, v_ropearg)], inc=sS)

    def f(e):
        k3 = k_tm[:].rearrange("t (h d2 two) -> t h d2 two", h=NH, two=2)
        ke = k3[:, :, :, 0:1].squeeze(3)
        ko = k3[:, :, :, 1:2].squeeze(3)
        cosb = cos_t[:].unsqueeze(1).broadcast_to([HT, NH, HT])
        sinb = sin_t[:].unsqueeze(1).broadcast_to([HT, NH, HT])
        r1 = re1[:].rearrange("t (h f) -> t h f", h=NH)
        r2 = re2[:].rearrange("t (h f) -> t h f", h=NH)
        r3 = re3[:].rearrange("t (h f) -> t h f", h=NH)
        e.tensor_tensor(out=r1, in0=ke, in1=cosb, op=ALU.mult)
        e.tensor_tensor(out=r2, in0=ko, in1=sinb, op=ALU.mult)
        e.tensor_tensor(out=r3, in0=ke, in1=sinb, op=ALU.mult)
        e.drain()
        e.tensor_sub(re1[:], re1[:], re2[:])
        e.drain()
        e.tensor_tensor(out=r2, in0=ko, in1=cosb, op=ALU.mult)
        e.drain()
        e.tensor_add(re2[:], re2[:], re3[:])
        e.drain()
        e.tensor_copy(ke, r1)
        e.tensor_copy(ko, r2)
        e.drain()
        e.tensor_copy(k_bf[:], k_tm[:])
        e.tensor_copy(v_bf[:], v_tm[:])
        e.drain()
        return None
    v_rope = p.step("vector", f, waits=[(sS, s_tab), (sV, v_v)], inc=sV)

    v_kT = tr16_into(k_bf, HT, kT, HT, v_rope)

    prev_vsc = {}
    prev_vat = {}
    for hh in range(NH):
        par = hh % 2
        bsc = 4 - par
        bat = 5 - 3 * par
        cb = 4 if par == 0 else 17

        def f(e, hh=hh, bsc=bsc):
            e.matmul(psf(bsc)[0:NL, 0:HT],
                     latT[:, hh * NL:(hh + 1) * NL],
                     kT[:, hh * HT:(hh + 1) * HT],
                     start=True, stop=True, skip_group_check=True)
            return None
        pe_sc = p.step("tensor", f,
                       waits=[(sV, v_kT), (sV, v_latT),
                              (sV, prev_vsc.get(hh - 2, 0))],
                       inc=sP)

        def f(e, par=par, bsc=bsc, cb=cb):
            sc = sc_sb[:, par * HT:(par + 1) * HT]
            e.tensor_tensor(out=sc, in0=psf(bsc)[0:NL, 0:HT],
                            in1=mask_sb[:], op=ALU.add)
            e.drain()
            e.tensor_reduce(out=colp[0:NL, cb:cb + 1], in_=sc, axis=AX.X,
                            op=ALU.max)
            e.drain()
            e.tensor_scalar(out=sc, in0=sc, scalar1=colp[0:NL, cb:cb + 1],
                            scalar2=None, op0=ALU.subtract)
            e.drain()
            return None
        v_sc = p.step("vector", f, waits=[(sP, pe_sc)], inc=sV)
        prev_vsc[hh] = v_sc

        s_exp = p.step("scalar",
                       lambda e, par=par, cb=cb: e.activation(
                           out=sc_sb[:, par * HT:(par + 1) * HT],
                           in_=sc_sb[:, par * HT:(par + 1) * HT],
                           func=AF.Exp,
                           accum_out=colp[0:NL, cb + 1:cb + 2]),
                       waits=[(sV, v_sc)], inc=sS)

        def f(e, par=par, cb=cb):
            e.reciprocal(colp[0:NL, cb + 2:cb + 3], colp[0:NL, cb + 1:cb + 2])
            e.drain()
            e.tensor_scalar(out=w_bf[:, par * HT:(par + 1) * HT],
                            in0=sc_sb[:, par * HT:(par + 1) * HT],
                            scalar1=colp[0:NL, cb + 2:cb + 3],
                            scalar2=None, op0=ALU.mult)
            e.drain()
            return None
        v_w = p.step("vector", f, waits=[(sS, s_exp)], inc=sV)

        def f(e, par=par):
            e.transpose(psb(par)[0:HT, 0:NL],
                        w_bf[:, par * HT:(par + 1) * HT], id_bf[0:NL, 0:NL])
            return None
        pe_wT = p.step("tensor", f,
                       waits=[(sV, v_w), psb_guard[par]], inc=sP)

        def f(e, par=par):
            e.tensor_copy(wT_bf[:, par * NL:(par + 1) * NL],
                          psb(par)[0:HT, 0:NL])
            e.drain()
            return None
        v_wT = p.step("vector", f, waits=[(sP, pe_wT)], inc=sV)
        psb_guard[par] = (sV, v_wT)

        def f(e, hh=hh, par=par, bat=bat):
            e.matmul(psf(bat)[0:NL, 0:HD], wT_bf[:, par * NL:(par + 1) * NL],
                     v_bf[:, hh * HD:(hh + 1) * HD],
                     start=True, stop=True, skip_group_check=True)
            return None
        pe_at = p.step("tensor", f,
                       waits=[(sV, v_wT), (sV, prev_vat.get(hh - 2, 0))],
                       inc=sP)

        def f(e, hh=hh, bat=bat):
            e.tensor_copy(attn_sb[(hh % 2) * NL:(hh % 2 + 1) * NL,
                                  (hh // 2) * HD:(hh // 2 + 1) * HD],
                          psf(bat)[0:NL, 0:HD])
            e.drain()
            return None
        v_at = p.step("vector", f, waits=[(sP, pe_at)], inc=sV)
        prev_vat[hh] = v_at
    v_attn_done = p.v(sV)


    scope_close(m_p0)
    m_w2 = scope_mark()
    wpf2 = sbuf("wpf2", [128, 2 * I_FFN], F32)
    wqb2 = sbuf("wqb2", [128, 2 * I_FFN], BF16)
    wtr2 = sbuf("wtr2", [128, 2 * I_FFN], BF16)
    wtr_guard[0] = wtr_guard[1] = (sWpw, wp_written)
    w2_gate = [(sV, p.v(sV)), (sS, p.v(sS)), (sP, p.v(sP)), (sG, p.v(sG))]
    for i in range(16):
        prep_chunk(w2_in[i * 128:(i + 1) * 128, :], I_FFN,
                   w2qT_hbm[i, :, :], sW2w, i % 2,
                   wpf2, wqb2, wtr2, scol=sw2_all[:, i:i + 1],
                   extra_load_waits=(w2_gate if i == 0 else ()))
    w2_written = p.v(sW2w)
    scope_close(m_w2)
    prep_quiesce = ([(sV, p.v(sV)), (sS, p.v(sS)), (sP, p.v(sP)),
                     (sW2w, w2_written), (sWpw, wp_written)] +
                    [(sW1w[q], w1_written[q]) for q in range(4)])

    xt_slots = Slots(p, "xt", 2)
    w1slots = Slots(p, "w1s", 4)
    w2slots = Slots(p, "w2s", 2)
    wpslots = Slots(p, "wps", 2)

    xt_sb = sb_at(0, "xt_sb", [128, 2 * C], F32)
    xn_f = sb_at(16384, "xn_f", [128, C], F32)
    xn_bfile = sb_at(24576, "xn_bf", [128, C], BF16)
    xq_bfile = sb_at(28672, "xq_bf", [128, C], BF16)
    xnT = sb_at(32768, "xnT", [128, 16 * GT], BF16)
    gatT = sb_at(49152, "gatT", [128, 8 * GT], BF16)
    gat_s = sb_at(57344, "gat_s", [128, NH * NL], F32)
    gat_e = sb_at(61440, "gat_e", [128, NH * NL], F32)
    gat_bf = sb_at(65536, "gat_bf", [128, NH * NL], BF16)
    gat_c = sb_at(67584, "gat_c", [128, NH], F32)
    wgs_sb = sb_at(67648, "wgs", [128, 16 * 1024], BF16)
    h_fm = sb_at(0, "h_fm", [128, NI * GT], F32)
    w1b_sb = sb_at(90112, "w1b", [128, 4 * C], BF16)
    silu_sb = sb_at(106496, "silu", [128, 4 * 512], F32)
    mx_sb = sb_at(163840, "mx", [128, GT], F32)
    mn_sb = sb_at(165888, "mn", [128, GT], F32)
    w2b_sb = sb_at(49152, "w2b", [128, 2 * I_FFN], BF16)
    wpb_sb = sb_at(71680, "wpb", [128, 2 * C], BF16)
    out_sb = sb_at(79872, "outb", [128, 2 * 512], F32)
    xg_sb = sb_at(83968, "xg", [128, 2 * 512], F32)
    res_sb = sb_at(88064, "res", [128, 2 * 512], F32)
    sbc = sb_at(114688, "sbc", [128, 2 * GT], F32)
    hq = sb_at(118784, "hq", [128, NI * GT], BF16)

    gst = {"last_gates_pe": 0, "last_peo": 0,
           "last_v_o": v_attn_done, "last_v_r": v_attn_done,
           "prev_bq": list(prep_quiesce)}
    v_tile_tr = {}
    v_tile_qx = {}
    tile_jx = {}

    def emit_A_early_tile(g, ti):
        t0 = g * GT
        row = t0 + ti * 128
        gatew = (prep_quiesce if g == 0 else gst["prev_bq"]) if ti == 0 else ()
        xt_slots.load("scalar",
                      lambda jj: xt_sb[:, jj * C:(jj + 1) * C],
                      x_in[row:row + 128, :], extra_waits=gatew)
        wsem, wval = xt_slots.wait_of_next()
        jx = xt_slots.slot()

        def f(e, jx=jx):
            for q in range(4):
                e.bn_stats(bn6[:, q * 6:(q + 1) * 6],
                           xt_sb[:, jx * C + q * 512: jx * C + (q + 1) * 512])
            e.drain()
            e.bn_aggr(mv2[:], bn6[:])
            e.drain()
            return None
        v_st = p.step("vector", f, waits=[(wsem, wval)], inc=sV)

        def f(e):
            e.tensor_scalar(out=colp[:, 0:1], in0=mv2[:, 1:2],
                            scalar1=EPS_LN, scalar2=None, op0=ALU.add)
            e.drain()
            return None
        v_eps = p.step("vector", f, waits=[(sV, v_st)], inc=sV)
        s_sd = p.step("scalar",
                      lambda e: e.activation(out=colp[:, 0:1],
                                             in_=colp[:, 0:1],
                                             func=AF.Sqrt),
                      waits=[(sV, v_eps)], inc=sS)

        def f(e):
            e.reciprocal(colp[:, 1:2], colp[:, 0:1])
            e.drain()
            e.tensor_scalar(out=colp[:, 2:3], in0=mv2[:, 0:1],
                            scalar1=colp[:, 1:2], scalar2=-1.0,
                            op0=ALU.mult, op1=ALU.mult)
            e.drain()
            return None
        v_cl = p.step("vector", f, waits=[(sS, s_sd)], inc=sV)
        tile_jx[(g, ti)] = (jx, v_cl)

    def emit_A_early_tile_b(g, ti):
        (jx, v_cl) = tile_jx[(g, ti)]
        s_xn = p.step("scalar",
                      lambda e, jx=jx: e.activation(
                          out=xn_f[:], in_=xt_sb[:, jx * C:(jx + 1) * C],
                          func=AF.Identity, bias=colp[:, 2:3],
                          scale=colp[:, 1:2]),
                      waits=[(sV, v_cl)], inc=sS)
        xt_slots.done("scalar")

        def f(e):
            e.tensor_reduce(out=colp[:, 3:4], in_=xn_f[:], axis=AX.X,
                            op=ALU.max, apply_absolute_value=True)
            e.drain()
            e.reciprocal(colp[:, 5:6], colp[:, 3:4])
            e.drain()
            e.tensor_scalar(out=colp[:, 5:6], in0=colp[:, 5:6],
                            scalar1=127.0, scalar2=1e8,
                            op0=ALU.mult, op1=ALU.min)
            e.drain()
            e.reciprocal(colp[:, 4:5], colp[:, 5:6])
            e.drain()
            e.tensor_copy(xn_bfile[:], xn_f[:])
            e._custom_dve(RINT, out=xq_bfile[:], in0=xn_f[:],
                          in1=bcast(colp[:, 5:6], C), s0=MAGIC)
            e.drain()
            e.tensor_scalar(out=xq_bfile[:], in0=xq_bfile[:],
                            scalar1=colp[:, 4:5], scalar2=None,
                            op0=ALU.mult)
            e.drain()
            return None
        v_qx = p.step("vector", f, waits=[(sS, s_xn)], inc=sV)
        v_tile_qx[(g, ti)] = v_qx

    def emit_A_early_tr(g, ti):
        v_qx = v_tile_qx[(g, ti)]
        for (src, dst) in ((xn_bfile, xnT), (xq_bfile, xqT)):
            for half in range(2):
                bank = half
                pe_t = tr8(lambda b, src=src, half=half:
                           src[:, (half * 8 + b) * 128:
                               (half * 8 + b + 1) * 128],
                           8, 128, bank, [(sV, v_qx)])
                dstv = dst[:].rearrange("p (i w) -> p i w", w=GT)[
                    :, half * 8:(half + 1) * 8, ti * 128:(ti + 1) * 128]
                tr8_evac(dstv, 8, 128, bank, pe_t)
        v_tile_tr[(g, ti)] = p.v(sV)

    def emit_A_late(g):
        for hcc in range(16):
            lw = []
            if hcc == 0:
                lw = [(sWgsL, 256 * g), (sWgw, wg_written),
                      (sOut, 256 * g), (sV, gst["last_v_o"]),
                      (sP, gst["last_peo"])]
                if g == 0:
                    lw += prep_quiesce
            p.step("gpsimd",
                   lambda e, hcc=hcc: e.dma_start(
                       out=wgs_sb[:, hcc * 1024:(hcc + 1) * 1024],
                       in_=wgS_hbm[hcc * 128:(hcc + 1) * 128, :]),
                   waits=lw, inc=[(sWgsL, 16)])
        wgs_loaded = p.v(sWgsL)

        prev_v_gbf = 0
        for ti in range(NTI):
            wts = [(sWgsL, wgs_loaded), (sV, v_tile_tr[(g, ti)]),
                   (sV, prev_v_gbf)]
            if ti == 0:
                wts.append((sV, gst["last_v_o"]))

            def f(e, ti=ti):
                for half in range(2):
                    for hc in range(16):
                        e.matmul(psf(4 + half)[:, :],
                                 xnT[:, hc * GT + ti * 128:
                                     hc * GT + (ti + 1) * 128],
                                 wgs_sb[:, hc * 1024 + half * 512:
                                        hc * 1024 + (half + 1) * 512],
                                 start=(hc == 0), stop=(hc == 15),
                                 skip_group_check=True)
                return None
            pe_g = p.step("tensor", f, waits=wts, inc=sP)
            gst["last_gates_pe"] = pe_g

            def f(e):
                g3v = ps_f32[:, 4 * 512: 4 * 512 + 1024].rearrange(
                    "p (h l) -> p h l", l=NL)
                e.tensor_reduce(out=gat_c[:], in_=g3v, axis=AX.X, op=ALU.max)
                e.drain()
                e.tensor_tensor(out=gat_s[:].rearrange("p (h l) -> p h l", l=NL),
                                in0=g3v,
                                in1=gat_c[:].unsqueeze(2).broadcast_to(
                                    [128, NH, NL]),
                                op=ALU.subtract)
                e.drain()
                return None
            v_gs = p.step("vector", f, waits=[(sP, pe_g)], inc=sV)

            s_ge = p.step("scalar",
                          lambda e: e.activation(
                              out=gat_e[:], in_=gat_s[:], func=AF.Exp),
                          waits=[(sV, v_gs)], inc=sS)

            def f(e):
                e.tensor_reduce(out=gat_c[:],
                                in_=gat_e[:].rearrange("p (h l) -> p h l", l=NL),
                                axis=AX.X, op=ALU.add)
                e.drain()
                e.reciprocal(gat_c[:], gat_c[:])
                e.drain()
                e.tensor_tensor(out=gat_bf[:].rearrange("p (h l) -> p h l", l=NL),
                                in0=gat_e[:].rearrange("p (h l) -> p h l", l=NL),
                                in1=gat_c[:].unsqueeze(2).broadcast_to(
                                    [128, NH, NL]),
                                op=ALU.mult)
                e.drain()
                return None
            v_gbf = p.step("vector", f, waits=[(sS, s_ge)], inc=sV)
            prev_v_gbf = v_gbf

            pe_t = tr8(lambda b: gat_bf[:, b * 128:(b + 1) * 128],
                       8, 128, 0, [(sV, v_gbf)])
            dstv = gatT[:].rearrange("p (i w) -> p i w", w=GT)[
                :, 0:8, ti * 128:(ti + 1) * 128]
            tr8_evac(dstv, 8, 128, 0, pe_t)
        v_gates_done = p.v(sV)
        return v_gates_done

    def emit_M5_pair(g, hp, v_gates_done, prev_v_wtd):
        for hh in (2 * hp, 2 * hp + 1):
            bank = 4 + (hh % 2)
            wts = [(sV, v_gates_done), (sV, prev_v_wtd.get(hh - 2, 0))]

            def f(e, hh=hh, bank=bank):
                e.matmul(psf(bank)[:, :],
                         attn_sb[(hh % 2) * NL:(hh % 2 + 1) * NL,
                                 (hh // 2) * HD:(hh // 2 + 1) * HD],
                         gatT[(hh % 2) * NL:(hh % 2 + 1) * NL,
                              (hh // 2) * GT:(hh // 2 + 1) * GT],
                         start=True, stop=True, skip_group_check=True)
                return None
            pe_w = p.step("tensor", f, waits=wts, inc=sP)

            def f(e, hh=hh, bank=bank):
                e.tensor_copy(wtd[:, hh * GT:(hh + 1) * GT], psf(bank)[:, :])
                e.drain()
                return None
            v_wtd_t = p.step("vector", f, waits=[(sP, pe_w)], inc=sV)
            prev_v_wtd[hh] = v_wtd_t

    def emit_B(g, v_gates_done):
        v_ln_done = v_tile_tr[(g, NTI - 1)]

        def f(e):
            e.memset(mx_sb[:], NEGF)
            e.memset(mn_sb[:], POSF)
            e.drain()
            return None
        v_mm = p.step("vector", f, waits=list(gst["prev_bq"]), inc=sV)

        pair_si = {}
        pair_vh = {}
        prev_v_wtd = {}
        v_wtd_done = None
        for k in range(NI):
            pe_pair = []
            for half in range(2):
                img_row = 2 * k + half
                quarter = img_row // 22
                lw = [(sW1w[quarter], w1_written[quarter])] if g == 0 else []
                if k == 0 and half == 0:
                    lw += [(sP, max(gst["last_gates_pe"], gst["last_peo"]))]
                    if g == 0:
                        lw += prep_quiesce
                w1slots.load("sync",
                             lambda jj, img_row=img_row:
                                 w1b_sb[:, jj * C:(jj + 1) * C],
                             w1qT_hbm[img_row, :, :], extra_waits=lw)
                wsem, wval = w1slots.wait_of_next()
                jb = w1slots.slot()
                bank = 2 * (k % 2) + half
                wts = [(wsem, wval), (sV, v_ln_done)]
                if k == 0 and half == 0:
                    wts += gst["prev_bq"]
                if half == 0 and (k - 2) in pair_si:
                    wts.append((sS, pair_si[k - 2]))

                def f(e, jb=jb, bank=bank):
                    for hc in range(16):
                        e.matmul(psf(bank)[:, :],
                                 w1b_sb[:, jb * C + hc * 128:
                                        jb * C + (hc + 1) * 128],
                                 xqT[:, hc * GT:(hc + 1) * GT],
                                 start=(hc == 0), stop=(hc == 15),
                                 skip_group_check=True)
                    return None
                pe_t = p.step("tensor", f, waits=wts, inc=sP)
                w1slots.done("tensor")
                pe_pair.append(pe_t)

            sl = k % 2
            swts = [(sP, pe_pair[1]), (sV, pair_vh.get(k - 2, 0))]
            s_si = p.step("scalar",
                          lambda e, sl=sl, k=k: e.activation(
                              out=silu_sb[:, sl * 512:(sl + 1) * 512],
                              in_=psf(2 * (k % 2))[:, :], func=AF.Silu,
                              scale=sw1_all[:, 2 * k:2 * k + 1]),
                          waits=swts, inc=sS)
            s_up = p.step("scalar",
                          lambda e, sl=sl, k=k: e.activation(
                              out=silu_sb[:, (2 + sl) * 512:(3 + sl) * 512],
                              in_=psf(2 * (k % 2) + 1)[:, :], func=AF.Identity,
                              scale=sw1_all[:, 2 * k + 1:2 * k + 2]),
                          waits=[], inc=sS)
            pair_si[k] = s_up

            vwts = [(sS, s_up)]
            if k == 39:
                vwts.append((sOut, 256 * g))

            def f(e, k=k, sl=sl):
                e.tensor_tensor(out=h_fm[:, k * GT:(k + 1) * GT],
                                in0=silu_sb[:, sl * 512:(sl + 1) * 512],
                                in1=silu_sb[:, (2 + sl) * 512:(3 + sl) * 512],
                                op=ALU.mult)
                e.drain()
                return None
            v_h = p.step("vector", f, waits=vwts, inc=sV)
            pair_vh[k] = v_h

            def f(e, k=k):
                e.tensor_tensor(out=mx_sb[:], in0=mx_sb[:],
                                in1=h_fm[:, k * GT:(k + 1) * GT], op=ALU.max)
                e.tensor_tensor(out=mn_sb[:], in0=mn_sb[:],
                                in1=h_fm[:, k * GT:(k + 1) * GT], op=ALU.min)
                e.drain()
                return None
            v_mm = p.step("vector", f, waits=[(sV, v_mm)] if k == 0 else [],
                          inc=sV)

            if 1 <= k <= 8:
                emit_M5_pair(g, k - 1, v_gates_done, prev_v_wtd)
                if k == 8:
                    v_wtd_done = p.v(sV)

        def f(e):
            e.tensor_scalar(out=mn_sb[:], in0=mn_sb[:], scalar1=-1.0,
                            scalar2=None, op0=ALU.mult)
            e.drain()
            e.tensor_tensor(out=mx_sb[:], in0=mx_sb[:], in1=mn_sb[:],
                            op=ALU.max)
            e.drain()
            return None
        v_M = p.step("vector", f, waits=[(sV, v_mm)], inc=sV)

        def f(e):
            for ti in range(NTI):
                e.transpose(psf(4, 128, ti * 128)[:, :],
                            mx_sb[:, ti * 128:(ti + 1) * 128], id_f32[:])
            return None
        pe_Mt = p.step("tensor", f, waits=[(sV, max(v_M, v_wtd_done))],
                       inc=sP)

        def f(e):
            e.tensor_reduce(out=colp[:, 8:12],
                            in_=psf(4)[:, :].rearrange("p (t w) -> p t w", w=128),
                            axis=AX.X, op=ALU.max)
            e.drain()
            e.reciprocal(colp[:, 16:20], colp[:, 8:12])
            e.drain()
            e.tensor_scalar(out=colp[:, 16:20], in0=colp[:, 16:20],
                            scalar1=127.0, scalar2=1e8,
                            op0=ALU.mult, op1=ALU.min)
            e.drain()
            e.reciprocal(colp[:, 12:16], colp[:, 16:20])
            e.drain()
            return None
        v_cols = p.step("vector", f, waits=[(sP, pe_Mt)], inc=sV)

        def f(e):
            for c in range(4):
                e.transpose(psf(5, 128, c * 128)[0:1, :],
                            colp[:, 12 + c:13 + c], id_f32[:])
            for c in range(4):
                e.transpose(psf(4, 128, c * 128)[0:1, :],
                            colp[:, 16 + c:17 + c], id_f32[:])
            return None
        pe_sT = p.step("tensor", f, waits=[(sV, v_cols)], inc=sP)

        def f(e):
            e.tensor_copy(mn_sb[0:1, 0:512], psf(5)[0:1, :])
            e.tensor_copy(mx_sb[0:1, 0:512], psf(4)[0:1, :])
            e.drain()
            return None
        v_sT = p.step("vector", f, waits=[(sP, pe_sT)], inc=sV)

        def f(e):
            for ti in range(NTI):
                e.matmul(psf(4, 128, ti * 128)[:, :], ones_sb[0:1, :],
                         mn_sb[0:1, ti * 128:(ti + 1) * 128],
                         start=True, stop=True, skip_group_check=True)
            for ti in range(NTI):
                e.matmul(psf(5, 128, ti * 128)[:, :], ones_sb[0:1, :],
                         mx_sb[0:1, ti * 128:(ti + 1) * 128],
                         start=True, stop=True, skip_group_check=True)
            return None
        pe_bc = p.step("tensor", f, waits=[(sV, v_sT)], inc=sP)

        def f(e):
            e.tensor_copy(sbc[:, 0:GT], psf(4)[:, :])
            e.tensor_copy(sbc[:, GT:2 * GT], psf(5)[:, :])
            e.drain()
            return None
        v_sbc = p.step("vector", f, waits=[(sP, pe_bc)], inc=sV)

        def f(e, k0=24, k1=NI):
            e._custom_dve(
                RINT,
                out=hq[:, k0 * GT:k1 * GT].rearrange("p (k t) -> p k t", t=GT),
                in0=h_fm[:, k0 * GT:k1 * GT].rearrange("p (k t) -> p k t", t=GT),
                in1=sbc[:, GT:2 * GT].unsqueeze(1).broadcast_to(
                    [128, k1 - k0, GT]),
                s0=MAGIC)
            e.drain()
            return None
        v_hq_p1 = p.step("vector", f, waits=[(sV, v_sbc)], inc=sV)

        def f(e, k0=0, k1=24):
            e._custom_dve(
                RINT,
                out=hq[:, k0 * GT:k1 * GT].rearrange("p (k t) -> p k t", t=GT),
                in0=h_fm[:, k0 * GT:k1 * GT].rearrange("p (k t) -> p k t", t=GT),
                in1=sbc[:, GT:2 * GT].unsqueeze(1).broadcast_to(
                    [128, k1 - k0, GT]),
                s0=MAGIC)
            e.drain()
            return None
        v_hq_done = p.step("vector", f, waits=[], inc=sV)
        b_quiesce = [(sV, v_hq_done), (sG, p.v(sG)), (sS, p.v(sS)),
                     (sP, p.v(sP))]
        qz_p1 = [(sV, v_hq_p1), (sG, p.v(sG)), (sS, p.v(sS)),
                 (sP, p.v(sP))]
        return qz_p1, v_hq_done, b_quiesce, v_wtd_done

    def emit_C(g, v_wtd_done, qz_p1, v_hq_done, b_quiesce, interleave):
        t0 = g * GT
        base_v_o = gst["last_v_o"]
        base_v_r = gst["last_v_r"]
        prev_v_o = {}
        prev_v_r = {}
        pend = []
        last_peo_l = [gst["last_peo"]]

        def emit_phase2(st):
            (osub, jr, tb, v_r) = st
            wts_o = [(sV, max(v_r, prev_v_o.get(osub - 2, base_v_o)))]

            def f(e, jr=jr, tb=tb):
                for ti in range(NTI):
                    e.transpose(psf(tb, 128, ti * 128)[:, :],
                                res_sb[:, jr * 512 + ti * 128:
                                       jr * 512 + (ti + 1) * 128],
                                id_f32[:])
                return None
            pe_o = p.step("tensor", f, waits=wts_o, inc=sP)

            sg = osub % 2
            v_wts = [(sP, pe_o), (sXg, 256 * g + 16 * (osub + 1)),
                     (sOut, 256 * g + (16 * (osub - 1) if osub >= 2 else 0))]

            def f(e, osub=osub, tb=tb, sg=sg):
                src = psf(tb)[:, :].rearrange("p (t w) -> p t w", w=128)
                xsrc = xg_sb[:, sg * 512:(sg + 1) * 512].rearrange(
                    "p (t w) -> p t w", w=128)
                dst = out_sb[:, sg * 512:(sg + 1) * 512].rearrange(
                    "p (t w) -> p t w", w=128)
                e.tensor_tensor(out=dst, in0=src, in1=xsrc, op=ALU.add)
                e.drain()
                return None
            v_o = p.step("vector", f, waits=v_wts, inc=sV)
            prev_v_o[osub] = v_o

            p.step("scalar",
                   lambda e, osub=osub, sg=sg, t0=t0: e.dma_start(
                       out=out_hbm[t0:t0 + GT,
                                   osub * 128:(osub + 1) * 128].rearrange(
                                       "(t q) c -> q t c", q=128),
                       in_=out_sb[:, sg * 512:(sg + 1) * 512].rearrange(
                           "p (t w) -> p t w", w=128)),
                   waits=[(sV, v_o), (sOut, 256 * g + 16 * osub)],
                   inc=[(sOut, 16)])
            return pe_o

        for osub in range(16):
            lw2 = list(qz_p1) if osub == 0 else []
            lwp = list(qz_p1) if osub == 0 else []
            if g == 0 and osub == 0:
                lw2.append((sW2w, w2_written))
                lwp.append((sWpw, wp_written))
            wp_src = wpS_hbm[:, :, osub * 128:(osub + 1) * 128].transpose(
                (1, 0, 2))
            wpslots.load("sync",
                         lambda jj: wpb_sb[:, jj * C:(jj + 1) * C].rearrange(
                             "p (i m) -> p i m", m=128),
                         wp_src, extra_waits=lwp)
            wpw, wpv = wpslots.wait_of_next()
            jp = wpslots.slot()
            w2slots.load("sync",
                         lambda jj: w2b_sb[:, jj * I_FFN:(jj + 1) * I_FFN],
                         w2qT_hbm[osub, :, :], extra_waits=lw2)
            w2w, w2v = w2slots.wait_of_next()
            j2 = w2slots.slot()

            bank_w = osub % 2
            bank_f = 2 + (osub % 2)
            wts_w = [(wpw, wpv),
                     (sV, max(v_wtd_done,
                              prev_v_r.get(osub - 2, base_v_r)))]

            def f(e, jp=jp, bank_w=bank_w):
                for hc in range(16):
                    e.matmul(psf(bank_w)[:, :],
                             wpb_sb[:, jp * C + hc * 128: jp * C + (hc + 1) * 128],
                             wtd[:, hc * GT:(hc + 1) * GT],
                             start=(hc == 0), stop=(hc == 15),
                             skip_group_check=True)
                return None
            pe_wp = p.step("tensor", f, waits=wts_w, inc=sP)
            wpslots.done("tensor")

            wts_f = [(w2w, w2v),
                     (sV, max(v_hq_done, prev_v_r.get(osub - 2, base_v_r)))]

            def f(e, j2=j2, bank_f=bank_f):
                for ic in range(NI):
                    e.matmul(psf(bank_f)[:, :],
                             w2b_sb[:, j2 * I_FFN + ic * 128:
                                    j2 * I_FFN + (ic + 1) * 128],
                             hq[:, ic * GT:(ic + 1) * GT],
                             start=(ic == 0), stop=(ic == NI - 1),
                             skip_group_check=True)
                return None
            pe_f2 = p.step("tensor", f, waits=wts_f, inc=sP)
            w2slots.done("tensor")

            if len(pend) >= 1:
                last_peo_l[0] = emit_phase2(pend.pop(0))

            xg_wts = [(sXg, 256 * g + 16 * osub),
                      (sV, prev_v_o.get(osub - 2, base_v_o))]
            if osub == 0:
                xg_wts += qz_p1 + [(sP, last_peo_l[0])]
            p.step("sync",
                   lambda e, osub=osub, t0=t0: e.dma_start(
                       out=xg_sb[:, (osub % 2) * 512:
                                 (osub % 2 + 1) * 512].rearrange(
                           "p (t w) -> p t w", w=128),
                       in_=x_in[t0:t0 + GT,
                                osub * 128:(osub + 1) * 128].rearrange(
                           "(t q) c -> q t c", q=128)),
                   waits=xg_wts, inc=[(sXg, 16)])

            jr = osub % 2

            def f(e, jr=jr, bank_w=bank_w, bank_f=bank_f, osub=osub):
                e.tensor_scalar(out=res_sb[:, jr * 512:(jr + 1) * 512],
                                in0=psf(bank_f)[:, :],
                                scalar1=sw2_all[:, osub:osub + 1],
                                scalar2=None, op0=ALU.mult)
                e.drain()
                e.tensor_tensor(out=res_sb[:, jr * 512:(jr + 1) * 512],
                                in0=res_sb[:, jr * 512:(jr + 1) * 512],
                                in1=sbc[:, 0:GT], op=ALU.mult)
                e.drain()
                e.tensor_tensor(out=res_sb[:, jr * 512:(jr + 1) * 512],
                                in0=res_sb[:, jr * 512:(jr + 1) * 512],
                                in1=psf(bank_w)[:, :], op=ALU.add)
                e.drain()
                return None
            v_r = p.step("vector", f, waits=[(sP, pe_f2)], inc=sV)
            prev_v_r[osub] = v_r
            pend.append((osub, jr, 4 + (osub % 2), v_r))

            interleave(osub)

        while pend:
            last_peo_l[0] = emit_phase2(pend.pop(0))
        gst["last_peo"] = last_peo_l[0]
        gst["last_v_o"] = p.v(sV)
        gst["last_v_r"] = p.v(sV)

    for ti in range(NTI):
        emit_A_early_tile(0, ti)
        emit_A_early_tile_b(0, ti)
        emit_A_early_tr(0, ti)
    for g in range(NG):
        v_gates_done = emit_A_late(g)
        bret = emit_B(g, v_gates_done)
        v_hq_p1, v_hq_done, b_quiesce, v_wtd_done = bret
        gst["prev_bq"] = b_quiesce

        _pa_at = {1: 0, 4: 1, 7: 2, 10: 3}
        _pb_at = {2: 0, 5: 1, 8: 2, 11: 3}
        _tr_at = {4: 0, 7: 1, 10: 2, 13: 3}

        def ilv(osub, g=g):
            if g + 1 < NG:
                if osub in _tr_at:
                    emit_A_early_tr(g + 1, _tr_at[osub])
                if osub in _pa_at:
                    emit_A_early_tile(g + 1, _pa_at[osub])
                if osub in _pb_at:
                    emit_A_early_tile_b(g + 1, _pb_at[osub])
        emit_C(g, v_wtd_done, v_hq_p1, v_hq_done, b_quiesce, ilv)

    p.step("sync", lambda e: None, waits=[(sOut, p.v(sOut))])

    with nc.Block() as block:
        p.emit(block)
    nc.finalize()
    return nc


_NC_CACHE = None


def kernel(**ins):
    global _NC_CACHE
    xa = np.ascontiguousarray(ins["L0_attn_x"], dtype=np.float32)
    lat = np.ascontiguousarray(ins["L0_attn_latent_queries"],
                               dtype=np.float32).reshape(NL, NH * HD)
    wk = np.ascontiguousarray(ins["L0_attn_Wk"], dtype=np.float32)
    wv = np.ascontiguousarray(ins["L0_attn_Wv"], dtype=np.float32)
    wg = np.ascontiguousarray(ins["L0_attn_Wg"], dtype=np.float32)
    wp = np.ascontiguousarray(ins["L0_attn_Wp"], dtype=np.float32)
    w1 = np.ascontiguousarray(ins["L0_ffn_w_gate_up"], dtype=np.float32)
    w2 = np.ascontiguousarray(ins["L0_ffn_w_down"], dtype=np.float32)

    if _NC_CACHE is None:
        _NC_CACHE = build_program()
    nc = _NC_CACHE

    in_maps = []
    for c in range(8):
        b = c // 4
        r0 = (c % 4) * TT
        in_maps.append({
            "x": xa[b, r0:r0 + TT, :],
            "xh": xa[b, 0:HT, :],
            "lat": lat,
            "wk": wk, "wv": wv, "wg": wg, "wp": wp,
            "w1": w1, "w2": w2,
        })
    res = run_bass_kernel_spmd(nc, in_maps, core_ids=list(range(8)))
    out = np.empty((2, 8192, C), dtype=np.float32)
    for c in range(8):
        b = c // 4
        r0 = (c % 4) * TT
        out[b, r0:r0 + TT, :] = np.asarray(res.results[c]["out"])
    return out


# version 17
# speedup vs baseline: 1.0306x; 1.0306x over previous
import numpy as np
import ml_dtypes
import concourse.bacc as bacc
import concourse.mybir as mybir
from concourse.bass_utils import run_bass_kernel_spmd

F32 = mybir.dt.float32
BF16 = mybir.dt.bfloat16
ALU = mybir.AluOpType
AF = mybir.ActivationFunctionType
AX = mybir.AxisListType

MAGIC = 12582912.0
SCALE = float(1.0 / np.sqrt(128.0))
EPS_LN = 1e-5
INV127 = float(np.float32(1.0) / np.float32(127.0))
EPS_Q = 1e-08
PI = float(np.pi)
TWO_PI = float(2.0 * np.pi)
LN1E4 = float(np.log(10000.0))
NEGF = -3.0e38
POSF = 3.0e38

C = 2048
TT = 2048
NG = 4
GT = TT // NG
NTI = GT // 128
I_FFN = 5632
NI = I_FFN // 128
NH, HD, NL, HT = 16, 128, 64, 64
N1B = 2 * I_FFN // 128


def _register_rint_op():
    import concourse.dve_ops as dve_ops
    from concourse.dve_op import DveOp
    from concourse.dve_spec import Spec, Src0, Src1, C0, lower
    from concourse.dve_table_gen import dve_ver_for
    for op in dve_ops.OPS:
        if op.name == "RINT_SCALE_TT":
            return op
    name = "RINT_SCALE_TT"
    spec = Spec(body=(Src0 * Src1 + C0) - C0)
    ver = dve_ver_for("TRN2")
    compiled = lower(spec, ver=ver, perf_en=False, name=name)
    op = DveOp(name, spec, subdim=False, uops_sha={ver: compiled.sha(ver)},
               reference=lambda in0, in1, s0, s1, imm2: np.rint(in0 * in1))
    row = len(dve_ops.OPS) + dve_ops._CUSTOM_DVE_ROW_BASE
    assert row <= 31, "no free custom-DVE opcode row"
    dve_ops.OPS.append(op)
    dve_ops._SUB_OPCODE_FOR_NAME[name] = row
    dve_ops.CUSTOM_DVE_SPECS[name] = spec
    dve_ops.CUSTOM_DVE_REFS[name] = op.ref
    return op


RINT = _register_rint_op()


class P:
    def __init__(self, nc):
        self.nc = nc
        self.steps = []
        self.count = {}

    def sem(self, name):
        s = self.nc.alloc_semaphore(name)
        self.count[s] = 0
        return s

    def step(self, eng, fn, waits=(), inc=None):
        incs = []
        if inc is not None:
            if not isinstance(inc, (list, tuple)):
                inc = [(inc, 1)]
            for s, n in inc:
                self.count[s] += n
                incs.append((s, n))

        def run(e, fn=fn, waits=tuple(waits), incs=tuple(incs)):
            for s, v in waits:
                if v > 0:
                    e.wait_ge(s, v)
            last = fn(e)
            if incs:
                if last is None:
                    last = e.drain() if hasattr(e, "drain") else e.nop()
                for s, n in incs:
                    last.then_inc(s, n)
        self.steps.append((eng, run))
        return self.count[incs[0][0]] if incs else None

    def v(self, s):
        return self.count[s]

    def emit(self, block):
        steps = self.steps

        def mk(name):
            def f(eng):
                for e, run in steps:
                    if e == name:
                        run(eng)
            return f
        block.sync(mk("sync"))
        block.scalar(mk("scalar"))
        block.vector(mk("vector"))
        block.tensor(mk("tensor"))
        block.gpsimd(mk("gpsimd"))


class Slots:
    def __init__(self, p, name, n):
        self.p, self.n = p, n
        self.ld = [p.sem(f"{name}_ld{j}") for j in range(n)]
        self.fr = [p.sem(f"{name}_fr{j}") for j in range(n)]
        self.il = 0
        self.iu = 0

    def load(self, queue, dst_fn, src_ap, extra_waits=()):
        j = self.il % self.n
        r = self.il // self.n
        waits = list(extra_waits)
        if r > 0:
            waits.append((self.fr[j], r))
            waits.append((self.ld[j], 16 * r))
        self.p.step(queue,
                    lambda e, j=j, src_ap=src_ap, dst_fn=dst_fn:
                        e.dma_start(out=dst_fn(j), in_=src_ap),
                    waits=waits, inc=[(self.ld[j], 16)])
        self.il += 1
        return j

    def wait_of_next(self):
        j = self.iu % self.n
        return (self.ld[j], 16 * (self.iu // self.n + 1))

    def slot(self):
        return self.iu % self.n

    def done(self, eng):
        j = self.iu % self.n
        self.p.step(eng, lambda e: None, inc=[(self.fr[j], 1)])
        self.iu += 1


def build_program():
    nc = bacc.Bacc("TRN2", target_bir_lowering=False)

    x_in = nc.dram_tensor("x", [TT, C], F32, kind="ExternalInput").ap()
    xh_in = nc.dram_tensor("xh", [HT, C], F32, kind="ExternalInput").ap()
    lat_in = nc.dram_tensor("lat", [NL, C], F32, kind="ExternalInput").ap()
    wk_in = nc.dram_tensor("wk", [C, C], F32, kind="ExternalInput").ap()
    wv_in = nc.dram_tensor("wv", [C, C], F32, kind="ExternalInput").ap()
    wg_in = nc.dram_tensor("wg", [C, NH * NL], F32, kind="ExternalInput").ap()
    wp_in = nc.dram_tensor("wp", [C, C], F32, kind="ExternalInput").ap()
    w1_in = nc.dram_tensor("w1", [2 * I_FFN, C], F32, kind="ExternalInput").ap()
    w2_in = nc.dram_tensor("w2", [C, I_FFN], F32, kind="ExternalInput").ap()
    out_hbm = nc.dram_tensor("out", [TT, C], F32, kind="ExternalOutput").ap()

    w1qT_hbm = nc.dram_tensor("w1qT", [N1B, 128, C], BF16, kind="Internal").ap()
    w2qT_hbm = nc.dram_tensor("w2qT", [16, 128, I_FFN], BF16, kind="Internal").ap()
    wpS_hbm = nc.dram_tensor("wpS", [16, 128, C], BF16, kind="Internal").ap()
    wgS_hbm = nc.dram_tensor("wgS", [C, NH * NL], BF16, kind="Internal").ap()

    id_bf_h = nc.inline_tensor(np.eye(128, dtype=ml_dtypes.bfloat16), "id_bf_h").ap()
    id_f32_h = nc.inline_tensor(np.eye(128, dtype=np.float32), "id_f32_h").ap()
    maskv = np.where(np.arange(HT)[None, :] <= np.arange(NL)[:, None],
                     np.float32(0.0), np.float32(-1000000000.0)).astype(np.float32)
    mask_h = nc.inline_tensor(maskv, "cmask").ap()
    tpart_h = nc.inline_tensor(
        np.broadcast_to(np.arange(HT, dtype=np.float32)[:, None],
                        (HT, HT)).copy(), "tpart_h").ap()
    fidx_h = nc.inline_tensor(
        np.broadcast_to(np.arange(HT, dtype=np.float32)[None, :],
                        (HT, HT)).copy(), "fidx_h").ap()
    ones_h = nc.inline_tensor(np.ones((1, 128), dtype=np.float32), "ones_h").ap()

    p = P(nc)

    id_bf = nc.alloc_sbuf_tensor("id_bf", [128, 128], BF16)
    id_f32 = nc.alloc_sbuf_tensor("id_f32", [128, 128], F32)
    attn_sb = nc.alloc_sbuf_tensor("attn_sb", [128, 8 * HD], BF16)
    xqT = nc.alloc_sbuf_tensor("xqT", [128, 16 * GT], BF16)
    wtd = nc.alloc_sbuf_tensor("wtd", [128, 16 * GT], BF16)
    colp = nc.alloc_sbuf_tensor("colp", [128, 24], F32)
    bn6 = nc.alloc_sbuf_tensor("bn6", [128, 24], F32)
    mv2 = nc.alloc_sbuf_tensor("mv2", [128, 2], F32)
    ones_sb = nc.alloc_sbuf_tensor("ones_sb", [1, 128], F32)
    sw1_all = nc.alloc_sbuf_tensor("sw1_all", [128, 2 * NI], F32)
    sw2_all = nc.alloc_sbuf_tensor("sw2_all", [128, 16], F32)

    ps_f32 = nc.alloc_psum_tensor("ps_f32", [128, 3072], F32)
    ps_bf = nc.alloc_psum_tensor("ps_bf", [128, 2048], BF16)

    def psf(bank, w=512, off=0):
        return ps_f32[:, bank * 512 + off: bank * 512 + off + w]

    def psb(bank, w=1024):
        return ps_bf[:, bank * 1024: bank * 1024 + w]

    sP = p.sem("sP")
    sV = p.sem("sV")
    sS = p.sem("sS")
    sG = p.sem("sG")
    sQs = p.sem("sQs")
    sQid = p.sem("sQid")
    sOut = p.sem("sOut")
    sW1w = [p.sem(f"sW1w{i}") for i in range(4)]
    sW2w = p.sem("sW2w")
    sWpw = p.sem("sWpw")
    sWgw = p.sem("sWgw")
    sXg = p.sem("sXg")
    sWgsL = p.sem("sWgsL")

    psb_guard = {0: (None, 0), 1: (None, 0)}
    wtr_guard = {0: None, 1: None}

    def bcast(col_ap, n):
        return col_ap.broadcast_to([col_ap.shape[0], n])

    ARENA_B = (nc.sbuf_top - nc.sbuf_base - 64) // 32 * 32
    arena_h = nc.sbuf_tensor("arena", [128, ARENA_B], mybir.dt.uint8).__enter__()
    AB = nc.lookup_mloc(arena_h).addr

    def _bytes(shape, dt):
        n = 1
        for d in shape[1:]:
            n *= d
        return n * mybir.dt.size(dt)

    def sb_at(off, name, shape, dt):
        assert off % 32 == 0 and off + _bytes(shape, dt) <= ARENA_B, \
            (name, off, _bytes(shape, dt), ARENA_B)
        return nc.alloc_sbuf_tensor_at(name, shape, dt, offset=AB + off)

    _cur = [0]

    def sbuf(name, shape, dt, side="left"):
        off = (_cur[0] + 31) // 32 * 32
        _cur[0] = off + _bytes(shape, dt)
        return sb_at(off, name, shape, dt)

    def scope_mark():
        return _cur[0]

    def scope_close(mark):
        _cur[0] = mark

    def tr8(src_fn, nblk, n_rows, bank, waits):
        w = list(waits) + [psb_guard[bank]]

        def f(e, nblk=nblk, n_rows=n_rows, bank=bank, src_fn=src_fn):
            for b in range(nblk):
                e.transpose(psb(bank)[0:128, b * 128: b * 128 + n_rows],
                            src_fn(b), id_bf[0:n_rows, 0:n_rows])
            return None
        return p.step("tensor", f, waits=w, inc=sP)

    def tr8_evac(dst_ap, nblk, n_rows, bank, pe_t):
        def f(e, dst_ap=dst_ap, nblk=nblk, n_rows=n_rows, bank=bank):
            srcs = psb(bank)[:].rearrange("p (b w) -> p b w", w=128)[
                :, 0:nblk, 0:n_rows]
            e.tensor_copy(dst_ap, srcs)
            e.drain()
            return None
        t = p.step("vector", f, waits=[(sP, pe_t)], inc=sV)
        psb_guard[bank] = (sV, t)
        return t

    def tr8_evac_s(dst_ap, nblk, n_rows, bank, pe_t):
        def f(e, dst_ap=dst_ap, nblk=nblk, n_rows=n_rows, bank=bank):
            srcs = psb(bank)[:].rearrange("p (b w) -> p b w", w=128)[
                :, 0:nblk, 0:n_rows]
            e.activation(out=dst_ap, in_=srcs, func=AF.Copy)
            return None
        t = p.step("scalar", f, waits=[(sP, pe_t)], inc=sS)
        psb_guard[bank] = (sS, t)
        return t

    p.step("scalar", lambda e: e.dma_start(out=id_bf[:], in_=id_bf_h[:, :]),
           inc=[(sQid, 16)])
    p.step("scalar", lambda e: e.dma_start(out=id_f32[:], in_=id_f32_h[:, :]),
           inc=[(sQid, 16)])
    qs_consts = p.v(sQid)

    m_p0 = scope_mark()
    xh_sb = sbuf("xh_sb", [HT, C], F32)
    lat_sb = sbuf("lat_sb", [NL, C], F32)
    mask_sb = sbuf("mask_sb", [NL, HT], F32)
    xnh = sbuf("xnh", [HT, C], F32)
    xnh_bf = sbuf("xnh_bf", [HT, C], BF16)
    xnhT = sbuf("xnhT", [128, 16 * HT], BF16)
    lat_bf = sbuf("lat_bf", [NL, C], BF16)
    latT = sbuf("latT", [128, 16 * NL], BF16)
    kvf = sbuf("kvf", [128, 2 * C], F32)
    kvb = sbuf("kvb", [128, 2 * C], BF16)
    k_tm = sbuf("k_tm", [HT, C], F32)
    v_tm = sbuf("v_tm", [HT, C], F32)
    k_bf = sbuf("k_bf", [HT, C], BF16)
    v_bf = sbuf("v_bf", [HT, C], BF16)
    kT = sbuf("kT", [128, 16 * HT], BF16)
    rt_t = sbuf("rt_t", [HT, HT], F32)
    rt_i = sbuf("rt_i", [HT, HT], F32)
    rt_k = sbuf("rt_k", [HT, HT], F32)
    cos_t = sbuf("cos_t", [HT, HT], F32)
    sin_t = sbuf("sin_t", [HT, HT], F32)
    re1 = sbuf("re1", [HT, C // 2], F32)
    re2 = sbuf("re2", [HT, C // 2], F32)
    re3 = sbuf("re3", [HT, C // 2], F32)
    sc_sb = sbuf("sc_sb", [NL, 2 * HT], F32)
    w_bf = sbuf("w_bf", [NL, 2 * HT], BF16)
    wT_bf = sbuf("wT_bf", [HT, 2 * NL], BF16)
    wpf = sbuf("wpf", [128, 2 * C], F32)
    wqb = sbuf("wqb", [128, 2 * C], BF16)
    wtr = sbuf("wtr", [128, 2 * C], BF16)
    wpfp = sbuf("wpfp", [128, 2 * C], F32)
    wtrp = sbuf("wtrp", [128, 2 * C], BF16)
    wgf = sbuf("wgf", [128, 2 * NH * NL], F32)
    wgb = sbuf("wgb", [128, 2 * NH * NL], BF16)

    def f(e):
        e.memset(colp[0:1, 23:24], 0.25)
        e.drain()
        return None
    v_seed = p.step("vector", f, inc=sV)

    def f(e):
        for fn in (AF.Copy, AF.Identity, AF.Sqrt, AF.Exp, AF.Sin, AF.Silu):
            e.activation(out=colp[0:1, 22:23], in_=colp[0:1, 23:24], func=fn)
            e.drain()
        return None
    p.step("scalar", f, waits=[(sV, v_seed)], inc=sS)

    p.step("scalar", lambda e: e.dma_start(out=xh_sb[:], in_=xh_in[:, :]),
           inc=[(sQs, 16)])
    p.step("scalar", lambda e: e.dma_start(out=lat_sb[:], in_=lat_in[:, :]),
           inc=[(sQs, 16)])
    p.step("scalar", lambda e: e.dma_start(out=mask_sb[:], in_=mask_h[:, :]),
           inc=[(sQs, 16)])
    qs_head = p.v(sQs)

    wst = Slots(p, "wst", 2)
    wgst = Slots(p, "wgst", 2)
    wpst = Slots(p, "wpst", 2)
    wg_slot_guard = {0: 0, 1: 0}
    wp_slot_guard = {0: 0, 1: 0}
    GW = NH * NL

    def emit_wg_chunk(i):
        wgst.load("sync", lambda j: wgf[:, j * GW: (j + 1) * GW],
                  wg_in[i * 128:(i + 1) * 128, :])
        wsem, wval = wgst.wait_of_next()
        j = wgst.slot()

        def f(e, j=j):
            e.tensor_scalar(out=wgb[:, j * GW:(j + 1) * GW],
                            in0=wgf[:, j * GW:(j + 1) * GW],
                            scalar1=SCALE, scalar2=None, op0=ALU.mult)
            e.drain()
            return None
        v_c = p.step("vector", f,
                     waits=[(wsem, wval), (sWgw, wg_slot_guard[j])], inc=sV)
        wgst.done("vector")
        p.step("gpsimd",
               lambda e, i=i, j=j: e.dma_start(
                   out=wgS_hbm[i * 128:(i + 1) * 128, :],
                   in_=wgb[:, j * GW:(j + 1) * GW]),
               waits=[(sV, v_c), (sWgw, p.v(sWgw))], inc=[(sWgw, 16)])
        wg_slot_guard[j] = p.v(sWgw)

    def emit_wp_chunk(ic):
        wpst.load("sync", lambda j: wpfp[:, j * C:(j + 1) * C],
                  wp_in[ic * 128:(ic + 1) * 128, :])
        wsem, wval = wpst.wait_of_next()
        j = wpst.slot()
        s_c = p.step("scalar",
                     lambda e, j=j: e.activation(
                         out=wtrp[:, j * C:(j + 1) * C],
                         in_=wpfp[:, j * C:(j + 1) * C], func=AF.Copy),
                     waits=[(wsem, wval), (sWpw, wp_slot_guard[j])], inc=sS)
        wpst.done("scalar")
        p.step("gpsimd",
               lambda e, ic=ic, j=j: e.dma_start(
                   out=wpS_hbm[ic, :, :],
                   in_=wtrp[:, j * C:(j + 1) * C]),
               waits=[(sS, s_c), (sWpw, p.v(sWpw))], inc=[(sWpw, 16)])
        wp_slot_guard[j] = p.v(sWpw)


    trslot = [0]
    state = {"tr": {0: 0, 1: 0}}

    def prep_chunk(src_row_ap, W, dst_img_row, done_sem, cpar,
                   b_wpf, b_wqb, b_wtr, scol=None, extra_load_waits=()):
        base = 8 + 4 * cpar
        wst.load("sync", lambda j: b_wpf[:, j * W: j * W + W], src_row_ap,
                 extra_waits=extra_load_waits)
        wsem, wval = wst.wait_of_next()
        j = wst.slot()

        def f(e, j=j, W=W, base=base, cpar=cpar, b_wpf=b_wpf, b_wqb=b_wqb,
              scol=scol):
            e.tensor_reduce(out=colp[:, base:base + 1],
                            in_=b_wpf[:, j * W: j * W + W],
                            axis=AX.X, op=ALU.max, apply_absolute_value=True)
            e.drain()
            e.reciprocal(colp[:, base + 2:base + 3], colp[:, base:base + 1])
            e.drain()
            e.tensor_scalar(out=colp[:, base + 2:base + 3],
                            in0=colp[:, base + 2:base + 3], scalar1=127.0,
                            scalar2=1e8, op0=ALU.mult, op1=ALU.min)
            e.drain()
            e.reciprocal(scol, colp[:, base + 2:base + 3])
            e.drain()
            e._custom_dve(RINT, out=b_wqb[:, cpar * W:cpar * W + W],
                          in0=b_wpf[:, j * W: j * W + W],
                          in1=bcast(colp[:, base + 2:base + 3], W), s0=MAGIC)
            e.drain()
            return None
        v_q = p.step("vector", f,
                     waits=[(wsem, wval), (sP, state["tr"][cpar])], inc=sV)
        wst.done("vector")

        ts = trslot[0]
        trslot[0] ^= 1
        nsub = W // 128
        last_s = None
        pe_t = None
        for lo in range(0, nsub, 8):
            hi = min(lo + 8, nsub)
            bank = (lo // 8) % 2
            extra = [(sV, v_q), (sQid, qs_consts)]
            if wtr_guard[ts] is not None:
                extra.append(wtr_guard[ts])
            pe_t = tr8(lambda b, lo=lo, cpar=cpar, W=W, b_wqb=b_wqb:
                       b_wqb[:, cpar * W + (lo + b) * 128:
                             cpar * W + (lo + b + 1) * 128],
                       hi - lo, 128, bank, extra)
            dst = b_wtr[:, ts * W + lo * 128: ts * W + hi * 128].rearrange(
                "p (b w) -> p b w", w=128)
            last_s = tr8_evac_s(dst, hi - lo, 128, bank, pe_t)
        state["tr"][cpar] = pe_t

        wv_t = p.step("gpsimd",
                      lambda e, ts=ts, W=W, dst_img_row=dst_img_row, b_wtr=b_wtr:
                          e.dma_start(out=dst_img_row,
                                      in_=b_wtr[:, ts * W: ts * W + W]),
                      waits=[(sS, last_s), (done_sem, p.v(done_sem))],
                      inc=[(done_sem, 16)])
        wtr_guard[ts] = (done_sem, wv_t)

    def f(e):
        for q in range(4):
            e.bn_stats(bn6[0:HT, q * 6:(q + 1) * 6],
                       xh_sb[:, q * 512:(q + 1) * 512])
        e.drain()
        e.bn_aggr(mv2[0:HT, :], bn6[0:HT, :])
        e.drain()
        return None
    v_hstats = p.step("vector", f, waits=[(sQs, qs_head)], inc=sV)

    def f(e):
        e.tensor_scalar(out=colp[0:HT, 0:1], in0=mv2[0:HT, 1:2],
                        scalar1=EPS_LN, scalar2=None, op0=ALU.add)
        e.drain()
        return None
    v_heps = p.step("vector", f, waits=[(sV, v_hstats)], inc=sV)
    s_hstd = p.step("scalar",
                    lambda e: e.activation(out=colp[0:HT, 0:1], in_=colp[0:HT, 0:1],
                                           func=AF.Sqrt),
                    waits=[(sV, v_heps)], inc=sS)

    def f(e):
        e.reciprocal(colp[0:HT, 1:2], colp[0:HT, 0:1])
        e.drain()
        e.tensor_scalar(out=colp[0:HT, 2:3], in0=mv2[0:HT, 0:1],
                        scalar1=colp[0:HT, 1:2], scalar2=-1.0,
                        op0=ALU.mult, op1=ALU.mult)
        e.drain()
        return None
    v_hcol = p.step("vector", f, waits=[(sS, s_hstd)], inc=sV)

    s_xnh = p.step("scalar",
                   lambda e: e.activation(out=xnh[:], in_=xh_sb[:], func=AF.Identity,
                                          bias=colp[0:HT, 2:3],
                                          scale=colp[0:HT, 1:2]),
                   waits=[(sV, v_hcol)], inc=sS)

    def f(e):
        e.tensor_copy(xnh_bf[:], xnh[:])
        e.tensor_scalar(out=lat_bf[:], in0=lat_sb[:], scalar1=SCALE,
                        scalar2=None, op0=ALU.mult)
        e.drain()
        return None
    v_hbf = p.step("vector", f, waits=[(sS, s_xnh), (sQs, qs_head)], inc=sV)

    def tr16_into(src_sb, n_rows, dst_sb, dstw, wait_v):
        last = None
        for half in range(2):
            bank = half
            pe_t = tr8(lambda b, half=half, src_sb=src_sb, n_rows=n_rows:
                       src_sb[0:n_rows,
                              (half * 8 + b) * 128:(half * 8 + b + 1) * 128],
                       8, n_rows, bank, [(sV, wait_v), (sQid, qs_consts)])
            dst = dst_sb[:].rearrange("p (i w) -> p i w", w=dstw)[
                :, half * 8:(half + 1) * 8, 0:n_rows]
            last = tr8_evac(dst, 8, n_rows, bank, pe_t)
        return last

    v_xnhT = tr16_into(xnh_bf, HT, xnhT, HT, v_hbf)
    v_latT = tr16_into(lat_bf, NL, latT, NL, v_hbf)

    kvslots = Slots(p, "kv", 2)
    kvb_guard = {0: 0, 1: 0}
    kv_pe = [None]

    def emit_kv_chunk(w_in, hc):
        kvslots.load("sync", lambda j: kvf[:, j * C:(j + 1) * C],
                     w_in[hc * 128:(hc + 1) * 128, :])
        wsem, wval = kvslots.wait_of_next()
        j = kvslots.slot()
        s_c = p.step("scalar",
                     lambda e, j=j: e.activation(
                         out=kvb[:, j * C:(j + 1) * C],
                         in_=kvf[:, j * C:(j + 1) * C], func=AF.Copy),
                     waits=[(wsem, wval), (sP, kvb_guard[j])], inc=sS)
        kvslots.done("scalar")

        def f(e, hc=hc, j=j):
            for oc in range(4):
                e.matmul(psf(oc)[0:HT, :],
                         xnhT[:, hc * HT:(hc + 1) * HT],
                         kvb[:, j * C + oc * 512: j * C + (oc + 1) * 512],
                         start=(hc == 0), stop=(hc == 15),
                         skip_group_check=True)
            return None
        pe_t = p.step("tensor", f, waits=[(sS, s_c), (sV, v_xnhT)], inc=sP)
        kvb_guard[j] = pe_t
        kv_pe[0] = pe_t

    def emit_kv_evac(dst):
        def f(e, dst=dst):
            e.tensor_copy(dst[:], ps_f32[0:HT, 0:2048])
            e.drain()
            return None
        return p.step("vector", f, waits=[(sP, kv_pe[0])], inc=sV)

    w1_order = []
    for k in range(NI):
        w1_order.append(k)
        w1_order.append(NI + k)
    for idx, blk in enumerate(w1_order):
        quarter = idx // 22
        prep_chunk(w1_in[blk * 128:(blk + 1) * 128, :], C,
                   w1qT_hbm[idx, :, :], sW1w[quarter], idx % 2,
                   wpf, wqb, wtr, scol=sw1_all[:, idx:idx + 1])
        if idx % 5 == 1 and idx // 5 < 16:
            emit_wg_chunk(idx // 5)
        if idx % 5 == 3 and idx // 5 < 16:
            emit_wp_chunk(idx // 5)
        if 40 <= idx < 56:
            emit_kv_chunk(wk_in, idx - 40)
        if idx == 56:
            v_k = emit_kv_evac(k_tm)
        if 57 <= idx < 73:
            emit_kv_chunk(wv_in, idx - 57)
        if idx == 73:
            v_v = emit_kv_evac(v_tm)
    w1_written = [p.v(s) for s in sW1w]
    wg_written = p.v(sWgw)
    wp_written = p.v(sWpw)

    p.step("scalar", lambda e: e.dma_start(out=rt_t[:], in_=tpart_h[:, :]),
           inc=[(sQid, 16)])
    p.step("scalar", lambda e: e.dma_start(out=rt_i[:], in_=fidx_h[:, :]),
           inc=[(sQid, 16)])
    p.step("scalar", lambda e: e.dma_start(out=ones_sb[:], in_=ones_h[:, :]),
           inc=[(sQid, 16)])
    qs_iota = p.v(sQid)

    def f(e):
        e.tensor_scalar(out=rt_i[:], in0=rt_i[:],
                        scalar1=float(-2.0 * LN1E4 / 128.0), scalar2=None,
                        op0=ALU.mult)
        e.drain()
        return None
    v_invfs = p.step("vector", f, waits=[(sQid, qs_iota)], inc=sV)
    s_invf = p.step("scalar",
                    lambda e: e.activation(out=rt_i[:], in_=rt_i[:], func=AF.Exp),
                    waits=[(sV, v_invfs)], inc=sS)

    def f(e):
        e.tensor_tensor(out=rt_t[:], in0=rt_t[:], in1=rt_i[:], op=ALU.mult)
        e.memset(rt_k[:], float(1.0 / TWO_PI))
        e.drain()
        e._custom_dve(RINT, out=rt_i[:], in0=rt_t[:], in1=rt_k[:], s0=MAGIC)
        e.drain()
        e.scalar_tensor_tensor(out=sin_t[:], in0=rt_i[:], scalar=-TWO_PI,
                               in1=rt_t[:], op0=ALU.mult, op1=ALU.add)
        e.drain()
        e.tensor_scalar(out=sin_t[:], in0=sin_t[:], scalar1=PI, scalar2=-PI,
                        op0=ALU.min, op1=ALU.max)
        e.tensor_scalar(out=rt_t[:], in0=rt_t[:], scalar1=float(PI / 2),
                        scalar2=None, op0=ALU.add)
        e.drain()
        e._custom_dve(RINT, out=rt_i[:], in0=rt_t[:], in1=rt_k[:], s0=MAGIC)
        e.drain()
        e.scalar_tensor_tensor(out=cos_t[:], in0=rt_i[:], scalar=-TWO_PI,
                               in1=rt_t[:], op0=ALU.mult, op1=ALU.add)
        e.drain()
        e.tensor_scalar(out=cos_t[:], in0=cos_t[:], scalar1=PI, scalar2=-PI,
                        op0=ALU.min, op1=ALU.max)
        e.drain()
        return None
    v_ropearg = p.step("vector", f, waits=[(sS, s_invf)], inc=sV)

    def f(e):
        e.activation(out=sin_t[:], in_=sin_t[:], func=AF.Sin)
        e.drain()
        e.activation(out=cos_t[:], in_=cos_t[:], func=AF.Sin)
        e.drain()
        return None
    s_tab = p.step("scalar", f, waits=[(sV, v_ropearg)], inc=sS)

    def f(e):
        k3 = k_tm[:].rearrange("t (h d2 two) -> t h d2 two", h=NH, two=2)
        ke = k3[:, :, :, 0:1].squeeze(3)
        ko = k3[:, :, :, 1:2].squeeze(3)
        cosb = cos_t[:].unsqueeze(1).broadcast_to([HT, NH, HT])
        sinb = sin_t[:].unsqueeze(1).broadcast_to([HT, NH, HT])
        r1 = re1[:].rearrange("t (h f) -> t h f", h=NH)
        r2 = re2[:].rearrange("t (h f) -> t h f", h=NH)
        r3 = re3[:].rearrange("t (h f) -> t h f", h=NH)
        e.tensor_tensor(out=r1, in0=ke, in1=cosb, op=ALU.mult)
        e.tensor_tensor(out=r2, in0=ko, in1=sinb, op=ALU.mult)
        e.tensor_tensor(out=r3, in0=ke, in1=sinb, op=ALU.mult)
        e.drain()
        e.tensor_sub(re1[:], re1[:], re2[:])
        e.drain()
        e.tensor_tensor(out=r2, in0=ko, in1=cosb, op=ALU.mult)
        e.drain()
        e.tensor_add(re2[:], re2[:], re3[:])
        e.drain()
        e.tensor_copy(ke, r1)
        e.tensor_copy(ko, r2)
        e.drain()
        e.tensor_copy(k_bf[:], k_tm[:])
        e.tensor_copy(v_bf[:], v_tm[:])
        e.drain()
        return None
    v_rope = p.step("vector", f, waits=[(sS, s_tab), (sV, v_v)], inc=sV)

    v_kT = tr16_into(k_bf, HT, kT, HT, v_rope)

    prev_vsc = {}
    prev_vat = {}
    for hh in range(NH):
        par = hh % 2
        bsc = 4 - par
        bat = 5 - 3 * par
        cb = 4 if par == 0 else 17

        def f(e, hh=hh, bsc=bsc):
            e.matmul(psf(bsc)[0:NL, 0:HT],
                     latT[:, hh * NL:(hh + 1) * NL],
                     kT[:, hh * HT:(hh + 1) * HT],
                     start=True, stop=True, skip_group_check=True)
            return None
        pe_sc = p.step("tensor", f,
                       waits=[(sV, v_kT), (sV, v_latT),
                              (sV, prev_vsc.get(hh - 2, 0))],
                       inc=sP)

        def f(e, par=par, bsc=bsc, cb=cb):
            sc = sc_sb[:, par * HT:(par + 1) * HT]
            e.tensor_tensor(out=sc, in0=psf(bsc)[0:NL, 0:HT],
                            in1=mask_sb[:], op=ALU.add)
            e.drain()
            e.tensor_reduce(out=colp[0:NL, cb:cb + 1], in_=sc, axis=AX.X,
                            op=ALU.max)
            e.drain()
            e.tensor_scalar(out=sc, in0=sc, scalar1=colp[0:NL, cb:cb + 1],
                            scalar2=None, op0=ALU.subtract)
            e.drain()
            return None
        v_sc = p.step("vector", f, waits=[(sP, pe_sc)], inc=sV)
        prev_vsc[hh] = v_sc

        s_exp = p.step("scalar",
                       lambda e, par=par, cb=cb: e.activation(
                           out=sc_sb[:, par * HT:(par + 1) * HT],
                           in_=sc_sb[:, par * HT:(par + 1) * HT],
                           func=AF.Exp,
                           accum_out=colp[0:NL, cb + 1:cb + 2]),
                       waits=[(sV, v_sc)], inc=sS)

        def f(e, par=par, cb=cb):
            e.reciprocal(colp[0:NL, cb + 2:cb + 3], colp[0:NL, cb + 1:cb + 2])
            e.drain()
            e.tensor_scalar(out=w_bf[:, par * HT:(par + 1) * HT],
                            in0=sc_sb[:, par * HT:(par + 1) * HT],
                            scalar1=colp[0:NL, cb + 2:cb + 3],
                            scalar2=None, op0=ALU.mult)
            e.drain()
            return None
        v_w = p.step("vector", f, waits=[(sS, s_exp)], inc=sV)

        def f(e, par=par):
            e.transpose(psb(par)[0:HT, 0:NL],
                        w_bf[:, par * HT:(par + 1) * HT], id_bf[0:NL, 0:NL])
            return None
        pe_wT = p.step("tensor", f,
                       waits=[(sV, v_w), psb_guard[par]], inc=sP)

        def f(e, par=par):
            e.tensor_copy(wT_bf[:, par * NL:(par + 1) * NL],
                          psb(par)[0:HT, 0:NL])
            e.drain()
            return None
        v_wT = p.step("vector", f, waits=[(sP, pe_wT)], inc=sV)
        psb_guard[par] = (sV, v_wT)

        def f(e, hh=hh, par=par, bat=bat):
            e.matmul(psf(bat)[0:NL, 0:HD], wT_bf[:, par * NL:(par + 1) * NL],
                     v_bf[:, hh * HD:(hh + 1) * HD],
                     start=True, stop=True, skip_group_check=True)
            return None
        pe_at = p.step("tensor", f,
                       waits=[(sV, v_wT), (sV, prev_vat.get(hh - 2, 0))],
                       inc=sP)

        def f(e, hh=hh, bat=bat):
            e.tensor_copy(attn_sb[(hh % 2) * NL:(hh % 2 + 1) * NL,
                                  (hh // 2) * HD:(hh // 2 + 1) * HD],
                          psf(bat)[0:NL, 0:HD])
            e.drain()
            return None
        v_at = p.step("vector", f, waits=[(sP, pe_at)], inc=sV)
        prev_vat[hh] = v_at
    v_attn_done = p.v(sV)




    scope_close(m_p0)
    m_w2 = scope_mark()
    wpf2 = sbuf("wpf2", [128, 2 * I_FFN], F32)
    wqb2 = sbuf("wqb2", [128, 2 * I_FFN], BF16)
    wtr2 = sbuf("wtr2", [128, 2 * I_FFN], BF16)
    wtr_guard[0] = wtr_guard[1] = (sWpw, wp_written)
    w2_gate = [(sV, p.v(sV)), (sS, p.v(sS)), (sP, p.v(sP)), (sG, p.v(sG))]
    for i in range(16):
        prep_chunk(w2_in[i * 128:(i + 1) * 128, :], I_FFN,
                   w2qT_hbm[i, :, :], sW2w, i % 2,
                   wpf2, wqb2, wtr2, scol=sw2_all[:, i:i + 1],
                   extra_load_waits=(w2_gate if i == 0 else ()))
    w2_written = p.v(sW2w)
    scope_close(m_w2)
    prep_quiesce = ([(sV, p.v(sV)), (sS, p.v(sS)), (sP, p.v(sP)),
                     (sW2w, w2_written), (sWpw, wp_written)] +
                    [(sW1w[q], w1_written[q]) for q in range(4)])

    xt_slots = Slots(p, "xt", 2)
    w1slots = Slots(p, "w1s", 4)
    w2slots = Slots(p, "w2s", 2)
    wpslots = Slots(p, "wps", 2)

    xt_sb = sb_at(0, "xt_sb", [128, 2 * C], F32)
    xn_f = sb_at(16384, "xn_f", [128, C], F32)
    xn_bfile = sb_at(24576, "xn_bf", [128, C], BF16)
    xq_bfile = sb_at(28672, "xq_bf", [128, C], BF16)
    xnT = sb_at(32768, "xnT", [128, 16 * GT], BF16)
    gatT = sb_at(49152, "gatT", [128, 8 * GT], BF16)
    gat_s = sb_at(57344, "gat_s", [128, NH * NL], F32)
    gat_e = sb_at(61440, "gat_e", [128, NH * NL], F32)
    gat_bf = sb_at(65536, "gat_bf", [128, NH * NL], BF16)
    gat_c = sb_at(67584, "gat_c", [128, NH], F32)
    wgs_sb = sb_at(67648, "wgs", [128, 16 * 1024], BF16)
    h_fm = sb_at(0, "h_fm", [128, NI * GT], F32)
    w1b_sb = sb_at(90112, "w1b", [128, 4 * C], BF16)
    silu_sb = sb_at(106496, "silu", [128, 4 * 512], F32)
    mx_sb = sb_at(163840, "mx", [128, GT], F32)
    mn_sb = sb_at(165888, "mn", [128, GT], F32)
    w2b_sb = sb_at(49152, "w2b", [128, 2 * I_FFN], BF16)
    wpb_sb = sb_at(71680, "wpb", [128, 2 * C], BF16)
    out_sb = sb_at(79872, "outb", [128, 2 * 512], F32)
    xg_sb = sb_at(83968, "xg", [128, 2 * 512], F32)
    res_sb = sb_at(88064, "res", [128, 2 * 512], F32)
    sbc = sb_at(114688, "sbc", [128, 2 * GT], F32)
    hq = sb_at(118784, "hq", [128, NI * GT], BF16)

    gst = {"last_gates_pe": 0, "last_peo": 0,
           "last_v_o": v_attn_done, "last_v_r": v_attn_done,
           "prev_bq": list(prep_quiesce)}
    v_tile_tr = {}
    v_tile_qx = {}
    tile_jx = {}

    def emit_A_early_tile(g, ti):
        t0 = g * GT
        row = t0 + ti * 128
        gatew = (prep_quiesce if g == 0 else gst["prev_bq"]) if ti == 0 else ()
        xt_slots.load("scalar",
                      lambda jj: xt_sb[:, jj * C:(jj + 1) * C],
                      x_in[row:row + 128, :], extra_waits=gatew)
        wsem, wval = xt_slots.wait_of_next()
        jx = xt_slots.slot()

        def f(e, jx=jx):
            for q in range(4):
                e.bn_stats(bn6[:, q * 6:(q + 1) * 6],
                           xt_sb[:, jx * C + q * 512: jx * C + (q + 1) * 512])
            e.drain()
            e.bn_aggr(mv2[:], bn6[:])
            e.drain()
            return None
        v_st = p.step("vector", f, waits=[(wsem, wval)], inc=sV)

        def f(e):
            e.tensor_scalar(out=colp[:, 0:1], in0=mv2[:, 1:2],
                            scalar1=EPS_LN, scalar2=None, op0=ALU.add)
            e.drain()
            return None
        v_eps = p.step("vector", f, waits=[(sV, v_st)], inc=sV)
        s_sd = p.step("scalar",
                      lambda e: e.activation(out=colp[:, 0:1],
                                             in_=colp[:, 0:1],
                                             func=AF.Sqrt),
                      waits=[(sV, v_eps)], inc=sS)

        def f(e):
            e.reciprocal(colp[:, 1:2], colp[:, 0:1])
            e.drain()
            e.tensor_scalar(out=colp[:, 2:3], in0=mv2[:, 0:1],
                            scalar1=colp[:, 1:2], scalar2=-1.0,
                            op0=ALU.mult, op1=ALU.mult)
            e.drain()
            return None
        v_cl = p.step("vector", f, waits=[(sS, s_sd)], inc=sV)
        tile_jx[(g, ti)] = (jx, v_cl)

    def emit_A_early_tile_b(g, ti):
        (jx, v_cl) = tile_jx[(g, ti)]
        s_xn = p.step("scalar",
                      lambda e, jx=jx: e.activation(
                          out=xn_f[:], in_=xt_sb[:, jx * C:(jx + 1) * C],
                          func=AF.Identity, bias=colp[:, 2:3],
                          scale=colp[:, 1:2]),
                      waits=[(sV, v_cl)], inc=sS)
        xt_slots.done("scalar")

        def f(e):
            e.tensor_reduce(out=colp[:, 3:4], in_=xn_f[:], axis=AX.X,
                            op=ALU.max, apply_absolute_value=True)
            e.drain()
            e.reciprocal(colp[:, 5:6], colp[:, 3:4])
            e.drain()
            e.tensor_scalar(out=colp[:, 5:6], in0=colp[:, 5:6],
                            scalar1=127.0, scalar2=1e8,
                            op0=ALU.mult, op1=ALU.min)
            e.drain()
            e.reciprocal(colp[:, 4:5], colp[:, 5:6])
            e.drain()
            e.tensor_copy(xn_bfile[:], xn_f[:])
            e._custom_dve(RINT, out=xq_bfile[:], in0=xn_f[:],
                          in1=bcast(colp[:, 5:6], C), s0=MAGIC)
            e.drain()
            e.tensor_scalar(out=xq_bfile[:], in0=xq_bfile[:],
                            scalar1=colp[:, 4:5], scalar2=None,
                            op0=ALU.mult)
            e.drain()
            return None
        v_qx = p.step("vector", f, waits=[(sS, s_xn)], inc=sV)
        v_tile_qx[(g, ti)] = v_qx

    def emit_A_early_tr(g, ti):
        v_qx = v_tile_qx[(g, ti)]
        for (src, dst) in ((xn_bfile, xnT), (xq_bfile, xqT)):
            for half in range(2):
                bank = half
                pe_t = tr8(lambda b, src=src, half=half:
                           src[:, (half * 8 + b) * 128:
                               (half * 8 + b + 1) * 128],
                           8, 128, bank, [(sV, v_qx)])
                dstv = dst[:].rearrange("p (i w) -> p i w", w=GT)[
                    :, half * 8:(half + 1) * 8, ti * 128:(ti + 1) * 128]
                tr8_evac(dstv, 8, 128, bank, pe_t)
        v_tile_tr[(g, ti)] = p.v(sV)

    def emit_A_late(g):
        for hcc in range(16):
            lw = []
            if hcc == 0:
                lw = [(sWgsL, 256 * g), (sWgw, wg_written),
                      (sOut, 256 * g), (sV, gst["last_v_o"]),
                      (sP, gst["last_peo"])]
                if g == 0:
                    lw += prep_quiesce
            p.step("gpsimd",
                   lambda e, hcc=hcc: e.dma_start(
                       out=wgs_sb[:, hcc * 1024:(hcc + 1) * 1024],
                       in_=wgS_hbm[hcc * 128:(hcc + 1) * 128, :]),
                   waits=lw, inc=[(sWgsL, 16)])
        wgs_loaded = p.v(sWgsL)

        prev_v_gbf = 0
        for ti in range(NTI):
            wts = [(sWgsL, wgs_loaded), (sV, v_tile_tr[(g, ti)]),
                   (sV, prev_v_gbf)]
            if ti == 0:
                wts.append((sV, gst["last_v_o"]))

            def f(e, ti=ti):
                for half in range(2):
                    for hc in range(16):
                        e.matmul(psf(4 + half)[:, :],
                                 xnT[:, hc * GT + ti * 128:
                                     hc * GT + (ti + 1) * 128],
                                 wgs_sb[:, hc * 1024 + half * 512:
                                        hc * 1024 + (half + 1) * 512],
                                 start=(hc == 0), stop=(hc == 15),
                                 skip_group_check=True)
                return None
            pe_g = p.step("tensor", f, waits=wts, inc=sP)
            gst["last_gates_pe"] = pe_g

            def f(e):
                g3v = ps_f32[:, 4 * 512: 4 * 512 + 1024].rearrange(
                    "p (h l) -> p h l", l=NL)
                e.tensor_reduce(out=gat_c[:], in_=g3v, axis=AX.X, op=ALU.max)
                e.drain()
                e.tensor_tensor(out=gat_s[:].rearrange("p (h l) -> p h l", l=NL),
                                in0=g3v,
                                in1=gat_c[:].unsqueeze(2).broadcast_to(
                                    [128, NH, NL]),
                                op=ALU.subtract)
                e.drain()
                return None
            v_gs = p.step("vector", f, waits=[(sP, pe_g)], inc=sV)

            s_ge = p.step("scalar",
                          lambda e: e.activation(
                              out=gat_e[:], in_=gat_s[:], func=AF.Exp),
                          waits=[(sV, v_gs)], inc=sS)

            def f(e):
                e.tensor_reduce(out=gat_c[:],
                                in_=gat_e[:].rearrange("p (h l) -> p h l", l=NL),
                                axis=AX.X, op=ALU.add)
                e.drain()
                e.reciprocal(gat_c[:], gat_c[:])
                e.drain()
                e.tensor_tensor(out=gat_bf[:].rearrange("p (h l) -> p h l", l=NL),
                                in0=gat_e[:].rearrange("p (h l) -> p h l", l=NL),
                                in1=gat_c[:].unsqueeze(2).broadcast_to(
                                    [128, NH, NL]),
                                op=ALU.mult)
                e.drain()
                return None
            v_gbf = p.step("vector", f, waits=[(sS, s_ge)], inc=sV)
            prev_v_gbf = v_gbf

            pe_t = tr8(lambda b: gat_bf[:, b * 128:(b + 1) * 128],
                       8, 128, 0, [(sV, v_gbf)])
            dstv = gatT[:].rearrange("p (i w) -> p i w", w=GT)[
                :, 0:8, ti * 128:(ti + 1) * 128]
            tr8_evac(dstv, 8, 128, 0, pe_t)
        v_gates_done = p.v(sV)
        return v_gates_done

    def emit_M5_pair(g, hp, v_gates_done, prev_v_wtd):
        for hh in (2 * hp, 2 * hp + 1):
            bank = 4 + (hh % 2)
            wts = [(sV, v_gates_done), (sV, prev_v_wtd.get(hh - 2, 0))]

            def f(e, hh=hh, bank=bank):
                e.matmul(psf(bank)[:, :],
                         attn_sb[(hh % 2) * NL:(hh % 2 + 1) * NL,
                                 (hh // 2) * HD:(hh // 2 + 1) * HD],
                         gatT[(hh % 2) * NL:(hh % 2 + 1) * NL,
                              (hh // 2) * GT:(hh // 2 + 1) * GT],
                         start=True, stop=True, skip_group_check=True)
                return None
            pe_w = p.step("tensor", f, waits=wts, inc=sP)

            def f(e, hh=hh, bank=bank):
                e.tensor_copy(wtd[:, hh * GT:(hh + 1) * GT], psf(bank)[:, :])
                e.drain()
                return None
            v_wtd_t = p.step("vector", f, waits=[(sP, pe_w)], inc=sV)
            prev_v_wtd[hh] = v_wtd_t

    def emit_B(g, v_gates_done):
        v_ln_done = v_tile_tr[(g, NTI - 1)]

        def f(e):
            e.memset(mx_sb[:], NEGF)
            e.memset(mn_sb[:], POSF)
            e.drain()
            return None
        v_mm = p.step("vector", f, waits=list(gst["prev_bq"]), inc=sV)

        pair_si = {}
        pair_vh = {}
        prev_v_wtd = {}
        v_wtd_done = None
        for k in range(NI):
            pe_pair = []
            for half in range(2):
                img_row = 2 * k + half
                quarter = img_row // 22
                lw = [(sW1w[quarter], w1_written[quarter])] if g == 0 else []
                if k == 0 and half == 0:
                    lw += [(sP, max(gst["last_gates_pe"], gst["last_peo"]))]
                    if g == 0:
                        lw += prep_quiesce
                w1slots.load("sync",
                             lambda jj, img_row=img_row:
                                 w1b_sb[:, jj * C:(jj + 1) * C],
                             w1qT_hbm[img_row, :, :], extra_waits=lw)
                wsem, wval = w1slots.wait_of_next()
                jb = w1slots.slot()
                bank = 2 * (k % 2) + half
                wts = [(wsem, wval), (sV, v_ln_done)]
                if k == 0 and half == 0:
                    wts += gst["prev_bq"]
                if half == 0 and (k - 2) in pair_si:
                    wts.append((sS, pair_si[k - 2]))

                def f(e, jb=jb, bank=bank):
                    for hc in range(16):
                        e.matmul(psf(bank)[:, :],
                                 w1b_sb[:, jb * C + hc * 128:
                                        jb * C + (hc + 1) * 128],
                                 xqT[:, hc * GT:(hc + 1) * GT],
                                 start=(hc == 0), stop=(hc == 15),
                                 skip_group_check=True)
                    return None
                pe_t = p.step("tensor", f, waits=wts, inc=sP)
                w1slots.done("tensor")
                pe_pair.append(pe_t)

            sl = k % 2
            swts = [(sP, pe_pair[1]), (sV, pair_vh.get(k - 2, 0))]
            s_si = p.step("scalar",
                          lambda e, sl=sl, k=k: e.activation(
                              out=silu_sb[:, sl * 512:(sl + 1) * 512],
                              in_=psf(2 * (k % 2))[:, :], func=AF.Silu,
                              scale=sw1_all[:, 2 * k:2 * k + 1]),
                          waits=swts, inc=sS)
            s_up = p.step("scalar",
                          lambda e, sl=sl, k=k: e.activation(
                              out=silu_sb[:, (2 + sl) * 512:(3 + sl) * 512],
                              in_=psf(2 * (k % 2) + 1)[:, :], func=AF.Identity,
                              scale=sw1_all[:, 2 * k + 1:2 * k + 2]),
                          waits=[], inc=sS)
            pair_si[k] = s_up

            vwts = [(sS, s_up)]
            if k == 39:
                vwts.append((sOut, 256 * g))

            def f(e, k=k, sl=sl):
                e.tensor_tensor(out=h_fm[:, k * GT:(k + 1) * GT],
                                in0=silu_sb[:, sl * 512:(sl + 1) * 512],
                                in1=silu_sb[:, (2 + sl) * 512:(3 + sl) * 512],
                                op=ALU.mult)
                e.drain()
                return None
            v_h = p.step("vector", f, waits=vwts, inc=sV)
            pair_vh[k] = v_h

            def f(e, k=k):
                e.tensor_tensor(out=mx_sb[:], in0=mx_sb[:],
                                in1=h_fm[:, k * GT:(k + 1) * GT], op=ALU.max)
                e.tensor_tensor(out=mn_sb[:], in0=mn_sb[:],
                                in1=h_fm[:, k * GT:(k + 1) * GT], op=ALU.min)
                e.drain()
                return None
            v_mm = p.step("vector", f, waits=[(sV, v_mm)] if k == 0 else [],
                          inc=sV)

            if 1 <= k <= 8:
                emit_M5_pair(g, k - 1, v_gates_done, prev_v_wtd)
                if k == 8:
                    v_wtd_done = p.v(sV)

        def f(e):
            e.tensor_scalar(out=mn_sb[:], in0=mn_sb[:], scalar1=-1.0,
                            scalar2=None, op0=ALU.mult)
            e.drain()
            e.tensor_tensor(out=mx_sb[:], in0=mx_sb[:], in1=mn_sb[:],
                            op=ALU.max)
            e.drain()
            return None
        v_M = p.step("vector", f, waits=[(sV, v_mm)], inc=sV)

        def f(e):
            for ti in range(NTI):
                e.transpose(psf(4, 128, ti * 128)[:, :],
                            mx_sb[:, ti * 128:(ti + 1) * 128], id_f32[:])
            return None
        pe_Mt = p.step("tensor", f, waits=[(sV, max(v_M, v_wtd_done))],
                       inc=sP)

        def f(e):
            e.tensor_reduce(out=colp[:, 8:12],
                            in_=psf(4)[:, :].rearrange("p (t w) -> p t w", w=128),
                            axis=AX.X, op=ALU.max)
            e.drain()
            e.reciprocal(colp[:, 16:20], colp[:, 8:12])
            e.drain()
            e.tensor_scalar(out=colp[:, 16:20], in0=colp[:, 16:20],
                            scalar1=127.0, scalar2=1e8,
                            op0=ALU.mult, op1=ALU.min)
            e.drain()
            e.reciprocal(colp[:, 12:16], colp[:, 16:20])
            e.drain()
            return None
        v_cols = p.step("vector", f, waits=[(sP, pe_Mt)], inc=sV)

        def f(e):
            for c in range(4):
                e.transpose(psf(5, 128, c * 128)[0:1, :],
                            colp[:, 12 + c:13 + c], id_f32[:])
            for c in range(4):
                e.transpose(psf(4, 128, c * 128)[0:1, :],
                            colp[:, 16 + c:17 + c], id_f32[:])
            return None
        pe_sT = p.step("tensor", f, waits=[(sV, v_cols)], inc=sP)

        def f(e):
            e.tensor_copy(mn_sb[0:1, 0:512], psf(5)[0:1, :])
            e.tensor_copy(mx_sb[0:1, 0:512], psf(4)[0:1, :])
            e.drain()
            return None
        v_sT = p.step("vector", f, waits=[(sP, pe_sT)], inc=sV)

        def f(e):
            for ti in range(NTI):
                e.matmul(psf(4, 128, ti * 128)[:, :], ones_sb[0:1, :],
                         mn_sb[0:1, ti * 128:(ti + 1) * 128],
                         start=True, stop=True, skip_group_check=True)
            for ti in range(NTI):
                e.matmul(psf(5, 128, ti * 128)[:, :], ones_sb[0:1, :],
                         mx_sb[0:1, ti * 128:(ti + 1) * 128],
                         start=True, stop=True, skip_group_check=True)
            return None
        pe_bc = p.step("tensor", f, waits=[(sV, v_sT)], inc=sP)

        def f(e):
            e.tensor_copy(sbc[:, 0:GT], psf(4)[:, :])
            e.tensor_copy(sbc[:, GT:2 * GT], psf(5)[:, :])
            e.drain()
            return None
        v_sbc = p.step("vector", f, waits=[(sP, pe_bc)], inc=sV)

        def f(e, k0=24, k1=NI):
            e._custom_dve(
                RINT,
                out=hq[:, k0 * GT:k1 * GT].rearrange("p (k t) -> p k t", t=GT),
                in0=h_fm[:, k0 * GT:k1 * GT].rearrange("p (k t) -> p k t", t=GT),
                in1=sbc[:, GT:2 * GT].unsqueeze(1).broadcast_to(
                    [128, k1 - k0, GT]),
                s0=MAGIC)
            e.drain()
            return None
        v_hq_p1 = p.step("vector", f, waits=[(sV, v_sbc)], inc=sV)

        def f(e, k0=0, k1=24):
            e._custom_dve(
                RINT,
                out=hq[:, k0 * GT:k1 * GT].rearrange("p (k t) -> p k t", t=GT),
                in0=h_fm[:, k0 * GT:k1 * GT].rearrange("p (k t) -> p k t", t=GT),
                in1=sbc[:, GT:2 * GT].unsqueeze(1).broadcast_to(
                    [128, k1 - k0, GT]),
                s0=MAGIC)
            e.drain()
            return None
        v_hq_done = p.step("vector", f, waits=[], inc=sV)
        b_quiesce = [(sV, v_hq_done), (sG, p.v(sG)), (sS, p.v(sS)),
                     (sP, p.v(sP))]
        qz_p1 = [(sV, v_hq_p1), (sG, p.v(sG)), (sS, p.v(sS)),
                 (sP, p.v(sP))]
        return qz_p1, v_hq_done, b_quiesce, v_wtd_done

    def emit_C(g, v_wtd_done, qz_p1, v_hq_done, b_quiesce, interleave):
        t0 = g * GT
        base_v_o = gst["last_v_o"]
        base_v_r = gst["last_v_r"]
        prev_v_o = {}
        prev_v_r = {}
        pend = []
        last_peo_l = [gst["last_peo"]]

        def emit_phase2(st):
            (osub, jr, tb, v_r) = st
            wts_o = [(sV, max(v_r, prev_v_o.get(osub - 2, base_v_o)))]

            def f(e, jr=jr, tb=tb):
                for ti in range(NTI):
                    e.transpose(psf(tb, 128, ti * 128)[:, :],
                                res_sb[:, jr * 512 + ti * 128:
                                       jr * 512 + (ti + 1) * 128],
                                id_f32[:])
                return None
            pe_o = p.step("tensor", f, waits=wts_o, inc=sP)

            sg = osub % 2
            v_wts = [(sP, pe_o), (sXg, 256 * g + 16 * (osub + 1)),
                     (sOut, 256 * g + (16 * (osub - 1) if osub >= 2 else 0))]

            def f(e, osub=osub, tb=tb, sg=sg):
                src = psf(tb)[:, :].rearrange("p (t w) -> p t w", w=128)
                xsrc = xg_sb[:, sg * 512:(sg + 1) * 512].rearrange(
                    "p (t w) -> p t w", w=128)
                dst = out_sb[:, sg * 512:(sg + 1) * 512].rearrange(
                    "p (t w) -> p t w", w=128)
                e.tensor_tensor(out=dst, in0=src, in1=xsrc, op=ALU.add)
                e.drain()
                return None
            v_o = p.step("vector", f, waits=v_wts, inc=sV)
            prev_v_o[osub] = v_o

            p.step("scalar",
                   lambda e, osub=osub, sg=sg, t0=t0: e.dma_start(
                       out=out_hbm[t0:t0 + GT,
                                   osub * 128:(osub + 1) * 128].rearrange(
                                       "(t q) c -> q t c", q=128),
                       in_=out_sb[:, sg * 512:(sg + 1) * 512].rearrange(
                           "p (t w) -> p t w", w=128)),
                   waits=[(sV, v_o), (sOut, 256 * g + 16 * osub)],
                   inc=[(sOut, 16)])
            return pe_o

        for osub in range(16):
            lw2 = list(qz_p1) if osub == 0 else []
            lwp = list(qz_p1) if osub == 0 else []
            if g == 0 and osub == 0:
                lw2.append((sW2w, w2_written))
                lwp.append((sWpw, wp_written))
            wp_src = wpS_hbm[:, :, osub * 128:(osub + 1) * 128].transpose(
                (1, 0, 2))
            wpslots.load("sync",
                         lambda jj: wpb_sb[:, jj * C:(jj + 1) * C].rearrange(
                             "p (i m) -> p i m", m=128),
                         wp_src, extra_waits=lwp)
            wpw, wpv = wpslots.wait_of_next()
            jp = wpslots.slot()
            w2slots.load("sync",
                         lambda jj: w2b_sb[:, jj * I_FFN:(jj + 1) * I_FFN],
                         w2qT_hbm[osub, :, :], extra_waits=lw2)
            w2w, w2v = w2slots.wait_of_next()
            j2 = w2slots.slot()

            bank_w = osub % 2
            bank_f = 2 + (osub % 2)
            wts_w = [(wpw, wpv),
                     (sV, max(v_wtd_done,
                              prev_v_r.get(osub - 2, base_v_r)))]

            def f(e, jp=jp, bank_w=bank_w):
                for hc in range(16):
                    e.matmul(psf(bank_w)[:, :],
                             wpb_sb[:, jp * C + hc * 128: jp * C + (hc + 1) * 128],
                             wtd[:, hc * GT:(hc + 1) * GT],
                             start=(hc == 0), stop=(hc == 15),
                             skip_group_check=True)
                return None
            pe_wp = p.step("tensor", f, waits=wts_w, inc=sP)
            wpslots.done("tensor")

            wts_f = [(w2w, w2v),
                     (sV, max(v_hq_done, prev_v_r.get(osub - 2, base_v_r)))]

            def f(e, j2=j2, bank_f=bank_f):
                for ic in range(NI):
                    e.matmul(psf(bank_f)[:, :],
                             w2b_sb[:, j2 * I_FFN + ic * 128:
                                    j2 * I_FFN + (ic + 1) * 128],
                             hq[:, ic * GT:(ic + 1) * GT],
                             start=(ic == 0), stop=(ic == NI - 1),
                             skip_group_check=True)
                return None
            pe_f2 = p.step("tensor", f, waits=wts_f, inc=sP)
            w2slots.done("tensor")

            if len(pend) >= 1:
                last_peo_l[0] = emit_phase2(pend.pop(0))

            xg_wts = [(sXg, 256 * g + 16 * osub),
                      (sV, prev_v_o.get(osub - 2, base_v_o))]
            if osub == 0:
                xg_wts += qz_p1 + [(sP, last_peo_l[0])]
            p.step("sync",
                   lambda e, osub=osub, t0=t0: e.dma_start(
                       out=xg_sb[:, (osub % 2) * 512:
                                 (osub % 2 + 1) * 512].rearrange(
                           "p (t w) -> p t w", w=128),
                       in_=x_in[t0:t0 + GT,
                                osub * 128:(osub + 1) * 128].rearrange(
                           "(t q) c -> q t c", q=128)),
                   waits=xg_wts, inc=[(sXg, 16)])

            jr = osub % 2

            def f(e, jr=jr, bank_w=bank_w, bank_f=bank_f, osub=osub):
                e.tensor_scalar(out=res_sb[:, jr * 512:(jr + 1) * 512],
                                in0=psf(bank_f)[:, :],
                                scalar1=sw2_all[:, osub:osub + 1],
                                scalar2=None, op0=ALU.mult)
                e.drain()
                e.tensor_tensor(out=res_sb[:, jr * 512:(jr + 1) * 512],
                                in0=res_sb[:, jr * 512:(jr + 1) * 512],
                                in1=sbc[:, 0:GT], op=ALU.mult)
                e.drain()
                e.tensor_tensor(out=res_sb[:, jr * 512:(jr + 1) * 512],
                                in0=res_sb[:, jr * 512:(jr + 1) * 512],
                                in1=psf(bank_w)[:, :], op=ALU.add)
                e.drain()
                return None
            v_r = p.step("vector", f, waits=[(sP, pe_f2)], inc=sV)
            prev_v_r[osub] = v_r
            pend.append((osub, jr, 4 + (osub % 2), v_r))

            interleave(osub)

        while pend:
            last_peo_l[0] = emit_phase2(pend.pop(0))
        gst["last_peo"] = last_peo_l[0]
        gst["last_v_o"] = p.v(sV)
        gst["last_v_r"] = p.v(sV)

    for ti in range(NTI):
        emit_A_early_tile(0, ti)
        emit_A_early_tile_b(0, ti)
        emit_A_early_tr(0, ti)
    for g in range(NG):
        v_gates_done = emit_A_late(g)
        bret = emit_B(g, v_gates_done)
        v_hq_p1, v_hq_done, b_quiesce, v_wtd_done = bret
        gst["prev_bq"] = b_quiesce

        _pa_at = {1: 0, 4: 1, 7: 2, 10: 3}
        _pb_at = {2: 0, 5: 1, 8: 2, 11: 3}
        _tr_at = {4: 0, 7: 1, 10: 2, 13: 3}

        def ilv(osub, g=g):
            if g + 1 < NG:
                if osub in _tr_at:
                    emit_A_early_tr(g + 1, _tr_at[osub])
                if osub in _pa_at:
                    emit_A_early_tile(g + 1, _pa_at[osub])
                if osub in _pb_at:
                    emit_A_early_tile_b(g + 1, _pb_at[osub])
        emit_C(g, v_wtd_done, v_hq_p1, v_hq_done, b_quiesce, ilv)

    p.step("sync", lambda e: None, waits=[(sOut, p.v(sOut))])

    with nc.Block() as block:
        p.emit(block)
    nc.finalize()
    return nc


_NC_CACHE = None


def kernel(**ins):
    global _NC_CACHE
    xa = np.ascontiguousarray(ins["L0_attn_x"], dtype=np.float32)
    lat = np.ascontiguousarray(ins["L0_attn_latent_queries"],
                               dtype=np.float32).reshape(NL, NH * HD)
    wk = np.ascontiguousarray(ins["L0_attn_Wk"], dtype=np.float32)
    wv = np.ascontiguousarray(ins["L0_attn_Wv"], dtype=np.float32)
    wg = np.ascontiguousarray(ins["L0_attn_Wg"], dtype=np.float32)
    wp = np.ascontiguousarray(ins["L0_attn_Wp"], dtype=np.float32)
    w1 = np.ascontiguousarray(ins["L0_ffn_w_gate_up"], dtype=np.float32)
    w2 = np.ascontiguousarray(ins["L0_ffn_w_down"], dtype=np.float32)

    if _NC_CACHE is None:
        _NC_CACHE = build_program()
    nc = _NC_CACHE

    in_maps = []
    for c in range(8):
        b = c // 4
        r0 = (c % 4) * TT
        in_maps.append({
            "x": xa[b, r0:r0 + TT, :],
            "xh": xa[b, 0:HT, :],
            "lat": lat,
            "wk": wk, "wv": wv, "wg": wg, "wp": wp,
            "w1": w1, "w2": w2,
        })
    res = run_bass_kernel_spmd(nc, in_maps, core_ids=list(range(8)))
    out = np.empty((2, 8192, C), dtype=np.float32)
    for c in range(8):
        b = c // 4
        r0 = (c % 4) * TT
        out[b, r0:r0 + TT, :] = np.asarray(res.results[c]["out"])
    return out


# version 18
# speedup vs baseline: 1.0488x; 1.0176x over previous
import numpy as np
import ml_dtypes
import concourse.bacc as bacc
import concourse.mybir as mybir
from concourse.bass_utils import run_bass_kernel_spmd

F32 = mybir.dt.float32
BF16 = mybir.dt.bfloat16
ALU = mybir.AluOpType
AF = mybir.ActivationFunctionType
AX = mybir.AxisListType

MAGIC = 12582912.0
SCALE = float(1.0 / np.sqrt(128.0))
EPS_LN = 1e-5
INV127 = float(np.float32(1.0) / np.float32(127.0))
EPS_Q = 1e-08
PI = float(np.pi)
TWO_PI = float(2.0 * np.pi)
LN1E4 = float(np.log(10000.0))
NEGF = -3.0e38
POSF = 3.0e38

C = 2048
TT = 2048
NG = 4
GT = TT // NG
NTI = GT // 128
I_FFN = 5632
NI = I_FFN // 128
NH, HD, NL, HT = 16, 128, 64, 64
N1B = 2 * I_FFN // 128


def _register_rint_op():
    import concourse.dve_ops as dve_ops
    from concourse.dve_op import DveOp
    from concourse.dve_spec import Spec, Src0, Src1, C0, lower
    from concourse.dve_table_gen import dve_ver_for
    for op in dve_ops.OPS:
        if op.name == "RINT_SCALE_TT":
            return op
    name = "RINT_SCALE_TT"
    spec = Spec(body=(Src0 * Src1 + C0) - C0)
    ver = dve_ver_for("TRN2")
    compiled = lower(spec, ver=ver, perf_en=False, name=name)
    op = DveOp(name, spec, subdim=False, uops_sha={ver: compiled.sha(ver)},
               reference=lambda in0, in1, s0, s1, imm2: np.rint(in0 * in1))
    row = len(dve_ops.OPS) + dve_ops._CUSTOM_DVE_ROW_BASE
    assert row <= 31, "no free custom-DVE opcode row"
    dve_ops.OPS.append(op)
    dve_ops._SUB_OPCODE_FOR_NAME[name] = row
    dve_ops.CUSTOM_DVE_SPECS[name] = spec
    dve_ops.CUSTOM_DVE_REFS[name] = op.ref
    return op


RINT = _register_rint_op()


class P:
    def __init__(self, nc):
        self.nc = nc
        self.steps = []
        self.count = {}

    def sem(self, name):
        s = self.nc.alloc_semaphore(name)
        self.count[s] = 0
        return s

    def step(self, eng, fn, waits=(), inc=None):
        incs = []
        if inc is not None:
            if not isinstance(inc, (list, tuple)):
                inc = [(inc, 1)]
            for s, n in inc:
                self.count[s] += n
                incs.append((s, n))

        def run(e, fn=fn, waits=tuple(waits), incs=tuple(incs)):
            for s, v in waits:
                if v > 0:
                    e.wait_ge(s, v)
            last = fn(e)
            if incs:
                if last is None:
                    last = e.drain() if hasattr(e, "drain") else e.nop()
                for s, n in incs:
                    last.then_inc(s, n)
        self.steps.append((eng, run))
        return self.count[incs[0][0]] if incs else None

    def v(self, s):
        return self.count[s]

    def emit(self, block):
        steps = self.steps

        def mk(name):
            def f(eng):
                for e, run in steps:
                    if e == name:
                        run(eng)
            return f
        block.sync(mk("sync"))
        block.scalar(mk("scalar"))
        block.vector(mk("vector"))
        block.tensor(mk("tensor"))
        block.gpsimd(mk("gpsimd"))


class Slots:
    def __init__(self, p, name, n):
        self.p, self.n = p, n
        self.ld = [p.sem(f"{name}_ld{j}") for j in range(n)]
        self.fr = [p.sem(f"{name}_fr{j}") for j in range(n)]
        self.il = 0
        self.iu = 0

    def load(self, queue, dst_fn, src_ap, extra_waits=()):
        j = self.il % self.n
        r = self.il // self.n
        waits = list(extra_waits)
        if r > 0:
            waits.append((self.fr[j], r))
            waits.append((self.ld[j], 16 * r))
        self.p.step(queue,
                    lambda e, j=j, src_ap=src_ap, dst_fn=dst_fn:
                        e.dma_start(out=dst_fn(j), in_=src_ap),
                    waits=waits, inc=[(self.ld[j], 16)])
        self.il += 1
        return j

    def wait_of_next(self):
        j = self.iu % self.n
        return (self.ld[j], 16 * (self.iu // self.n + 1))

    def slot(self):
        return self.iu % self.n

    def done(self, eng):
        j = self.iu % self.n
        self.p.step(eng, lambda e: None, inc=[(self.fr[j], 1)])
        self.iu += 1


def build_program():
    nc = bacc.Bacc("TRN2", target_bir_lowering=False)

    x_in = nc.dram_tensor("x", [TT, C], F32, kind="ExternalInput").ap()
    xh_in = nc.dram_tensor("xh", [HT, C], F32, kind="ExternalInput").ap()
    lat_in = nc.dram_tensor("lat", [NL, C], F32, kind="ExternalInput").ap()
    wk_in = nc.dram_tensor("wk", [C, C], F32, kind="ExternalInput").ap()
    wv_in = nc.dram_tensor("wv", [C, C], F32, kind="ExternalInput").ap()
    wg_in = nc.dram_tensor("wg", [C, NH * NL], F32, kind="ExternalInput").ap()
    wp_in = nc.dram_tensor("wp", [C, C], F32, kind="ExternalInput").ap()
    w1_in = nc.dram_tensor("w1", [2 * I_FFN, C], F32, kind="ExternalInput").ap()
    w2_in = nc.dram_tensor("w2", [C, I_FFN], F32, kind="ExternalInput").ap()
    out_hbm = nc.dram_tensor("out", [TT, C], F32, kind="ExternalOutput").ap()

    w1qT_hbm = nc.dram_tensor("w1qT", [N1B, 128, C], BF16, kind="Internal").ap()
    w2qT_hbm = nc.dram_tensor("w2qT", [16, 128, I_FFN], BF16, kind="Internal").ap()
    wpS_hbm = nc.dram_tensor("wpS", [16, 128, C], BF16, kind="Internal").ap()
    wgS_hbm = nc.dram_tensor("wgS", [C, NH * NL], BF16, kind="Internal").ap()

    id_bf_h = nc.inline_tensor(np.eye(128, dtype=ml_dtypes.bfloat16), "id_bf_h").ap()
    id_f32_h = nc.inline_tensor(np.eye(128, dtype=np.float32), "id_f32_h").ap()
    maskv = np.where(np.arange(HT)[None, :] <= np.arange(NL)[:, None],
                     np.float32(0.0), np.float32(-1000000000.0)).astype(np.float32)
    mask_h = nc.inline_tensor(maskv, "cmask").ap()
    tpart_h = nc.inline_tensor(
        np.broadcast_to(np.arange(HT, dtype=np.float32)[:, None],
                        (HT, HT)).copy(), "tpart_h").ap()
    fidx_h = nc.inline_tensor(
        np.broadcast_to(np.arange(HT, dtype=np.float32)[None, :],
                        (HT, HT)).copy(), "fidx_h").ap()
    ones_h = nc.inline_tensor(np.ones((1, 128), dtype=np.float32), "ones_h").ap()

    p = P(nc)

    id_bf = nc.alloc_sbuf_tensor("id_bf", [128, 128], BF16)
    id_f32 = nc.alloc_sbuf_tensor("id_f32", [128, 128], F32)
    attn_sb = nc.alloc_sbuf_tensor("attn_sb", [128, 8 * HD], BF16)
    xqT = nc.alloc_sbuf_tensor("xqT", [128, 16 * GT], BF16)
    wtd = nc.alloc_sbuf_tensor("wtd", [128, 16 * GT], BF16)
    colp = nc.alloc_sbuf_tensor("colp", [128, 24], F32)
    bn6 = nc.alloc_sbuf_tensor("bn6", [128, 24], F32)
    mv2 = nc.alloc_sbuf_tensor("mv2", [128, 2], F32)
    ones_sb = nc.alloc_sbuf_tensor("ones_sb", [1, 128], F32)
    sw1_all = nc.alloc_sbuf_tensor("sw1_all", [128, 2 * NI], F32)
    sw2_all = nc.alloc_sbuf_tensor("sw2_all", [128, 16], F32)

    ps_f32 = nc.alloc_psum_tensor("ps_f32", [128, 3072], F32)
    ps_bf = nc.alloc_psum_tensor("ps_bf", [128, 2048], BF16)

    def psf(bank, w=512, off=0):
        return ps_f32[:, bank * 512 + off: bank * 512 + off + w]

    def psb(bank, w=1024):
        return ps_bf[:, bank * 1024: bank * 1024 + w]

    sP = p.sem("sP")
    sV = p.sem("sV")
    sS = p.sem("sS")
    sG = p.sem("sG")
    sQs = p.sem("sQs")
    sQid = p.sem("sQid")
    sOut = p.sem("sOut")
    sW1w = [p.sem(f"sW1w{i}") for i in range(4)]
    sW2w = p.sem("sW2w")
    sWpw = p.sem("sWpw")
    sWgw = p.sem("sWgw")
    sXg = p.sem("sXg")
    sWgsL = p.sem("sWgsL")

    psb_guard = {0: (None, 0), 1: (None, 0)}
    wtr_guard = {0: None, 1: None}

    def bcast(col_ap, n):
        return col_ap.broadcast_to([col_ap.shape[0], n])

    ARENA_B = (nc.sbuf_top - nc.sbuf_base - 64) // 32 * 32
    arena_h = nc.sbuf_tensor("arena", [128, ARENA_B], mybir.dt.uint8).__enter__()
    AB = nc.lookup_mloc(arena_h).addr

    def _bytes(shape, dt):
        n = 1
        for d in shape[1:]:
            n *= d
        return n * mybir.dt.size(dt)

    def sb_at(off, name, shape, dt):
        assert off % 32 == 0 and off + _bytes(shape, dt) <= ARENA_B, \
            (name, off, _bytes(shape, dt), ARENA_B)
        return nc.alloc_sbuf_tensor_at(name, shape, dt, offset=AB + off)

    _cur = [0]

    def sbuf(name, shape, dt, side="left"):
        off = (_cur[0] + 31) // 32 * 32
        _cur[0] = off + _bytes(shape, dt)
        return sb_at(off, name, shape, dt)

    def scope_mark():
        return _cur[0]

    def scope_close(mark):
        _cur[0] = mark

    def tr8(src_fn, nblk, n_rows, bank, waits):
        w = list(waits) + [psb_guard[bank]]

        def f(e, nblk=nblk, n_rows=n_rows, bank=bank, src_fn=src_fn):
            for b in range(nblk):
                e.transpose(psb(bank)[0:128, b * 128: b * 128 + n_rows],
                            src_fn(b), id_bf[0:n_rows, 0:n_rows])
            return None
        return p.step("tensor", f, waits=w, inc=sP)

    def tr8_evac(dst_ap, nblk, n_rows, bank, pe_t):
        def f(e, dst_ap=dst_ap, nblk=nblk, n_rows=n_rows, bank=bank):
            srcs = psb(bank)[:].rearrange("p (b w) -> p b w", w=128)[
                :, 0:nblk, 0:n_rows]
            e.tensor_copy(dst_ap, srcs)
            e.drain()
            return None
        t = p.step("vector", f, waits=[(sP, pe_t)], inc=sV)
        psb_guard[bank] = (sV, t)
        return t

    def tr8_evac_s(dst_ap, nblk, n_rows, bank, pe_t):
        def f(e, dst_ap=dst_ap, nblk=nblk, n_rows=n_rows, bank=bank):
            srcs = psb(bank)[:].rearrange("p (b w) -> p b w", w=128)[
                :, 0:nblk, 0:n_rows]
            e.activation(out=dst_ap, in_=srcs, func=AF.Copy)
            return None
        t = p.step("scalar", f, waits=[(sP, pe_t)], inc=sS)
        psb_guard[bank] = (sS, t)
        return t

    p.step("scalar", lambda e: e.dma_start(out=id_bf[:], in_=id_bf_h[:, :]),
           inc=[(sQid, 16)])
    p.step("scalar", lambda e: e.dma_start(out=id_f32[:], in_=id_f32_h[:, :]),
           inc=[(sQid, 16)])
    qs_consts = p.v(sQid)

    m_p0 = scope_mark()
    xh_sb = sbuf("xh_sb", [HT, C], F32)
    lat_sb = sbuf("lat_sb", [NL, C], F32)
    mask_sb = sbuf("mask_sb", [NL, HT], F32)
    xnh = sbuf("xnh", [HT, C], F32)
    xnh_bf = sbuf("xnh_bf", [HT, C], BF16)
    xnhT = sbuf("xnhT", [128, 16 * HT], BF16)
    lat_bf = sbuf("lat_bf", [NL, C], BF16)
    latT = sbuf("latT", [128, 16 * NL], BF16)
    kvf = sbuf("kvf", [128, 2 * C], F32)
    kvb = sbuf("kvb", [128, 2 * C], BF16)
    k_tm = sbuf("k_tm", [HT, C], F32)
    v_tm = sbuf("v_tm", [HT, C], F32)
    k_bf = sbuf("k_bf", [HT, C], BF16)
    v_bf = sbuf("v_bf", [HT, C], BF16)
    kT = sbuf("kT", [128, 16 * HT], BF16)
    rt_t = sbuf("rt_t", [HT, HT], F32)
    rt_i = sbuf("rt_i", [HT, HT], F32)
    rt_k = sbuf("rt_k", [HT, HT], F32)
    cos_t = sbuf("cos_t", [HT, HT], F32)
    sin_t = sbuf("sin_t", [HT, HT], F32)
    re1 = sbuf("re1", [HT, C // 2], F32)
    re2 = sbuf("re2", [HT, C // 2], F32)
    re3 = sbuf("re3", [HT, C // 2], F32)
    sc_sb = sbuf("sc_sb", [NL, 2 * HT], F32)
    w_bf = sbuf("w_bf", [NL, 2 * HT], BF16)
    wT_bf = sbuf("wT_bf", [HT, 2 * NL], BF16)
    wpf = sbuf("wpf", [128, 2 * C], F32)
    wqb = sbuf("wqb", [128, 2 * C], BF16)
    wtr = sbuf("wtr", [128, 2 * C], BF16)
    wpfp = sbuf("wpfp", [128, 2 * C], F32)
    wtrp = sbuf("wtrp", [128, 2 * C], BF16)
    wgf = sbuf("wgf", [128, 2 * NH * NL], F32)
    wgb = sbuf("wgb", [128, 2 * NH * NL], BF16)

    def f(e):
        e.memset(colp[0:1, 23:24], 0.25)
        e.drain()
        return None
    v_seed = p.step("vector", f, inc=sV)

    def f(e):
        for fn in (AF.Copy, AF.Identity, AF.Sqrt, AF.Exp, AF.Sin, AF.Silu):
            e.activation(out=colp[0:1, 22:23], in_=colp[0:1, 23:24], func=fn)
            e.drain()
        return None
    p.step("scalar", f, waits=[(sV, v_seed)], inc=sS)

    p.step("scalar", lambda e: e.dma_start(out=xh_sb[:], in_=xh_in[:, :]),
           inc=[(sQs, 16)])
    p.step("scalar", lambda e: e.dma_start(out=lat_sb[:], in_=lat_in[:, :]),
           inc=[(sQs, 16)])
    p.step("scalar", lambda e: e.dma_start(out=mask_sb[:], in_=mask_h[:, :]),
           inc=[(sQs, 16)])
    qs_head = p.v(sQs)

    wst = Slots(p, "wst", 2)
    wgst = Slots(p, "wgst", 2)
    wpst = Slots(p, "wpst", 2)
    wg_slot_guard = {0: 0, 1: 0}
    wp_slot_guard = {0: 0, 1: 0}
    GW = NH * NL

    def emit_wg_chunk(i):
        wgst.load("sync", lambda j: wgf[:, j * GW: (j + 1) * GW],
                  wg_in[i * 128:(i + 1) * 128, :])
        wsem, wval = wgst.wait_of_next()
        j = wgst.slot()

        def f(e, j=j):
            e.tensor_scalar(out=wgb[:, j * GW:(j + 1) * GW],
                            in0=wgf[:, j * GW:(j + 1) * GW],
                            scalar1=SCALE, scalar2=None, op0=ALU.mult)
            e.drain()
            return None
        v_c = p.step("vector", f,
                     waits=[(wsem, wval), (sWgw, wg_slot_guard[j])], inc=sV)
        wgst.done("vector")
        p.step("gpsimd",
               lambda e, i=i, j=j: e.dma_start(
                   out=wgS_hbm[i * 128:(i + 1) * 128, :],
                   in_=wgb[:, j * GW:(j + 1) * GW]),
               waits=[(sV, v_c), (sWgw, p.v(sWgw))], inc=[(sWgw, 16)])
        wg_slot_guard[j] = p.v(sWgw)

    def emit_wp_chunk(ic):
        wpst.load("sync", lambda j: wpfp[:, j * C:(j + 1) * C],
                  wp_in[ic * 128:(ic + 1) * 128, :])
        wsem, wval = wpst.wait_of_next()
        j = wpst.slot()
        s_c = p.step("scalar",
                     lambda e, j=j: e.activation(
                         out=wtrp[:, j * C:(j + 1) * C],
                         in_=wpfp[:, j * C:(j + 1) * C], func=AF.Copy),
                     waits=[(wsem, wval), (sWpw, wp_slot_guard[j])], inc=sS)
        wpst.done("scalar")
        p.step("gpsimd",
               lambda e, ic=ic, j=j: e.dma_start(
                   out=wpS_hbm[ic, :, :],
                   in_=wtrp[:, j * C:(j + 1) * C]),
               waits=[(sS, s_c), (sWpw, p.v(sWpw))], inc=[(sWpw, 16)])
        wp_slot_guard[j] = p.v(sWpw)


    trslot = [0]
    state = {"tr": {0: 0, 1: 0}}

    def prep_chunk(src_row_ap, W, dst_img_row, done_sem, cpar,
                   b_wpf, b_wqb, b_wtr, scol=None, extra_load_waits=()):
        base = 8 + 4 * cpar
        wst.load("sync", lambda j: b_wpf[:, j * W: j * W + W], src_row_ap,
                 extra_waits=extra_load_waits)
        wsem, wval = wst.wait_of_next()
        j = wst.slot()

        def f(e, j=j, W=W, base=base, cpar=cpar, b_wpf=b_wpf, b_wqb=b_wqb,
              scol=scol):
            e.tensor_reduce(out=colp[:, base:base + 1],
                            in_=b_wpf[:, j * W: j * W + W],
                            axis=AX.X, op=ALU.max, apply_absolute_value=True)
            e.drain()
            e.reciprocal(colp[:, base + 2:base + 3], colp[:, base:base + 1])
            e.drain()
            e.tensor_scalar(out=colp[:, base + 2:base + 3],
                            in0=colp[:, base + 2:base + 3], scalar1=127.0,
                            scalar2=1e8, op0=ALU.mult, op1=ALU.min)
            e.drain()
            e.reciprocal(scol, colp[:, base + 2:base + 3])
            e.drain()
            e._custom_dve(RINT, out=b_wqb[:, cpar * W:cpar * W + W],
                          in0=b_wpf[:, j * W: j * W + W],
                          in1=bcast(colp[:, base + 2:base + 3], W), s0=MAGIC)
            e.drain()
            return None
        v_q = p.step("vector", f,
                     waits=[(wsem, wval), (sP, state["tr"][cpar])], inc=sV)
        wst.done("vector")

        ts = trslot[0]
        trslot[0] ^= 1
        nsub = W // 128
        last_s = None
        pe_t = None
        for lo in range(0, nsub, 8):
            hi = min(lo + 8, nsub)
            bank = (lo // 8) % 2
            extra = [(sV, v_q), (sQid, qs_consts)]
            if wtr_guard[ts] is not None:
                extra.append(wtr_guard[ts])
            pe_t = tr8(lambda b, lo=lo, cpar=cpar, W=W, b_wqb=b_wqb:
                       b_wqb[:, cpar * W + (lo + b) * 128:
                             cpar * W + (lo + b + 1) * 128],
                       hi - lo, 128, bank, extra)
            dst = b_wtr[:, ts * W + lo * 128: ts * W + hi * 128].rearrange(
                "p (b w) -> p b w", w=128)
            last_s = tr8_evac_s(dst, hi - lo, 128, bank, pe_t)
        state["tr"][cpar] = pe_t

        wv_t = p.step("gpsimd",
                      lambda e, ts=ts, W=W, dst_img_row=dst_img_row, b_wtr=b_wtr:
                          e.dma_start(out=dst_img_row,
                                      in_=b_wtr[:, ts * W: ts * W + W]),
                      waits=[(sS, last_s), (done_sem, p.v(done_sem))],
                      inc=[(done_sem, 16)])
        wtr_guard[ts] = (done_sem, wv_t)

    def f(e):
        for q in range(4):
            e.bn_stats(bn6[0:HT, q * 6:(q + 1) * 6],
                       xh_sb[:, q * 512:(q + 1) * 512])
        e.drain()
        e.bn_aggr(mv2[0:HT, :], bn6[0:HT, :])
        e.drain()
        return None
    v_hstats = p.step("vector", f, waits=[(sQs, qs_head)], inc=sV)

    def f(e):
        e.tensor_scalar(out=colp[0:HT, 0:1], in0=mv2[0:HT, 1:2],
                        scalar1=EPS_LN, scalar2=None, op0=ALU.add)
        e.drain()
        return None
    v_heps = p.step("vector", f, waits=[(sV, v_hstats)], inc=sV)
    s_hstd = p.step("scalar",
                    lambda e: e.activation(out=colp[0:HT, 0:1], in_=colp[0:HT, 0:1],
                                           func=AF.Sqrt),
                    waits=[(sV, v_heps)], inc=sS)

    def f(e):
        e.reciprocal(colp[0:HT, 1:2], colp[0:HT, 0:1])
        e.drain()
        e.tensor_scalar(out=colp[0:HT, 2:3], in0=mv2[0:HT, 0:1],
                        scalar1=colp[0:HT, 1:2], scalar2=-1.0,
                        op0=ALU.mult, op1=ALU.mult)
        e.drain()
        return None
    v_hcol = p.step("vector", f, waits=[(sS, s_hstd)], inc=sV)

    s_xnh = p.step("scalar",
                   lambda e: e.activation(out=xnh[:], in_=xh_sb[:], func=AF.Identity,
                                          bias=colp[0:HT, 2:3],
                                          scale=colp[0:HT, 1:2]),
                   waits=[(sV, v_hcol)], inc=sS)

    def f(e):
        e.tensor_copy(xnh_bf[:], xnh[:])
        e.tensor_scalar(out=lat_bf[:], in0=lat_sb[:], scalar1=SCALE,
                        scalar2=None, op0=ALU.mult)
        e.drain()
        return None
    v_hbf = p.step("vector", f, waits=[(sS, s_xnh), (sQs, qs_head)], inc=sV)

    def tr16_into(src_sb, n_rows, dst_sb, dstw, wait_v):
        last = None
        for half in range(2):
            bank = half
            pe_t = tr8(lambda b, half=half, src_sb=src_sb, n_rows=n_rows:
                       src_sb[0:n_rows,
                              (half * 8 + b) * 128:(half * 8 + b + 1) * 128],
                       8, n_rows, bank, [(sV, wait_v), (sQid, qs_consts)])
            dst = dst_sb[:].rearrange("p (i w) -> p i w", w=dstw)[
                :, half * 8:(half + 1) * 8, 0:n_rows]
            last = tr8_evac(dst, 8, n_rows, bank, pe_t)
        return last

    v_xnhT = tr16_into(xnh_bf, HT, xnhT, HT, v_hbf)
    v_latT = tr16_into(lat_bf, NL, latT, NL, v_hbf)

    kvslots = Slots(p, "kv", 2)
    kvb_guard = {0: 0, 1: 0}
    kv_pe = [None]

    def emit_kv_chunk(w_in, hc):
        kvslots.load("sync", lambda j: kvf[:, j * C:(j + 1) * C],
                     w_in[hc * 128:(hc + 1) * 128, :])
        wsem, wval = kvslots.wait_of_next()
        j = kvslots.slot()
        s_c = p.step("scalar",
                     lambda e, j=j: e.activation(
                         out=kvb[:, j * C:(j + 1) * C],
                         in_=kvf[:, j * C:(j + 1) * C], func=AF.Copy),
                     waits=[(wsem, wval), (sP, kvb_guard[j])], inc=sS)
        kvslots.done("scalar")

        def f(e, hc=hc, j=j):
            for oc in range(4):
                e.matmul(psf(oc)[0:HT, :],
                         xnhT[:, hc * HT:(hc + 1) * HT],
                         kvb[:, j * C + oc * 512: j * C + (oc + 1) * 512],
                         start=(hc == 0), stop=(hc == 15),
                         skip_group_check=True)
            return None
        pe_t = p.step("tensor", f, waits=[(sS, s_c), (sV, v_xnhT)], inc=sP)
        kvb_guard[j] = pe_t
        kv_pe[0] = pe_t

    def emit_kv_evac(dst):
        def f(e, dst=dst):
            e.tensor_copy(dst[:], ps_f32[0:HT, 0:2048])
            e.drain()
            return None
        return p.step("vector", f, waits=[(sP, kv_pe[0])], inc=sV)

    w1_order = []
    for k in range(NI):
        w1_order.append(k)
        w1_order.append(NI + k)
    for idx, blk in enumerate(w1_order):
        quarter = idx // 22
        prep_chunk(w1_in[blk * 128:(blk + 1) * 128, :], C,
                   w1qT_hbm[idx, :, :], sW1w[quarter], idx % 2,
                   wpf, wqb, wtr, scol=sw1_all[:, idx:idx + 1])
        if idx % 5 == 1 and idx // 5 < 16:
            emit_wg_chunk(idx // 5)
        if idx % 5 == 3 and idx // 5 < 16:
            emit_wp_chunk(idx // 5)
        if 40 <= idx < 56:
            emit_kv_chunk(wk_in, idx - 40)
        if idx == 56:
            v_k = emit_kv_evac(k_tm)
        if 57 <= idx < 73:
            emit_kv_chunk(wv_in, idx - 57)
        if idx == 73:
            v_v = emit_kv_evac(v_tm)
    w1_written = [p.v(s) for s in sW1w]
    wg_written = p.v(sWgw)
    wp_written = p.v(sWpw)

    p.step("scalar", lambda e: e.dma_start(out=rt_t[:], in_=tpart_h[:, :]),
           inc=[(sQid, 16)])
    p.step("scalar", lambda e: e.dma_start(out=rt_i[:], in_=fidx_h[:, :]),
           inc=[(sQid, 16)])
    p.step("scalar", lambda e: e.dma_start(out=ones_sb[:], in_=ones_h[:, :]),
           inc=[(sQid, 16)])
    qs_iota = p.v(sQid)

    def f(e):
        e.tensor_scalar(out=rt_i[:], in0=rt_i[:],
                        scalar1=float(-2.0 * LN1E4 / 128.0), scalar2=None,
                        op0=ALU.mult)
        e.drain()
        return None
    v_invfs = p.step("vector", f, waits=[(sQid, qs_iota)], inc=sV)
    s_invf = p.step("scalar",
                    lambda e: e.activation(out=rt_i[:], in_=rt_i[:], func=AF.Exp),
                    waits=[(sV, v_invfs)], inc=sS)

    def f(e):
        e.tensor_tensor(out=rt_t[:], in0=rt_t[:], in1=rt_i[:], op=ALU.mult)
        e.memset(rt_k[:], float(1.0 / TWO_PI))
        e.drain()
        e._custom_dve(RINT, out=rt_i[:], in0=rt_t[:], in1=rt_k[:], s0=MAGIC)
        e.drain()
        e.scalar_tensor_tensor(out=sin_t[:], in0=rt_i[:], scalar=-TWO_PI,
                               in1=rt_t[:], op0=ALU.mult, op1=ALU.add)
        e.drain()
        e.tensor_scalar(out=sin_t[:], in0=sin_t[:], scalar1=PI, scalar2=-PI,
                        op0=ALU.min, op1=ALU.max)
        e.tensor_scalar(out=rt_t[:], in0=rt_t[:], scalar1=float(PI / 2),
                        scalar2=None, op0=ALU.add)
        e.drain()
        e._custom_dve(RINT, out=rt_i[:], in0=rt_t[:], in1=rt_k[:], s0=MAGIC)
        e.drain()
        e.scalar_tensor_tensor(out=cos_t[:], in0=rt_i[:], scalar=-TWO_PI,
                               in1=rt_t[:], op0=ALU.mult, op1=ALU.add)
        e.drain()
        e.tensor_scalar(out=cos_t[:], in0=cos_t[:], scalar1=PI, scalar2=-PI,
                        op0=ALU.min, op1=ALU.max)
        e.drain()
        return None
    v_ropearg = p.step("vector", f, waits=[(sS, s_invf)], inc=sV)

    def f(e):
        e.activation(out=sin_t[:], in_=sin_t[:], func=AF.Sin)
        e.drain()
        e.activation(out=cos_t[:], in_=cos_t[:], func=AF.Sin)
        e.drain()
        return None
    s_tab = p.step("scalar", f, waits=[(sV, v_ropearg)], inc=sS)

    def f(e):
        k3 = k_tm[:].rearrange("t (h d2 two) -> t h d2 two", h=NH, two=2)
        ke = k3[:, :, :, 0:1].squeeze(3)
        ko = k3[:, :, :, 1:2].squeeze(3)
        cosb = cos_t[:].unsqueeze(1).broadcast_to([HT, NH, HT])
        sinb = sin_t[:].unsqueeze(1).broadcast_to([HT, NH, HT])
        r1 = re1[:].rearrange("t (h f) -> t h f", h=NH)
        r2 = re2[:].rearrange("t (h f) -> t h f", h=NH)
        r3 = re3[:].rearrange("t (h f) -> t h f", h=NH)
        e.tensor_tensor(out=r1, in0=ke, in1=cosb, op=ALU.mult)
        e.tensor_tensor(out=r2, in0=ko, in1=sinb, op=ALU.mult)
        e.tensor_tensor(out=r3, in0=ke, in1=sinb, op=ALU.mult)
        e.drain()
        e.tensor_sub(re1[:], re1[:], re2[:])
        e.drain()
        e.tensor_tensor(out=r2, in0=ko, in1=cosb, op=ALU.mult)
        e.drain()
        e.tensor_add(re2[:], re2[:], re3[:])
        e.drain()
        e.tensor_copy(ke, r1)
        e.tensor_copy(ko, r2)
        e.drain()
        e.tensor_copy(k_bf[:], k_tm[:])
        e.tensor_copy(v_bf[:], v_tm[:])
        e.drain()
        return None
    v_rope = p.step("vector", f, waits=[(sS, s_tab), (sV, v_v)], inc=sV)

    v_kT = tr16_into(k_bf, HT, kT, HT, v_rope)

    prev_vsc = {}
    prev_vat = {}
    for hh in range(NH):
        par = hh % 2
        bsc = 4 - par
        bat = 5 - 3 * par
        cb = 4 if par == 0 else 17

        def f(e, hh=hh, bsc=bsc):
            e.matmul(psf(bsc)[0:NL, 0:HT],
                     latT[:, hh * NL:(hh + 1) * NL],
                     kT[:, hh * HT:(hh + 1) * HT],
                     start=True, stop=True, skip_group_check=True)
            return None
        pe_sc = p.step("tensor", f,
                       waits=[(sV, v_kT), (sV, v_latT),
                              (sV, prev_vsc.get(hh - 2, 0))],
                       inc=sP)

        def f(e, par=par, bsc=bsc, cb=cb):
            sc = sc_sb[:, par * HT:(par + 1) * HT]
            e.tensor_tensor(out=sc, in0=psf(bsc)[0:NL, 0:HT],
                            in1=mask_sb[:], op=ALU.add)
            e.drain()
            e.tensor_reduce(out=colp[0:NL, cb:cb + 1], in_=sc, axis=AX.X,
                            op=ALU.max)
            e.drain()
            e.tensor_scalar(out=sc, in0=sc, scalar1=colp[0:NL, cb:cb + 1],
                            scalar2=None, op0=ALU.subtract)
            e.drain()
            return None
        v_sc = p.step("vector", f, waits=[(sP, pe_sc)], inc=sV)
        prev_vsc[hh] = v_sc

        s_exp = p.step("scalar",
                       lambda e, par=par, cb=cb: e.activation(
                           out=sc_sb[:, par * HT:(par + 1) * HT],
                           in_=sc_sb[:, par * HT:(par + 1) * HT],
                           func=AF.Exp,
                           accum_out=colp[0:NL, cb + 1:cb + 2]),
                       waits=[(sV, v_sc)], inc=sS)

        def f(e, par=par, cb=cb):
            e.reciprocal(colp[0:NL, cb + 2:cb + 3], colp[0:NL, cb + 1:cb + 2])
            e.drain()
            e.tensor_scalar(out=w_bf[:, par * HT:(par + 1) * HT],
                            in0=sc_sb[:, par * HT:(par + 1) * HT],
                            scalar1=colp[0:NL, cb + 2:cb + 3],
                            scalar2=None, op0=ALU.mult)
            e.drain()
            return None
        v_w = p.step("vector", f, waits=[(sS, s_exp)], inc=sV)

        def f(e, par=par):
            e.transpose(psb(par)[0:HT, 0:NL],
                        w_bf[:, par * HT:(par + 1) * HT], id_bf[0:NL, 0:NL])
            return None
        pe_wT = p.step("tensor", f,
                       waits=[(sV, v_w), psb_guard[par]], inc=sP)

        def f(e, par=par):
            e.tensor_copy(wT_bf[:, par * NL:(par + 1) * NL],
                          psb(par)[0:HT, 0:NL])
            e.drain()
            return None
        v_wT = p.step("vector", f, waits=[(sP, pe_wT)], inc=sV)
        psb_guard[par] = (sV, v_wT)

        def f(e, hh=hh, par=par, bat=bat):
            e.matmul(psf(bat)[0:NL, 0:HD], wT_bf[:, par * NL:(par + 1) * NL],
                     v_bf[:, hh * HD:(hh + 1) * HD],
                     start=True, stop=True, skip_group_check=True)
            return None
        pe_at = p.step("tensor", f,
                       waits=[(sV, v_wT), (sV, prev_vat.get(hh - 2, 0))],
                       inc=sP)

        def f(e, hh=hh, bat=bat):
            e.tensor_copy(attn_sb[(hh % 2) * NL:(hh % 2 + 1) * NL,
                                  (hh // 2) * HD:(hh // 2 + 1) * HD],
                          psf(bat)[0:NL, 0:HD])
            e.drain()
            return None
        v_at = p.step("vector", f, waits=[(sP, pe_at)], inc=sV)
        prev_vat[hh] = v_at
    v_attn_done = p.v(sV)




    scope_close(m_p0)
    m_w2 = scope_mark()
    wpf2 = sbuf("wpf2", [128, 2 * I_FFN], F32)
    wqb2 = sbuf("wqb2", [128, 2 * I_FFN], BF16)
    wtr2 = sbuf("wtr2", [128, 2 * I_FFN], BF16)
    wtr_guard[0] = wtr_guard[1] = (sWpw, wp_written)
    w2_gate = [(sV, p.v(sV)), (sS, p.v(sS)), (sP, p.v(sP)), (sG, p.v(sG))]
    for i in range(16):
        prep_chunk(w2_in[i * 128:(i + 1) * 128, :], I_FFN,
                   w2qT_hbm[i, :, :], sW2w, i % 2,
                   wpf2, wqb2, wtr2, scol=sw2_all[:, i:i + 1],
                   extra_load_waits=(w2_gate if i == 0 else ()))
    w2_written = p.v(sW2w)
    scope_close(m_w2)
    prep_quiesce = ([(sV, p.v(sV)), (sS, p.v(sS)), (sP, p.v(sP)),
                     (sW2w, w2_written), (sWpw, wp_written)] +
                    [(sW1w[q], w1_written[q]) for q in range(4)])

    xt_slots = Slots(p, "xt", 2)
    w1slots = Slots(p, "w1s", 4)
    w2slots = Slots(p, "w2s", 2)
    wpslots = Slots(p, "wps", 2)

    xt_sb = sb_at(0, "xt_sb", [128, 2 * C], F32)
    xn_f = sb_at(16384, "xn_f", [128, C], F32)
    xn_bfile = sb_at(24576, "xn_bf", [128, C], BF16)
    xq_bfile = sb_at(28672, "xq_bf", [128, C], BF16)
    xnT = sb_at(32768, "xnT", [128, 16 * GT], BF16)
    gatT = sb_at(49152, "gatT", [128, 8 * GT], BF16)
    gat_s = sb_at(57344, "gat_s", [128, NH * NL], F32)
    gat_e = sb_at(61440, "gat_e", [128, NH * NL], F32)
    gat_bf = sb_at(65536, "gat_bf", [128, NH * NL], BF16)
    gat_c = sb_at(67584, "gat_c", [128, NH], F32)
    wgs_sb = sb_at(67648, "wgs", [128, 16 * 1024], BF16)
    h_fm = sb_at(0, "h_fm", [128, NI * GT], F32)
    w1b_sb = sb_at(90112, "w1b", [128, 4 * C], BF16)
    silu_sb = sb_at(106496, "silu", [128, 4 * 512], F32)
    mx_sb = sb_at(163840, "mx", [128, GT], F32)
    mn_sb = sb_at(165888, "mn", [128, GT], F32)
    w2b_sb = sb_at(49152, "w2b", [128, 2 * I_FFN], BF16)
    wpb_sb = sb_at(71680, "wpb", [128, 2 * C], BF16)
    out_sb = sb_at(79872, "outb", [128, 2 * 512], F32)
    xg_sb = sb_at(83968, "xg", [128, 2 * 512], F32)
    res_sb = sb_at(88064, "res", [128, 2 * 512], F32)
    sbc = sb_at(114688, "sbc", [128, 2 * GT], F32)
    hq = sb_at(118784, "hq", [128, NI * GT], BF16)

    gst = {"last_gates_pe": 0, "last_peo": 0,
           "last_v_o": v_attn_done, "last_v_r": v_attn_done,
           "prev_bq": list(prep_quiesce)}
    v_tile_tr = {}
    v_tile_qx = {}
    tile_jx = {}

    def emit_A_early_tile(g, ti):
        t0 = g * GT
        row = t0 + ti * 128
        gatew = (prep_quiesce if g == 0 else gst["prev_bq"]) if ti == 0 else ()
        xt_slots.load("scalar",
                      lambda jj: xt_sb[:, jj * C:(jj + 1) * C],
                      x_in[row:row + 128, :], extra_waits=gatew)
        wsem, wval = xt_slots.wait_of_next()
        jx = xt_slots.slot()

        def f(e, jx=jx):
            for q in range(4):
                e.bn_stats(bn6[:, q * 6:(q + 1) * 6],
                           xt_sb[:, jx * C + q * 512: jx * C + (q + 1) * 512])
            e.drain()
            e.bn_aggr(mv2[:], bn6[:])
            e.drain()
            return None
        v_st = p.step("vector", f, waits=[(wsem, wval)], inc=sV)

        def f(e):
            e.tensor_scalar(out=colp[:, 0:1], in0=mv2[:, 1:2],
                            scalar1=EPS_LN, scalar2=None, op0=ALU.add)
            e.drain()
            return None
        v_eps = p.step("vector", f, waits=[(sV, v_st)], inc=sV)
        s_sd = p.step("scalar",
                      lambda e: e.activation(out=colp[:, 0:1],
                                             in_=colp[:, 0:1],
                                             func=AF.Sqrt),
                      waits=[(sV, v_eps)], inc=sS)

        def f(e):
            e.reciprocal(colp[:, 1:2], colp[:, 0:1])
            e.drain()
            e.tensor_scalar(out=colp[:, 2:3], in0=mv2[:, 0:1],
                            scalar1=colp[:, 1:2], scalar2=-1.0,
                            op0=ALU.mult, op1=ALU.mult)
            e.drain()
            return None
        v_cl = p.step("vector", f, waits=[(sS, s_sd)], inc=sV)
        tile_jx[(g, ti)] = (jx, v_cl)

    def emit_A_early_tile_b(g, ti):
        (jx, v_cl) = tile_jx[(g, ti)]
        s_xn = p.step("scalar",
                      lambda e, jx=jx: e.activation(
                          out=xn_f[:], in_=xt_sb[:, jx * C:(jx + 1) * C],
                          func=AF.Identity, bias=colp[:, 2:3],
                          scale=colp[:, 1:2]),
                      waits=[(sV, v_cl)], inc=sS)
        xt_slots.done("scalar")

        def f(e):
            e.tensor_reduce(out=colp[:, 3:4], in_=xn_f[:], axis=AX.X,
                            op=ALU.max, apply_absolute_value=True)
            e.drain()
            e.reciprocal(colp[:, 5:6], colp[:, 3:4])
            e.drain()
            e.tensor_scalar(out=colp[:, 5:6], in0=colp[:, 5:6],
                            scalar1=127.0, scalar2=1e8,
                            op0=ALU.mult, op1=ALU.min)
            e.drain()
            e.reciprocal(colp[:, 4:5], colp[:, 5:6])
            e.drain()
            e.tensor_copy(xn_bfile[:], xn_f[:])
            e._custom_dve(RINT, out=xq_bfile[:], in0=xn_f[:],
                          in1=bcast(colp[:, 5:6], C), s0=MAGIC)
            e.drain()
            e.tensor_scalar(out=xq_bfile[:], in0=xq_bfile[:],
                            scalar1=colp[:, 4:5], scalar2=None,
                            op0=ALU.mult)
            e.drain()
            return None
        v_qx = p.step("vector", f, waits=[(sS, s_xn)], inc=sV)
        v_tile_qx[(g, ti)] = v_qx

    def emit_A_early_tr(g, ti):
        v_qx = v_tile_qx[(g, ti)]
        for (src, dst) in ((xn_bfile, xnT), (xq_bfile, xqT)):
            for half in range(2):
                bank = half
                pe_t = tr8(lambda b, src=src, half=half:
                           src[:, (half * 8 + b) * 128:
                               (half * 8 + b + 1) * 128],
                           8, 128, bank, [(sV, v_qx)])
                dstv = dst[:].rearrange("p (i w) -> p i w", w=GT)[
                    :, half * 8:(half + 1) * 8, ti * 128:(ti + 1) * 128]
                tr8_evac(dstv, 8, 128, bank, pe_t)
        v_tile_tr[(g, ti)] = p.v(sV)

    def emit_A_late(g):
        for hcc in range(16):
            lw = []
            if hcc == 0:
                lw = [(sWgsL, 256 * g), (sWgw, wg_written),
                      (sOut, 256 * g), (sV, gst["last_v_o"]),
                      (sP, gst["last_peo"])]
                lw += [(sem, p.v(sem)) for sem in w1slots.ld]
                if g == 0:
                    lw += prep_quiesce
            p.step("gpsimd",
                   lambda e, hcc=hcc: e.dma_start(
                       out=wgs_sb[:, hcc * 1024:(hcc + 1) * 1024],
                       in_=wgS_hbm[hcc * 128:(hcc + 1) * 128, :]),
                   waits=lw, inc=[(sWgsL, 16)])
        wgs_loaded = p.v(sWgsL)

        prev_v_gbf = 0
        for ti in range(NTI):
            wts = [(sWgsL, wgs_loaded), (sV, v_tile_tr[(g, ti)]),
                   (sV, prev_v_gbf)]
            if ti == 0:
                wts.append((sV, gst["last_v_o"]))

            def f(e, ti=ti):
                for half in range(2):
                    for hc in range(16):
                        e.matmul(psf(4 + half)[:, :],
                                 xnT[:, hc * GT + ti * 128:
                                     hc * GT + (ti + 1) * 128],
                                 wgs_sb[:, hc * 1024 + half * 512:
                                        hc * 1024 + (half + 1) * 512],
                                 start=(hc == 0), stop=(hc == 15),
                                 skip_group_check=True)
                return None
            pe_g = p.step("tensor", f, waits=wts, inc=sP)
            gst["last_gates_pe"] = pe_g

            def f(e):
                g3v = ps_f32[:, 4 * 512: 4 * 512 + 1024].rearrange(
                    "p (h l) -> p h l", l=NL)
                e.tensor_reduce(out=gat_c[:], in_=g3v, axis=AX.X, op=ALU.max)
                e.drain()
                e.tensor_tensor(out=gat_s[:].rearrange("p (h l) -> p h l", l=NL),
                                in0=g3v,
                                in1=gat_c[:].unsqueeze(2).broadcast_to(
                                    [128, NH, NL]),
                                op=ALU.subtract)
                e.drain()
                return None
            v_gs = p.step("vector", f, waits=[(sP, pe_g)], inc=sV)

            s_ge = p.step("scalar",
                          lambda e: e.activation(
                              out=gat_e[:], in_=gat_s[:], func=AF.Exp),
                          waits=[(sV, v_gs)], inc=sS)

            def f(e):
                e.tensor_reduce(out=gat_c[:],
                                in_=gat_e[:].rearrange("p (h l) -> p h l", l=NL),
                                axis=AX.X, op=ALU.add)
                e.drain()
                e.reciprocal(gat_c[:], gat_c[:])
                e.drain()
                e.tensor_tensor(out=gat_bf[:].rearrange("p (h l) -> p h l", l=NL),
                                in0=gat_e[:].rearrange("p (h l) -> p h l", l=NL),
                                in1=gat_c[:].unsqueeze(2).broadcast_to(
                                    [128, NH, NL]),
                                op=ALU.mult)
                e.drain()
                return None
            v_gbf = p.step("vector", f, waits=[(sS, s_ge)], inc=sV)
            prev_v_gbf = v_gbf

            pe_t = tr8(lambda b: gat_bf[:, b * 128:(b + 1) * 128],
                       8, 128, 0, [(sV, v_gbf)])
            dstv = gatT[:].rearrange("p (i w) -> p i w", w=GT)[
                :, 0:8, ti * 128:(ti + 1) * 128]
            tr8_evac(dstv, 8, 128, 0, pe_t)
        v_gates_done = p.v(sV)
        return v_gates_done

    def emit_M5_pair(g, hp, v_gates_done, prev_v_wtd):
        for hh in (2 * hp, 2 * hp + 1):
            bank = 4 + (hh % 2)
            wts = [(sV, v_gates_done), (sV, prev_v_wtd.get(hh - 2, 0))]

            def f(e, hh=hh, bank=bank):
                e.matmul(psf(bank)[:, :],
                         attn_sb[(hh % 2) * NL:(hh % 2 + 1) * NL,
                                 (hh // 2) * HD:(hh // 2 + 1) * HD],
                         gatT[(hh % 2) * NL:(hh % 2 + 1) * NL,
                              (hh // 2) * GT:(hh // 2 + 1) * GT],
                         start=True, stop=True, skip_group_check=True)
                return None
            pe_w = p.step("tensor", f, waits=wts, inc=sP)

            def f(e, hh=hh, bank=bank):
                e.tensor_copy(wtd[:, hh * GT:(hh + 1) * GT], psf(bank)[:, :])
                e.drain()
                return None
            v_wtd_t = p.step("vector", f, waits=[(sP, pe_w)], inc=sV)
            prev_v_wtd[hh] = v_wtd_t

    def emit_B(g, v_gates_done):
        v_ln_done = v_tile_tr[(g, NTI - 1)]

        def f(e):
            e.memset(mx_sb[:], NEGF)
            e.memset(mn_sb[:], POSF)
            e.drain()
            return None
        v_mm = p.step("vector", f, waits=list(gst["prev_bq"]), inc=sV)

        pair_si = {}
        pair_vh = {}
        prev_v_wtd = {}
        v_wtd_done = None
        for k in range(NI):
            jbs = []
            lws = []
            for half in range(2):
                img_row = 2 * k + half
                quarter = img_row // 22
                lw = [(sW1w[quarter], w1_written[quarter])] if g == 0 else []
                if k == 0 and half == 0:
                    lw += [(sP, max(gst["last_gates_pe"], gst["last_peo"]))]
                    if g == 0:
                        lw += prep_quiesce
                w1slots.load("sync",
                             lambda jj, img_row=img_row:
                                 w1b_sb[:, jj * C:(jj + 1) * C],
                             w1qT_hbm[img_row, :, :], extra_waits=lw)
                ju = w1slots.iu + half
                jbs.append(ju % w1slots.n)
                lws.append((w1slots.ld[ju % w1slots.n],
                            16 * (ju // w1slots.n + 1)))

            wts = lws + [(sV, v_ln_done)]
            if k == 0:
                wts += gst["prev_bq"]
            if (k - 2) in pair_si:
                wts.append((sS, pair_si[k - 2]))

            def f(e, jbs=tuple(jbs), k=k):
                for half in range(2):
                    jb = jbs[half]
                    bank = 2 * (k % 2) + half
                    for hc in range(16):
                        e.matmul(psf(bank)[:, :],
                                 w1b_sb[:, jb * C + hc * 128:
                                        jb * C + (hc + 1) * 128],
                                 xqT[:, hc * GT:(hc + 1) * GT],
                                 start=(hc == 0), stop=(hc == 15),
                                 skip_group_check=True)
                return None
            pe_t = p.step("tensor", f, waits=wts, inc=sP)
            w1slots.done("tensor")
            w1slots.done("tensor")
            pe_pair = [pe_t, pe_t]

            sl = k % 2
            swts = [(sP, pe_pair[1]), (sV, pair_vh.get(k - 2, 0))]
            s_si = p.step("scalar",
                          lambda e, sl=sl, k=k: e.activation(
                              out=silu_sb[:, sl * 512:(sl + 1) * 512],
                              in_=psf(2 * (k % 2))[:, :], func=AF.Silu,
                              scale=sw1_all[:, 2 * k:2 * k + 1]),
                          waits=swts, inc=sS)
            s_up = p.step("scalar",
                          lambda e, sl=sl, k=k: e.activation(
                              out=silu_sb[:, (2 + sl) * 512:(3 + sl) * 512],
                              in_=psf(2 * (k % 2) + 1)[:, :], func=AF.Identity,
                              scale=sw1_all[:, 2 * k + 1:2 * k + 2]),
                          waits=[], inc=sS)
            pair_si[k] = s_up

            vwts = [(sS, s_up)]
            if k == 39:
                vwts.append((sOut, 256 * g))

            def f(e, k=k, sl=sl):
                e.tensor_tensor(out=h_fm[:, k * GT:(k + 1) * GT],
                                in0=silu_sb[:, sl * 512:(sl + 1) * 512],
                                in1=silu_sb[:, (2 + sl) * 512:(3 + sl) * 512],
                                op=ALU.mult)
                e.drain()
                return None
            v_h = p.step("vector", f, waits=vwts, inc=sV)
            pair_vh[k] = v_h

            def f(e, k=k):
                e.tensor_tensor(out=mx_sb[:], in0=mx_sb[:],
                                in1=h_fm[:, k * GT:(k + 1) * GT], op=ALU.max)
                e.tensor_tensor(out=mn_sb[:], in0=mn_sb[:],
                                in1=h_fm[:, k * GT:(k + 1) * GT], op=ALU.min)
                e.drain()
                return None
            v_mm = p.step("vector", f, waits=[(sV, v_mm)] if k == 0 else [],
                          inc=sV)

            if 1 <= k <= 8:
                emit_M5_pair(g, k - 1, v_gates_done, prev_v_wtd)
                if k == 8:
                    v_wtd_done = p.v(sV)

        def f(e):
            e.tensor_scalar(out=mn_sb[:], in0=mn_sb[:], scalar1=-1.0,
                            scalar2=None, op0=ALU.mult)
            e.drain()
            e.tensor_tensor(out=mx_sb[:], in0=mx_sb[:], in1=mn_sb[:],
                            op=ALU.max)
            e.drain()
            return None
        v_M = p.step("vector", f, waits=[(sV, v_mm)], inc=sV)

        def f(e):
            for ti in range(NTI):
                e.transpose(psf(4, 128, ti * 128)[:, :],
                            mx_sb[:, ti * 128:(ti + 1) * 128], id_f32[:])
            return None
        pe_Mt = p.step("tensor", f, waits=[(sV, max(v_M, v_wtd_done))],
                       inc=sP)

        def f(e):
            e.tensor_reduce(out=colp[:, 8:12],
                            in_=psf(4)[:, :].rearrange("p (t w) -> p t w", w=128),
                            axis=AX.X, op=ALU.max)
            e.drain()
            e.reciprocal(colp[:, 16:20], colp[:, 8:12])
            e.drain()
            e.tensor_scalar(out=colp[:, 16:20], in0=colp[:, 16:20],
                            scalar1=127.0, scalar2=1e8,
                            op0=ALU.mult, op1=ALU.min)
            e.drain()
            e.reciprocal(colp[:, 12:16], colp[:, 16:20])
            e.drain()
            return None
        v_cols = p.step("vector", f, waits=[(sP, pe_Mt)], inc=sV)

        def f(e):
            for c in range(4):
                e.transpose(psf(5, 128, c * 128)[0:1, :],
                            colp[:, 12 + c:13 + c], id_f32[:])
            for c in range(4):
                e.transpose(psf(4, 128, c * 128)[0:1, :],
                            colp[:, 16 + c:17 + c], id_f32[:])
            return None
        pe_sT = p.step("tensor", f, waits=[(sV, v_cols)], inc=sP)

        def f(e):
            e.tensor_copy(mn_sb[0:1, 0:512], psf(5)[0:1, :])
            e.tensor_copy(mx_sb[0:1, 0:512], psf(4)[0:1, :])
            e.drain()
            return None
        v_sT = p.step("vector", f, waits=[(sP, pe_sT)], inc=sV)

        def f(e):
            for ti in range(NTI):
                e.matmul(psf(4, 128, ti * 128)[:, :], ones_sb[0:1, :],
                         mn_sb[0:1, ti * 128:(ti + 1) * 128],
                         start=True, stop=True, skip_group_check=True)
            for ti in range(NTI):
                e.matmul(psf(5, 128, ti * 128)[:, :], ones_sb[0:1, :],
                         mx_sb[0:1, ti * 128:(ti + 1) * 128],
                         start=True, stop=True, skip_group_check=True)
            return None
        pe_bc = p.step("tensor", f, waits=[(sV, v_sT)], inc=sP)

        def f(e):
            e.tensor_copy(sbc[:, 0:GT], psf(4)[:, :])
            e.tensor_copy(sbc[:, GT:2 * GT], psf(5)[:, :])
            e.drain()
            return None
        v_sbc = p.step("vector", f, waits=[(sP, pe_bc)], inc=sV)

        def f(e, k0=24, k1=NI):
            e._custom_dve(
                RINT,
                out=hq[:, k0 * GT:k1 * GT].rearrange("p (k t) -> p k t", t=GT),
                in0=h_fm[:, k0 * GT:k1 * GT].rearrange("p (k t) -> p k t", t=GT),
                in1=sbc[:, GT:2 * GT].unsqueeze(1).broadcast_to(
                    [128, k1 - k0, GT]),
                s0=MAGIC)
            e.drain()
            return None
        v_hq_p1 = p.step("vector", f, waits=[(sV, v_sbc)], inc=sV)

        def f(e, k0=0, k1=24):
            e._custom_dve(
                RINT,
                out=hq[:, k0 * GT:k1 * GT].rearrange("p (k t) -> p k t", t=GT),
                in0=h_fm[:, k0 * GT:k1 * GT].rearrange("p (k t) -> p k t", t=GT),
                in1=sbc[:, GT:2 * GT].unsqueeze(1).broadcast_to(
                    [128, k1 - k0, GT]),
                s0=MAGIC)
            e.drain()
            return None
        v_hq_done = p.step("vector", f, waits=[], inc=sV)
        b_quiesce = [(sV, v_hq_done), (sG, p.v(sG)), (sS, p.v(sS)),
                     (sP, p.v(sP))]
        qz_p1 = [(sV, v_hq_p1), (sG, p.v(sG)), (sS, p.v(sS)),
                 (sP, p.v(sP))]
        return qz_p1, v_hq_done, b_quiesce, v_wtd_done

    def emit_C(g, v_wtd_done, qz_p1, v_hq_done, b_quiesce, interleave):
        t0 = g * GT
        base_v_o = gst["last_v_o"]
        base_v_r = gst["last_v_r"]
        prev_v_o = {}
        prev_v_r = {}
        pend = []
        last_peo_l = [gst["last_peo"]]

        def emit_phase2(st):
            (osub, jr, tb, v_r) = st
            wts_o = [(sV, max(v_r, prev_v_o.get(osub - 2, base_v_o)))]

            def f(e, jr=jr, tb=tb):
                for ti in range(NTI):
                    e.transpose(psf(tb, 128, ti * 128)[:, :],
                                res_sb[:, jr * 512 + ti * 128:
                                       jr * 512 + (ti + 1) * 128],
                                id_f32[:])
                return None
            pe_o = p.step("tensor", f, waits=wts_o, inc=sP)

            sg = osub % 2
            v_wts = [(sP, pe_o), (sXg, 256 * g + 16 * (osub + 1)),
                     (sOut, 256 * g + (16 * (osub - 1) if osub >= 2 else 0))]

            def f(e, osub=osub, tb=tb, sg=sg):
                src = psf(tb)[:, :].rearrange("p (t w) -> p t w", w=128)
                xsrc = xg_sb[:, sg * 512:(sg + 1) * 512].rearrange(
                    "p (t w) -> p t w", w=128)
                dst = out_sb[:, sg * 512:(sg + 1) * 512].rearrange(
                    "p (t w) -> p t w", w=128)
                e.tensor_tensor(out=dst, in0=src, in1=xsrc, op=ALU.add)
                e.drain()
                return None
            v_o = p.step("vector", f, waits=v_wts, inc=sV)
            prev_v_o[osub] = v_o

            p.step("scalar",
                   lambda e, osub=osub, sg=sg, t0=t0: e.dma_start(
                       out=out_hbm[t0:t0 + GT,
                                   osub * 128:(osub + 1) * 128].rearrange(
                                       "(t q) c -> q t c", q=128),
                       in_=out_sb[:, sg * 512:(sg + 1) * 512].rearrange(
                           "p (t w) -> p t w", w=128)),
                   waits=[(sV, v_o), (sOut, 256 * g + 16 * osub)],
                   inc=[(sOut, 16)])
            return pe_o

        for osub in range(16):
            lw2 = list(qz_p1) if osub == 0 else []
            lwp = list(qz_p1) if osub == 0 else []
            if g == 0 and osub == 0:
                lw2.append((sW2w, w2_written))
                lwp.append((sWpw, wp_written))
            wp_src = wpS_hbm[:, :, osub * 128:(osub + 1) * 128].transpose(
                (1, 0, 2))
            wpslots.load("sync",
                         lambda jj: wpb_sb[:, jj * C:(jj + 1) * C].rearrange(
                             "p (i m) -> p i m", m=128),
                         wp_src, extra_waits=lwp)
            wpw, wpv = wpslots.wait_of_next()
            jp = wpslots.slot()
            w2slots.load("sync",
                         lambda jj: w2b_sb[:, jj * I_FFN:(jj + 1) * I_FFN],
                         w2qT_hbm[osub, :, :], extra_waits=lw2)
            w2w, w2v = w2slots.wait_of_next()
            j2 = w2slots.slot()

            bank_w = osub % 2
            bank_f = 2 + (osub % 2)
            wts_w = [(wpw, wpv),
                     (sV, max(v_wtd_done,
                              prev_v_r.get(osub - 2, base_v_r)))]

            def f(e, jp=jp, bank_w=bank_w):
                for hc in range(16):
                    e.matmul(psf(bank_w)[:, :],
                             wpb_sb[:, jp * C + hc * 128: jp * C + (hc + 1) * 128],
                             wtd[:, hc * GT:(hc + 1) * GT],
                             start=(hc == 0), stop=(hc == 15),
                             skip_group_check=True)
                return None
            pe_wp = p.step("tensor", f, waits=wts_w, inc=sP)
            wpslots.done("tensor")

            wts_f = [(w2w, w2v),
                     (sV, max(v_hq_done, prev_v_r.get(osub - 2, base_v_r)))]

            def f(e, j2=j2, bank_f=bank_f):
                for ic in range(NI):
                    e.matmul(psf(bank_f)[:, :],
                             w2b_sb[:, j2 * I_FFN + ic * 128:
                                    j2 * I_FFN + (ic + 1) * 128],
                             hq[:, ic * GT:(ic + 1) * GT],
                             start=(ic == 0), stop=(ic == NI - 1),
                             skip_group_check=True)
                return None
            pe_f2 = p.step("tensor", f, waits=wts_f, inc=sP)
            w2slots.done("tensor")

            if len(pend) >= 1:
                last_peo_l[0] = emit_phase2(pend.pop(0))

            xg_wts = [(sXg, 256 * g + 16 * osub),
                      (sV, prev_v_o.get(osub - 2, base_v_o))]
            if osub == 0:
                xg_wts += qz_p1 + [(sP, last_peo_l[0])]
            p.step("sync",
                   lambda e, osub=osub, t0=t0: e.dma_start(
                       out=xg_sb[:, (osub % 2) * 512:
                                 (osub % 2 + 1) * 512].rearrange(
                           "p (t w) -> p t w", w=128),
                       in_=x_in[t0:t0 + GT,
                                osub * 128:(osub + 1) * 128].rearrange(
                           "(t q) c -> q t c", q=128)),
                   waits=xg_wts, inc=[(sXg, 16)])

            jr = osub % 2

            def f(e, jr=jr, bank_w=bank_w, bank_f=bank_f, osub=osub):
                e.tensor_scalar(out=res_sb[:, jr * 512:(jr + 1) * 512],
                                in0=psf(bank_f)[:, :],
                                scalar1=sw2_all[:, osub:osub + 1],
                                scalar2=None, op0=ALU.mult)
                e.drain()
                e.tensor_tensor(out=res_sb[:, jr * 512:(jr + 1) * 512],
                                in0=res_sb[:, jr * 512:(jr + 1) * 512],
                                in1=sbc[:, 0:GT], op=ALU.mult)
                e.drain()
                e.tensor_tensor(out=res_sb[:, jr * 512:(jr + 1) * 512],
                                in0=res_sb[:, jr * 512:(jr + 1) * 512],
                                in1=psf(bank_w)[:, :], op=ALU.add)
                e.drain()
                return None
            v_r = p.step("vector", f, waits=[(sP, pe_f2)], inc=sV)
            prev_v_r[osub] = v_r
            pend.append((osub, jr, 4 + (osub % 2), v_r))

            interleave(osub)

        while pend:
            last_peo_l[0] = emit_phase2(pend.pop(0))
        gst["last_peo"] = last_peo_l[0]
        gst["last_v_o"] = p.v(sV)
        gst["last_v_r"] = p.v(sV)

    for ti in range(NTI):
        emit_A_early_tile(0, ti)
        emit_A_early_tile_b(0, ti)
        emit_A_early_tr(0, ti)
    for g in range(NG):
        v_gates_done = emit_A_late(g)
        bret = emit_B(g, v_gates_done)
        v_hq_p1, v_hq_done, b_quiesce, v_wtd_done = bret
        gst["prev_bq"] = b_quiesce

        _pa_at = {1: 0, 4: 1, 7: 2, 10: 3}
        _pb_at = {2: 0, 5: 1, 8: 2, 11: 3}
        _tr_at = {4: 0, 7: 1, 10: 2, 13: 3}

        def ilv(osub, g=g):
            if g + 1 < NG:
                if osub in _tr_at:
                    emit_A_early_tr(g + 1, _tr_at[osub])
                if osub in _pa_at:
                    emit_A_early_tile(g + 1, _pa_at[osub])
                if osub in _pb_at:
                    emit_A_early_tile_b(g + 1, _pb_at[osub])
        emit_C(g, v_wtd_done, v_hq_p1, v_hq_done, b_quiesce, ilv)

    final_waits = [(sOut, p.v(sOut))]
    for sl_ in (xt_slots, w1slots, w2slots, wpslots, wst, wgst, wpst, kvslots):
        final_waits += [(sem, p.v(sem)) for sem in sl_.ld if p.v(sem) > 0]
    p.step("sync", lambda e: None, waits=final_waits)

    with nc.Block() as block:
        p.emit(block)
    nc.finalize()
    return nc


_NC_CACHE = None


def kernel(**ins):
    global _NC_CACHE
    xa = np.ascontiguousarray(ins["L0_attn_x"], dtype=np.float32)
    lat = np.ascontiguousarray(ins["L0_attn_latent_queries"],
                               dtype=np.float32).reshape(NL, NH * HD)
    wk = np.ascontiguousarray(ins["L0_attn_Wk"], dtype=np.float32)
    wv = np.ascontiguousarray(ins["L0_attn_Wv"], dtype=np.float32)
    wg = np.ascontiguousarray(ins["L0_attn_Wg"], dtype=np.float32)
    wp = np.ascontiguousarray(ins["L0_attn_Wp"], dtype=np.float32)
    w1 = np.ascontiguousarray(ins["L0_ffn_w_gate_up"], dtype=np.float32)
    w2 = np.ascontiguousarray(ins["L0_ffn_w_down"], dtype=np.float32)

    if _NC_CACHE is None:
        _NC_CACHE = build_program()
    nc = _NC_CACHE

    in_maps = []
    for c in range(8):
        b = c // 4
        r0 = (c % 4) * TT
        in_maps.append({
            "x": xa[b, r0:r0 + TT, :],
            "xh": xa[b, 0:HT, :],
            "lat": lat,
            "wk": wk, "wv": wv, "wg": wg, "wp": wp,
            "w1": w1, "w2": w2,
        })
    res = run_bass_kernel_spmd(nc, in_maps, core_ids=list(range(8)))
    out = np.empty((2, 8192, C), dtype=np.float32)
    for c in range(8):
        b = c // 4
        r0 = (c % 4) * TT
        out[b, r0:r0 + TT, :] = np.asarray(res.results[c]["out"])
    return out


# version 19 (final)
# speedup vs baseline: 1.0685x; 1.0188x over previous
import numpy as np
import ml_dtypes
import concourse.bacc as bacc
import concourse.mybir as mybir
from concourse.bass_utils import run_bass_kernel_spmd

F32 = mybir.dt.float32
BF16 = mybir.dt.bfloat16
ALU = mybir.AluOpType
AF = mybir.ActivationFunctionType
AX = mybir.AxisListType

MAGIC = 12582912.0
SCALE = float(1.0 / np.sqrt(128.0))
EPS_LN = 1e-5
INV127 = float(np.float32(1.0) / np.float32(127.0))
EPS_Q = 1e-08
PI = float(np.pi)
TWO_PI = float(2.0 * np.pi)
LN1E4 = float(np.log(10000.0))
NEGF = -3.0e38
POSF = 3.0e38

C = 2048
TT = 2048
NG = 4
GT = TT // NG
NTI = GT // 128
I_FFN = 5632
NI = I_FFN // 128
NH, HD, NL, HT = 16, 128, 64, 64
N1B = 2 * I_FFN // 128


def _register_rint_op():
    import concourse.dve_ops as dve_ops
    from concourse.dve_op import DveOp
    from concourse.dve_spec import Spec, Src0, Src1, C0, lower
    from concourse.dve_table_gen import dve_ver_for
    for op in dve_ops.OPS:
        if op.name == "RINT_SCALE_TT":
            return op
    name = "RINT_SCALE_TT"
    spec = Spec(body=(Src0 * Src1 + C0) - C0)
    ver = dve_ver_for("TRN2")
    compiled = lower(spec, ver=ver, perf_en=False, name=name)
    op = DveOp(name, spec, subdim=False, uops_sha={ver: compiled.sha(ver)},
               reference=lambda in0, in1, s0, s1, imm2: np.rint(in0 * in1))
    row = len(dve_ops.OPS) + dve_ops._CUSTOM_DVE_ROW_BASE
    assert row <= 31, "no free custom-DVE opcode row"
    dve_ops.OPS.append(op)
    dve_ops._SUB_OPCODE_FOR_NAME[name] = row
    dve_ops.CUSTOM_DVE_SPECS[name] = spec
    dve_ops.CUSTOM_DVE_REFS[name] = op.ref
    return op


RINT = _register_rint_op()


class P:
    def __init__(self, nc):
        self.nc = nc
        self.steps = []
        self.count = {}

    def sem(self, name):
        s = self.nc.alloc_semaphore(name)
        self.count[s] = 0
        return s

    def step(self, eng, fn, waits=(), inc=None):
        incs = []
        if inc is not None:
            if not isinstance(inc, (list, tuple)):
                inc = [(inc, 1)]
            for s, n in inc:
                self.count[s] += n
                incs.append((s, n))

        def run(e, fn=fn, waits=tuple(waits), incs=tuple(incs)):
            for s, v in waits:
                if v > 0:
                    e.wait_ge(s, v)
            last = fn(e)
            if incs:
                if last is None:
                    last = e.drain() if hasattr(e, "drain") else e.nop()
                for s, n in incs:
                    last.then_inc(s, n)
        self.steps.append((eng, run))
        return self.count[incs[0][0]] if incs else None

    def v(self, s):
        return self.count[s]

    def emit(self, block):
        steps = self.steps

        def mk(name):
            def f(eng):
                for e, run in steps:
                    if e == name:
                        run(eng)
            return f
        block.sync(mk("sync"))
        block.scalar(mk("scalar"))
        block.vector(mk("vector"))
        block.tensor(mk("tensor"))
        block.gpsimd(mk("gpsimd"))


class Slots:
    def __init__(self, p, name, n):
        self.p, self.n = p, n
        self.ld = [p.sem(f"{name}_ld{j}") for j in range(n)]
        self.fr = [p.sem(f"{name}_fr{j}") for j in range(n)]
        self.il = 0
        self.iu = 0

    def load(self, queue, dst_fn, src_ap, extra_waits=()):
        j = self.il % self.n
        r = self.il // self.n
        waits = list(extra_waits)
        if r > 0:
            waits.append((self.fr[j], r))
            waits.append((self.ld[j], 16 * r))
        self.p.step(queue,
                    lambda e, j=j, src_ap=src_ap, dst_fn=dst_fn:
                        e.dma_start(out=dst_fn(j), in_=src_ap),
                    waits=waits, inc=[(self.ld[j], 16)])
        self.il += 1
        return j

    def wait_of_next(self):
        j = self.iu % self.n
        return (self.ld[j], 16 * (self.iu // self.n + 1))

    def slot(self):
        return self.iu % self.n

    def done(self, eng):
        j = self.iu % self.n
        self.p.step(eng, lambda e: None, inc=[(self.fr[j], 1)])
        self.iu += 1


def build_program():
    nc = bacc.Bacc("TRN2", target_bir_lowering=False)

    x_in = nc.dram_tensor("x", [TT, C], F32, kind="ExternalInput").ap()
    xh_in = nc.dram_tensor("xh", [HT, C], F32, kind="ExternalInput").ap()
    lat_in = nc.dram_tensor("lat", [NL, C], F32, kind="ExternalInput").ap()
    wk_in = nc.dram_tensor("wk", [C, C], F32, kind="ExternalInput").ap()
    wv_in = nc.dram_tensor("wv", [C, C], F32, kind="ExternalInput").ap()
    wg_in = nc.dram_tensor("wg", [C, NH * NL], F32, kind="ExternalInput").ap()
    wp_in = nc.dram_tensor("wp", [C, C], F32, kind="ExternalInput").ap()
    w1_in = nc.dram_tensor("w1", [2 * I_FFN, C], F32, kind="ExternalInput").ap()
    w2_in = nc.dram_tensor("w2", [C, I_FFN], F32, kind="ExternalInput").ap()
    out_hbm = nc.dram_tensor("out", [TT, C], F32, kind="ExternalOutput").ap()

    w1qT_hbm = nc.dram_tensor("w1qT", [N1B, 128, C], BF16, kind="Internal").ap()
    w2qT_hbm = nc.dram_tensor("w2qT", [16, 128, I_FFN], BF16, kind="Internal").ap()
    wpS_hbm = nc.dram_tensor("wpS", [16, 128, C], BF16, kind="Internal").ap()
    wgS_hbm = nc.dram_tensor("wgS", [C, NH * NL], BF16, kind="Internal").ap()

    id_bf_h = nc.inline_tensor(np.eye(128, dtype=ml_dtypes.bfloat16), "id_bf_h").ap()
    id_f32_h = nc.inline_tensor(np.eye(128, dtype=np.float32), "id_f32_h").ap()
    maskv = np.where(np.arange(HT)[None, :] <= np.arange(NL)[:, None],
                     np.float32(0.0), np.float32(-1000000000.0)).astype(np.float32)
    mask_h = nc.inline_tensor(maskv, "cmask").ap()
    tpart_h = nc.inline_tensor(
        np.broadcast_to(np.arange(HT, dtype=np.float32)[:, None],
                        (HT, HT)).copy(), "tpart_h").ap()
    fidx_h = nc.inline_tensor(
        np.broadcast_to(np.arange(HT, dtype=np.float32)[None, :],
                        (HT, HT)).copy(), "fidx_h").ap()
    ones_h = nc.inline_tensor(np.ones((1, 128), dtype=np.float32), "ones_h").ap()

    p = P(nc)

    id_bf = nc.alloc_sbuf_tensor("id_bf", [128, 128], BF16)
    id_f32 = nc.alloc_sbuf_tensor("id_f32", [128, 128], F32)
    attn_sb = nc.alloc_sbuf_tensor("attn_sb", [128, 8 * HD], BF16)
    xqT = nc.alloc_sbuf_tensor("xqT", [128, 16 * GT], BF16)
    wtd = nc.alloc_sbuf_tensor("wtd", [128, 16 * GT], BF16)
    colp = nc.alloc_sbuf_tensor("colp", [128, 24], F32)
    bn6 = nc.alloc_sbuf_tensor("bn6", [128, 24], F32)
    mv2 = nc.alloc_sbuf_tensor("mv2", [128, 2], F32)
    ones_sb = nc.alloc_sbuf_tensor("ones_sb", [1, 128], F32)
    sw1_all = nc.alloc_sbuf_tensor("sw1_all", [128, 2 * NI], F32)
    sw2_all = nc.alloc_sbuf_tensor("sw2_all", [128, 16], F32)

    ps_f32 = nc.alloc_psum_tensor("ps_f32", [128, 3072], F32)
    ps_bf = nc.alloc_psum_tensor("ps_bf", [128, 2048], BF16)

    def psf(bank, w=512, off=0):
        return ps_f32[:, bank * 512 + off: bank * 512 + off + w]

    def psb(bank, w=1024):
        return ps_bf[:, bank * 1024: bank * 1024 + w]

    sP = p.sem("sP")
    sV = p.sem("sV")
    sS = p.sem("sS")
    sG = p.sem("sG")
    sQs = p.sem("sQs")
    sQid = p.sem("sQid")
    sOut = p.sem("sOut")
    sW1w = [p.sem(f"sW1w{i}") for i in range(4)]
    sW2w = p.sem("sW2w")
    sWpw = p.sem("sWpw")
    sWgw = p.sem("sWgw")
    sXg = p.sem("sXg")
    sWgsL = p.sem("sWgsL")

    psb_guard = {0: (None, 0), 1: (None, 0)}
    wtr_guard = {0: None, 1: None}

    def bcast(col_ap, n):
        return col_ap.broadcast_to([col_ap.shape[0], n])

    ARENA_B = (nc.sbuf_top - nc.sbuf_base - 64) // 32 * 32
    arena_h = nc.sbuf_tensor("arena", [128, ARENA_B], mybir.dt.uint8).__enter__()
    AB = nc.lookup_mloc(arena_h).addr

    def _bytes(shape, dt):
        n = 1
        for d in shape[1:]:
            n *= d
        return n * mybir.dt.size(dt)

    def sb_at(off, name, shape, dt):
        assert off % 32 == 0 and off + _bytes(shape, dt) <= ARENA_B, \
            (name, off, _bytes(shape, dt), ARENA_B)
        return nc.alloc_sbuf_tensor_at(name, shape, dt, offset=AB + off)

    _cur = [0]

    def sbuf(name, shape, dt, side="left"):
        off = (_cur[0] + 31) // 32 * 32
        _cur[0] = off + _bytes(shape, dt)
        return sb_at(off, name, shape, dt)

    def scope_mark():
        return _cur[0]

    def scope_close(mark):
        _cur[0] = mark

    def tr8(src_fn, nblk, n_rows, bank, waits):
        w = list(waits) + [psb_guard[bank]]

        def f(e, nblk=nblk, n_rows=n_rows, bank=bank, src_fn=src_fn):
            for b in range(nblk):
                e.transpose(psb(bank)[0:128, b * 128: b * 128 + n_rows],
                            src_fn(b), id_bf[0:n_rows, 0:n_rows])
            return None
        return p.step("tensor", f, waits=w, inc=sP)

    def tr8_evac(dst_ap, nblk, n_rows, bank, pe_t):
        def f(e, dst_ap=dst_ap, nblk=nblk, n_rows=n_rows, bank=bank):
            srcs = psb(bank)[:].rearrange("p (b w) -> p b w", w=128)[
                :, 0:nblk, 0:n_rows]
            e.tensor_copy(dst_ap, srcs)
            e.drain()
            return None
        t = p.step("vector", f, waits=[(sP, pe_t)], inc=sV)
        psb_guard[bank] = (sV, t)
        return t

    def tr8_evac_s(dst_ap, nblk, n_rows, bank, pe_t):
        def f(e, dst_ap=dst_ap, nblk=nblk, n_rows=n_rows, bank=bank):
            srcs = psb(bank)[:].rearrange("p (b w) -> p b w", w=128)[
                :, 0:nblk, 0:n_rows]
            e.activation(out=dst_ap, in_=srcs, func=AF.Copy)
            return None
        t = p.step("scalar", f, waits=[(sP, pe_t)], inc=sS)
        psb_guard[bank] = (sS, t)
        return t

    p.step("scalar", lambda e: e.dma_start(out=id_bf[:], in_=id_bf_h[:, :]),
           inc=[(sQid, 16)])
    p.step("scalar", lambda e: e.dma_start(out=id_f32[:], in_=id_f32_h[:, :]),
           inc=[(sQid, 16)])
    qs_consts = p.v(sQid)

    m_p0 = scope_mark()
    xh_sb = sbuf("xh_sb", [HT, C], F32)
    lat_sb = sbuf("lat_sb", [NL, C], F32)
    mask_sb = sbuf("mask_sb", [NL, HT], F32)
    xnh = sbuf("xnh", [HT, C], F32)
    xnh_bf = sbuf("xnh_bf", [HT, C], BF16)
    xnhT = sbuf("xnhT", [128, 16 * HT], BF16)
    lat_bf = sbuf("lat_bf", [NL, C], BF16)
    latT = sbuf("latT", [128, 16 * NL], BF16)
    kvf = sbuf("kvf", [128, 2 * C], F32)
    kvb = sbuf("kvb", [128, 2 * C], BF16)
    k_tm = sbuf("k_tm", [HT, C], F32)
    v_tm = sbuf("v_tm", [HT, C], F32)
    k_bf = sbuf("k_bf", [HT, C], BF16)
    v_bf = sbuf("v_bf", [HT, C], BF16)
    kT = sbuf("kT", [128, 16 * HT], BF16)
    rt_t = sbuf("rt_t", [HT, HT], F32)
    rt_i = sbuf("rt_i", [HT, HT], F32)
    rt_k = sbuf("rt_k", [HT, HT], F32)
    cos_t = sbuf("cos_t", [HT, HT], F32)
    sin_t = sbuf("sin_t", [HT, HT], F32)
    re1 = sbuf("re1", [HT, C // 2], F32)
    re2 = sbuf("re2", [HT, C // 2], F32)
    re3 = sbuf("re3", [HT, C // 2], F32)
    sc_sb = sbuf("sc_sb", [NL, 2 * HT], F32)
    w_bf = sbuf("w_bf", [NL, 2 * HT], BF16)
    wT_bf = sbuf("wT_bf", [HT, 2 * NL], BF16)
    wpf = sbuf("wpf", [128, 2 * C], F32)
    wqb = sbuf("wqb", [128, 2 * C], BF16)
    wtr = sbuf("wtr", [128, 2 * C], BF16)
    wpfp = sbuf("wpfp", [128, 2 * C], F32)
    wtrp = sbuf("wtrp", [128, 2 * C], BF16)
    wgf = sbuf("wgf", [128, 2 * NH * NL], F32)
    wgb = sbuf("wgb", [128, 2 * NH * NL], BF16)

    def f(e):
        e.memset(colp[0:1, 23:24], 0.25)
        e.drain()
        return None
    v_seed = p.step("vector", f, inc=sV)

    def f(e):
        for fn in (AF.Copy, AF.Identity, AF.Sqrt, AF.Exp, AF.Sin, AF.Silu):
            e.activation(out=colp[0:1, 22:23], in_=colp[0:1, 23:24], func=fn)
            e.drain()
        return None
    p.step("scalar", f, waits=[(sV, v_seed)], inc=sS)

    p.step("scalar", lambda e: e.dma_start(out=xh_sb[:], in_=xh_in[:, :]),
           inc=[(sQs, 16)])
    p.step("scalar", lambda e: e.dma_start(out=lat_sb[:], in_=lat_in[:, :]),
           inc=[(sQs, 16)])
    p.step("scalar", lambda e: e.dma_start(out=mask_sb[:], in_=mask_h[:, :]),
           inc=[(sQs, 16)])
    qs_head = p.v(sQs)

    wst = Slots(p, "wst", 2)
    wgst = Slots(p, "wgst", 2)
    wpst = Slots(p, "wpst", 2)
    wg_slot_guard = {0: 0, 1: 0}
    wp_slot_guard = {0: 0, 1: 0}
    GW = NH * NL

    def emit_wg_chunk(i):
        wgst.load("sync", lambda j: wgf[:, j * GW: (j + 1) * GW],
                  wg_in[i * 128:(i + 1) * 128, :])
        wsem, wval = wgst.wait_of_next()
        j = wgst.slot()

        def f(e, j=j):
            e.tensor_scalar(out=wgb[:, j * GW:(j + 1) * GW],
                            in0=wgf[:, j * GW:(j + 1) * GW],
                            scalar1=SCALE, scalar2=None, op0=ALU.mult)
            e.drain()
            return None
        v_c = p.step("vector", f,
                     waits=[(wsem, wval), (sWgw, wg_slot_guard[j])], inc=sV)
        wgst.done("vector")
        p.step("gpsimd",
               lambda e, i=i, j=j: e.dma_start(
                   out=wgS_hbm[i * 128:(i + 1) * 128, :],
                   in_=wgb[:, j * GW:(j + 1) * GW]),
               waits=[(sV, v_c), (sWgw, p.v(sWgw))], inc=[(sWgw, 16)])
        wg_slot_guard[j] = p.v(sWgw)

    def emit_wp_chunk(ic):
        wpst.load("sync", lambda j: wpfp[:, j * C:(j + 1) * C],
                  wp_in[ic * 128:(ic + 1) * 128, :])
        wsem, wval = wpst.wait_of_next()
        j = wpst.slot()
        s_c = p.step("scalar",
                     lambda e, j=j: e.activation(
                         out=wtrp[:, j * C:(j + 1) * C],
                         in_=wpfp[:, j * C:(j + 1) * C], func=AF.Copy),
                     waits=[(wsem, wval), (sWpw, wp_slot_guard[j])], inc=sS)
        wpst.done("scalar")
        p.step("gpsimd",
               lambda e, ic=ic, j=j: e.dma_start(
                   out=wpS_hbm[ic, :, :],
                   in_=wtrp[:, j * C:(j + 1) * C]),
               waits=[(sS, s_c), (sWpw, p.v(sWpw))], inc=[(sWpw, 16)])
        wp_slot_guard[j] = p.v(sWpw)


    trslot = [0]
    state = {"tr": {0: 0, 1: 0}}

    def prep_chunk(src_row_ap, W, dst_img_row, done_sem, cpar,
                   b_wpf, b_wqb, b_wtr, scol=None, extra_load_waits=()):
        base = 8 + 4 * cpar
        wst.load("sync", lambda j: b_wpf[:, j * W: j * W + W], src_row_ap,
                 extra_waits=extra_load_waits)
        wsem, wval = wst.wait_of_next()
        j = wst.slot()

        def f(e, j=j, W=W, base=base, cpar=cpar, b_wpf=b_wpf, b_wqb=b_wqb,
              scol=scol):
            e.tensor_reduce(out=colp[:, base:base + 1],
                            in_=b_wpf[:, j * W: j * W + W],
                            axis=AX.X, op=ALU.max, apply_absolute_value=True)
            e.drain()
            e.reciprocal(colp[:, base + 2:base + 3], colp[:, base:base + 1])
            e.drain()
            e.tensor_scalar(out=colp[:, base + 2:base + 3],
                            in0=colp[:, base + 2:base + 3], scalar1=127.0,
                            scalar2=1e8, op0=ALU.mult, op1=ALU.min)
            e.drain()
            e.reciprocal(scol, colp[:, base + 2:base + 3])
            e.drain()
            e._custom_dve(RINT, out=b_wqb[:, cpar * W:cpar * W + W],
                          in0=b_wpf[:, j * W: j * W + W],
                          in1=bcast(colp[:, base + 2:base + 3], W), s0=MAGIC)
            e.drain()
            return None
        v_q = p.step("vector", f,
                     waits=[(wsem, wval), (sP, state["tr"][cpar])], inc=sV)
        wst.done("vector")

        ts = trslot[0]
        trslot[0] ^= 1
        nsub = W // 128
        last_s = None
        pe_t = None
        for lo in range(0, nsub, 8):
            hi = min(lo + 8, nsub)
            bank = (lo // 8) % 2
            extra = [(sV, v_q), (sQid, qs_consts)]
            if wtr_guard[ts] is not None:
                extra.append(wtr_guard[ts])
            pe_t = tr8(lambda b, lo=lo, cpar=cpar, W=W, b_wqb=b_wqb:
                       b_wqb[:, cpar * W + (lo + b) * 128:
                             cpar * W + (lo + b + 1) * 128],
                       hi - lo, 128, bank, extra)
            dst = b_wtr[:, ts * W + lo * 128: ts * W + hi * 128].rearrange(
                "p (b w) -> p b w", w=128)
            last_s = tr8_evac_s(dst, hi - lo, 128, bank, pe_t)
        state["tr"][cpar] = pe_t

        wv_t = p.step("gpsimd",
                      lambda e, ts=ts, W=W, dst_img_row=dst_img_row, b_wtr=b_wtr:
                          e.dma_start(out=dst_img_row,
                                      in_=b_wtr[:, ts * W: ts * W + W]),
                      waits=[(sS, last_s), (done_sem, p.v(done_sem))],
                      inc=[(done_sem, 16)])
        wtr_guard[ts] = (done_sem, wv_t)

    def f(e):
        for q in range(4):
            e.bn_stats(bn6[0:HT, q * 6:(q + 1) * 6],
                       xh_sb[:, q * 512:(q + 1) * 512])
        e.drain()
        e.bn_aggr(mv2[0:HT, :], bn6[0:HT, :])
        e.drain()
        return None
    v_hstats = p.step("vector", f, waits=[(sQs, qs_head)], inc=sV)

    def f(e):
        e.tensor_scalar(out=colp[0:HT, 0:1], in0=mv2[0:HT, 1:2],
                        scalar1=EPS_LN, scalar2=None, op0=ALU.add)
        e.drain()
        return None
    v_heps = p.step("vector", f, waits=[(sV, v_hstats)], inc=sV)
    s_hstd = p.step("scalar",
                    lambda e: e.activation(out=colp[0:HT, 0:1], in_=colp[0:HT, 0:1],
                                           func=AF.Sqrt),
                    waits=[(sV, v_heps)], inc=sS)

    def f(e):
        e.reciprocal(colp[0:HT, 1:2], colp[0:HT, 0:1])
        e.drain()
        e.tensor_scalar(out=colp[0:HT, 2:3], in0=mv2[0:HT, 0:1],
                        scalar1=colp[0:HT, 1:2], scalar2=-1.0,
                        op0=ALU.mult, op1=ALU.mult)
        e.drain()
        return None
    v_hcol = p.step("vector", f, waits=[(sS, s_hstd)], inc=sV)

    s_xnh = p.step("scalar",
                   lambda e: e.activation(out=xnh[:], in_=xh_sb[:], func=AF.Identity,
                                          bias=colp[0:HT, 2:3],
                                          scale=colp[0:HT, 1:2]),
                   waits=[(sV, v_hcol)], inc=sS)

    def f(e):
        e.tensor_copy(xnh_bf[:], xnh[:])
        e.tensor_scalar(out=lat_bf[:], in0=lat_sb[:], scalar1=SCALE,
                        scalar2=None, op0=ALU.mult)
        e.drain()
        return None
    v_hbf = p.step("vector", f, waits=[(sS, s_xnh), (sQs, qs_head)], inc=sV)

    def tr16_into(src_sb, n_rows, dst_sb, dstw, wait_v):
        last = None
        for half in range(2):
            bank = half
            pe_t = tr8(lambda b, half=half, src_sb=src_sb, n_rows=n_rows:
                       src_sb[0:n_rows,
                              (half * 8 + b) * 128:(half * 8 + b + 1) * 128],
                       8, n_rows, bank, [(sV, wait_v), (sQid, qs_consts)])
            dst = dst_sb[:].rearrange("p (i w) -> p i w", w=dstw)[
                :, half * 8:(half + 1) * 8, 0:n_rows]
            last = tr8_evac(dst, 8, n_rows, bank, pe_t)
        return last

    v_xnhT = tr16_into(xnh_bf, HT, xnhT, HT, v_hbf)
    v_latT = tr16_into(lat_bf, NL, latT, NL, v_hbf)

    kvslots = Slots(p, "kv", 2)
    kvb_guard = {0: 0, 1: 0}
    kv_pe = [None]

    def emit_kv_chunk(w_in, hc):
        kvslots.load("sync", lambda j: kvf[:, j * C:(j + 1) * C],
                     w_in[hc * 128:(hc + 1) * 128, :])
        wsem, wval = kvslots.wait_of_next()
        j = kvslots.slot()
        s_c = p.step("scalar",
                     lambda e, j=j: e.activation(
                         out=kvb[:, j * C:(j + 1) * C],
                         in_=kvf[:, j * C:(j + 1) * C], func=AF.Copy),
                     waits=[(wsem, wval), (sP, kvb_guard[j])], inc=sS)
        kvslots.done("scalar")

        def f(e, hc=hc, j=j):
            for oc in range(4):
                e.matmul(psf(oc)[0:HT, :],
                         xnhT[:, hc * HT:(hc + 1) * HT],
                         kvb[:, j * C + oc * 512: j * C + (oc + 1) * 512],
                         start=(hc == 0), stop=(hc == 15),
                         skip_group_check=True)
            return None
        pe_t = p.step("tensor", f, waits=[(sS, s_c), (sV, v_xnhT)], inc=sP)
        kvb_guard[j] = pe_t
        kv_pe[0] = pe_t

    def emit_kv_evac(dst):
        def f(e, dst=dst):
            e.tensor_copy(dst[:], ps_f32[0:HT, 0:2048])
            e.drain()
            return None
        return p.step("vector", f, waits=[(sP, kv_pe[0])], inc=sV)

    w1_order = []
    for k in range(NI):
        w1_order.append(k)
        w1_order.append(NI + k)
    for idx, blk in enumerate(w1_order):
        quarter = idx // 22
        prep_chunk(w1_in[blk * 128:(blk + 1) * 128, :], C,
                   w1qT_hbm[idx, :, :], sW1w[quarter], idx % 2,
                   wpf, wqb, wtr, scol=sw1_all[:, idx:idx + 1])
        if idx % 5 == 1 and idx // 5 < 16:
            emit_wg_chunk(idx // 5)
        if idx % 5 == 3 and idx // 5 < 16:
            emit_wp_chunk(idx // 5)
        if 40 <= idx < 56:
            emit_kv_chunk(wk_in, idx - 40)
        if idx == 56:
            v_k = emit_kv_evac(k_tm)
        if 57 <= idx < 73:
            emit_kv_chunk(wv_in, idx - 57)
        if idx == 73:
            v_v = emit_kv_evac(v_tm)
    w1_written = [p.v(s) for s in sW1w]
    wg_written = p.v(sWgw)
    wp_written = p.v(sWpw)

    p.step("scalar", lambda e: e.dma_start(out=rt_t[:], in_=tpart_h[:, :]),
           inc=[(sQid, 16)])
    p.step("scalar", lambda e: e.dma_start(out=rt_i[:], in_=fidx_h[:, :]),
           inc=[(sQid, 16)])
    p.step("scalar", lambda e: e.dma_start(out=ones_sb[:], in_=ones_h[:, :]),
           inc=[(sQid, 16)])
    qs_iota = p.v(sQid)

    def f(e):
        e.tensor_scalar(out=rt_i[:], in0=rt_i[:],
                        scalar1=float(-2.0 * LN1E4 / 128.0), scalar2=None,
                        op0=ALU.mult)
        e.drain()
        return None
    v_invfs = p.step("vector", f, waits=[(sQid, qs_iota)], inc=sV)
    s_invf = p.step("scalar",
                    lambda e: e.activation(out=rt_i[:], in_=rt_i[:], func=AF.Exp),
                    waits=[(sV, v_invfs)], inc=sS)

    def f(e):
        e.tensor_tensor(out=rt_t[:], in0=rt_t[:], in1=rt_i[:], op=ALU.mult)
        e.memset(rt_k[:], float(1.0 / TWO_PI))
        e.drain()
        e._custom_dve(RINT, out=rt_i[:], in0=rt_t[:], in1=rt_k[:], s0=MAGIC)
        e.drain()
        e.scalar_tensor_tensor(out=sin_t[:], in0=rt_i[:], scalar=-TWO_PI,
                               in1=rt_t[:], op0=ALU.mult, op1=ALU.add)
        e.drain()
        e.tensor_scalar(out=sin_t[:], in0=sin_t[:], scalar1=PI, scalar2=-PI,
                        op0=ALU.min, op1=ALU.max)
        e.tensor_scalar(out=rt_t[:], in0=rt_t[:], scalar1=float(PI / 2),
                        scalar2=None, op0=ALU.add)
        e.drain()
        e._custom_dve(RINT, out=rt_i[:], in0=rt_t[:], in1=rt_k[:], s0=MAGIC)
        e.drain()
        e.scalar_tensor_tensor(out=cos_t[:], in0=rt_i[:], scalar=-TWO_PI,
                               in1=rt_t[:], op0=ALU.mult, op1=ALU.add)
        e.drain()
        e.tensor_scalar(out=cos_t[:], in0=cos_t[:], scalar1=PI, scalar2=-PI,
                        op0=ALU.min, op1=ALU.max)
        e.drain()
        return None
    v_ropearg = p.step("vector", f, waits=[(sS, s_invf)], inc=sV)

    def f(e):
        e.activation(out=sin_t[:], in_=sin_t[:], func=AF.Sin)
        e.drain()
        e.activation(out=cos_t[:], in_=cos_t[:], func=AF.Sin)
        e.drain()
        return None
    s_tab = p.step("scalar", f, waits=[(sV, v_ropearg)], inc=sS)

    def f(e):
        k3 = k_tm[:].rearrange("t (h d2 two) -> t h d2 two", h=NH, two=2)
        ke = k3[:, :, :, 0:1].squeeze(3)
        ko = k3[:, :, :, 1:2].squeeze(3)
        cosb = cos_t[:].unsqueeze(1).broadcast_to([HT, NH, HT])
        sinb = sin_t[:].unsqueeze(1).broadcast_to([HT, NH, HT])
        r1 = re1[:].rearrange("t (h f) -> t h f", h=NH)
        r2 = re2[:].rearrange("t (h f) -> t h f", h=NH)
        r3 = re3[:].rearrange("t (h f) -> t h f", h=NH)
        e.tensor_tensor(out=r1, in0=ke, in1=cosb, op=ALU.mult)
        e.tensor_tensor(out=r2, in0=ko, in1=sinb, op=ALU.mult)
        e.tensor_tensor(out=r3, in0=ke, in1=sinb, op=ALU.mult)
        e.drain()
        e.tensor_sub(re1[:], re1[:], re2[:])
        e.drain()
        e.tensor_tensor(out=r2, in0=ko, in1=cosb, op=ALU.mult)
        e.drain()
        e.tensor_add(re2[:], re2[:], re3[:])
        e.drain()
        e.tensor_copy(ke, r1)
        e.tensor_copy(ko, r2)
        e.drain()
        e.tensor_copy(k_bf[:], k_tm[:])
        e.tensor_copy(v_bf[:], v_tm[:])
        e.drain()
        return None
    v_rope = p.step("vector", f, waits=[(sS, s_tab), (sV, v_v)], inc=sV)

    v_kT = tr16_into(k_bf, HT, kT, HT, v_rope)

    prev_vsc = {}
    prev_vat = {}
    for hh in range(NH):
        par = hh % 2
        bsc = 4 - par
        bat = 5 - 3 * par
        cb = 4 if par == 0 else 17

        def f(e, hh=hh, bsc=bsc):
            e.matmul(psf(bsc)[0:NL, 0:HT],
                     latT[:, hh * NL:(hh + 1) * NL],
                     kT[:, hh * HT:(hh + 1) * HT],
                     start=True, stop=True, skip_group_check=True)
            return None
        pe_sc = p.step("tensor", f,
                       waits=[(sV, v_kT), (sV, v_latT),
                              (sV, prev_vsc.get(hh - 2, 0))],
                       inc=sP)

        def f(e, par=par, bsc=bsc, cb=cb):
            sc = sc_sb[:, par * HT:(par + 1) * HT]
            e.tensor_tensor(out=sc, in0=psf(bsc)[0:NL, 0:HT],
                            in1=mask_sb[:], op=ALU.add)
            e.drain()
            e.tensor_reduce(out=colp[0:NL, cb:cb + 1], in_=sc, axis=AX.X,
                            op=ALU.max)
            e.drain()
            e.tensor_scalar(out=sc, in0=sc, scalar1=colp[0:NL, cb:cb + 1],
                            scalar2=None, op0=ALU.subtract)
            e.drain()
            return None
        v_sc = p.step("vector", f, waits=[(sP, pe_sc)], inc=sV)
        prev_vsc[hh] = v_sc

        s_exp = p.step("scalar",
                       lambda e, par=par, cb=cb: e.activation(
                           out=sc_sb[:, par * HT:(par + 1) * HT],
                           in_=sc_sb[:, par * HT:(par + 1) * HT],
                           func=AF.Exp,
                           accum_out=colp[0:NL, cb + 1:cb + 2]),
                       waits=[(sV, v_sc)], inc=sS)

        def f(e, par=par, cb=cb):
            e.reciprocal(colp[0:NL, cb + 2:cb + 3], colp[0:NL, cb + 1:cb + 2])
            e.drain()
            e.tensor_scalar(out=w_bf[:, par * HT:(par + 1) * HT],
                            in0=sc_sb[:, par * HT:(par + 1) * HT],
                            scalar1=colp[0:NL, cb + 2:cb + 3],
                            scalar2=None, op0=ALU.mult)
            e.drain()
            return None
        v_w = p.step("vector", f, waits=[(sS, s_exp)], inc=sV)

        def f(e, par=par):
            e.transpose(psb(par)[0:HT, 0:NL],
                        w_bf[:, par * HT:(par + 1) * HT], id_bf[0:NL, 0:NL])
            return None
        pe_wT = p.step("tensor", f,
                       waits=[(sV, v_w), psb_guard[par]], inc=sP)

        def f(e, par=par):
            e.tensor_copy(wT_bf[:, par * NL:(par + 1) * NL],
                          psb(par)[0:HT, 0:NL])
            e.drain()
            return None
        v_wT = p.step("vector", f, waits=[(sP, pe_wT)], inc=sV)
        psb_guard[par] = (sV, v_wT)

        def f(e, hh=hh, par=par, bat=bat):
            e.matmul(psf(bat)[0:NL, 0:HD], wT_bf[:, par * NL:(par + 1) * NL],
                     v_bf[:, hh * HD:(hh + 1) * HD],
                     start=True, stop=True, skip_group_check=True)
            return None
        pe_at = p.step("tensor", f,
                       waits=[(sV, v_wT), (sV, prev_vat.get(hh - 2, 0))],
                       inc=sP)

        def f(e, hh=hh, bat=bat):
            e.tensor_copy(attn_sb[(hh % 2) * NL:(hh % 2 + 1) * NL,
                                  (hh // 2) * HD:(hh // 2 + 1) * HD],
                          psf(bat)[0:NL, 0:HD])
            e.drain()
            return None
        v_at = p.step("vector", f, waits=[(sP, pe_at)], inc=sV)
        prev_vat[hh] = v_at
    v_attn_done = p.v(sV)




    scope_close(m_p0)
    m_w2 = scope_mark()
    wpf2 = sbuf("wpf2", [128, 2 * I_FFN], F32)
    wqb2 = sbuf("wqb2", [128, 2 * I_FFN], BF16)
    wtr2 = sbuf("wtr2", [128, 2 * I_FFN], BF16)
    wtr_guard[0] = wtr_guard[1] = (sWpw, wp_written)
    w2_gate = [(sV, p.v(sV)), (sS, p.v(sS)), (sP, p.v(sP)), (sG, p.v(sG))]
    for i in range(16):
        prep_chunk(w2_in[i * 128:(i + 1) * 128, :], I_FFN,
                   w2qT_hbm[i, :, :], sW2w, i % 2,
                   wpf2, wqb2, wtr2, scol=sw2_all[:, i:i + 1],
                   extra_load_waits=(w2_gate if i == 0 else ()))
    w2_written = p.v(sW2w)
    scope_close(m_w2)
    prep_quiesce = ([(sV, p.v(sV)), (sS, p.v(sS)), (sP, p.v(sP)),
                     (sW2w, w2_written), (sWpw, wp_written)] +
                    [(sW1w[q], w1_written[q]) for q in range(4)])

    xt_slots = Slots(p, "xt", 2)
    w1slots = Slots(p, "w1s", 4)
    w2slots = Slots(p, "w2s", 2)
    wpslots = Slots(p, "wps", 2)

    xt_sb = sb_at(0, "xt_sb", [128, 2 * C], F32)
    xn_f = sb_at(16384, "xn_f", [128, C], F32)
    xn_bfile = sb_at(24576, "xn_bf", [128, C], BF16)
    xq_bfile = sb_at(28672, "xq_bf", [128, C], BF16)
    xnT = sb_at(32768, "xnT", [128, 16 * GT], BF16)
    gatT = sb_at(49152, "gatT", [128, 8 * GT], BF16)
    gat_s = sb_at(57344, "gat_s", [128, NH * NL], F32)
    gat_e = sb_at(61440, "gat_e", [128, NH * NL], F32)
    gat_bf = sb_at(65536, "gat_bf", [128, NH * NL], BF16)
    gat_c = sb_at(67584, "gat_c", [128, NH], F32)
    wgs_sb = sb_at(67648, "wgs", [128, 16 * 1024], BF16)
    h_fm = sb_at(0, "h_fm", [128, NI * GT], F32)
    w1b_sb = sb_at(90112, "w1b", [128, 4 * C], BF16)
    silu_sb = sb_at(106496, "silu", [128, 4 * 512], F32)
    mx_sb = sb_at(163840, "mx", [128, GT], F32)
    mn_sb = sb_at(165888, "mn", [128, GT], F32)
    w2b_sb = sb_at(49152, "w2b", [128, 2 * I_FFN], BF16)
    wpb_sb = sb_at(71680, "wpb", [128, 2 * C], BF16)
    out_sb = sb_at(79872, "outb", [128, 2 * 512], F32)
    xg_sb = sb_at(83968, "xg", [128, 2 * 512], F32)
    res_sb = sb_at(88064, "res", [128, 2 * 512], F32)
    sbc = sb_at(114688, "sbc", [128, 2 * GT], F32)
    hq = sb_at(118784, "hq", [128, NI * GT], BF16)

    gst = {"last_gates_pe": 0, "last_peo": 0,
           "last_v_o": v_attn_done, "last_v_r": v_attn_done,
           "prev_bq": list(prep_quiesce)}
    v_tile_tr = {}
    v_tile_qx = {}
    tile_jx = {}

    def emit_A_early_tile(g, ti):
        t0 = g * GT
        row = t0 + ti * 128
        gatew = (prep_quiesce if g == 0 else gst["prev_bq"]) if ti == 0 else ()
        xt_slots.load("scalar",
                      lambda jj: xt_sb[:, jj * C:(jj + 1) * C],
                      x_in[row:row + 128, :], extra_waits=gatew)
        wsem, wval = xt_slots.wait_of_next()
        jx = xt_slots.slot()

        def f(e, jx=jx):
            for q in range(4):
                e.bn_stats(bn6[:, q * 6:(q + 1) * 6],
                           xt_sb[:, jx * C + q * 512: jx * C + (q + 1) * 512])
            e.drain()
            e.bn_aggr(mv2[:], bn6[:])
            e.drain()
            return None
        v_st = p.step("vector", f, waits=[(wsem, wval)], inc=sV)

        def f(e):
            e.tensor_scalar(out=colp[:, 0:1], in0=mv2[:, 1:2],
                            scalar1=EPS_LN, scalar2=None, op0=ALU.add)
            e.drain()
            return None
        v_eps = p.step("vector", f, waits=[(sV, v_st)], inc=sV)
        s_sd = p.step("scalar",
                      lambda e: e.activation(out=colp[:, 0:1],
                                             in_=colp[:, 0:1],
                                             func=AF.Sqrt),
                      waits=[(sV, v_eps)], inc=sS)

        def f(e):
            e.reciprocal(colp[:, 1:2], colp[:, 0:1])
            e.drain()
            e.tensor_scalar(out=colp[:, 2:3], in0=mv2[:, 0:1],
                            scalar1=colp[:, 1:2], scalar2=-1.0,
                            op0=ALU.mult, op1=ALU.mult)
            e.drain()
            return None
        v_cl = p.step("vector", f, waits=[(sS, s_sd)], inc=sV)
        tile_jx[(g, ti)] = (jx, v_cl)

    def emit_A_early_tile_b(g, ti):
        (jx, v_cl) = tile_jx[(g, ti)]
        s_xn = p.step("scalar",
                      lambda e, jx=jx: e.activation(
                          out=xn_f[:], in_=xt_sb[:, jx * C:(jx + 1) * C],
                          func=AF.Identity, bias=colp[:, 2:3],
                          scale=colp[:, 1:2]),
                      waits=[(sV, v_cl)], inc=sS)
        xt_slots.done("scalar")

        def f(e):
            e.tensor_reduce(out=colp[:, 3:4], in_=xn_f[:], axis=AX.X,
                            op=ALU.max, apply_absolute_value=True)
            e.drain()
            e.reciprocal(colp[:, 5:6], colp[:, 3:4])
            e.drain()
            e.tensor_scalar(out=colp[:, 5:6], in0=colp[:, 5:6],
                            scalar1=127.0, scalar2=1e8,
                            op0=ALU.mult, op1=ALU.min)
            e.drain()
            e.reciprocal(colp[:, 4:5], colp[:, 5:6])
            e.drain()
            e.tensor_copy(xn_bfile[:], xn_f[:])
            e._custom_dve(RINT, out=xq_bfile[:], in0=xn_f[:],
                          in1=bcast(colp[:, 5:6], C), s0=MAGIC)
            e.drain()
            e.tensor_scalar(out=xq_bfile[:], in0=xq_bfile[:],
                            scalar1=colp[:, 4:5], scalar2=None,
                            op0=ALU.mult)
            e.drain()
            return None
        v_qx = p.step("vector", f, waits=[(sS, s_xn)], inc=sV)
        v_tile_qx[(g, ti)] = v_qx

    def emit_A_early_tr(g, ti):
        v_qx = v_tile_qx[(g, ti)]
        for (src, dst) in ((xn_bfile, xnT), (xq_bfile, xqT)):
            for half in range(2):
                bank = half
                pe_t = tr8(lambda b, src=src, half=half:
                           src[:, (half * 8 + b) * 128:
                               (half * 8 + b + 1) * 128],
                           8, 128, bank, [(sV, v_qx)])
                dstv = dst[:].rearrange("p (i w) -> p i w", w=GT)[
                    :, half * 8:(half + 1) * 8, ti * 128:(ti + 1) * 128]
                tr8_evac(dstv, 8, 128, bank, pe_t)
        v_tile_tr[(g, ti)] = p.v(sV)

    def emit_A_late(g):
        for hcc in range(16):
            lw = []
            if hcc == 0:
                lw = [(sWgsL, 256 * g), (sWgw, wg_written),
                      (sOut, 256 * g), (sV, gst["last_v_o"]),
                      (sP, gst["last_peo"])]
                lw += [(sem, p.v(sem)) for sem in w1slots.ld]
                if g == 0:
                    lw += prep_quiesce
            p.step("gpsimd",
                   lambda e, hcc=hcc: e.dma_start(
                       out=wgs_sb[:, hcc * 1024:(hcc + 1) * 1024],
                       in_=wgS_hbm[hcc * 128:(hcc + 1) * 128, :]),
                   waits=lw, inc=[(sWgsL, 16)])
        wgs_loaded = p.v(sWgsL)

        prev_v_gbf = 0
        for ti in range(NTI):
            wts = [(sWgsL, wgs_loaded), (sV, v_tile_tr[(g, ti)]),
                   (sV, prev_v_gbf)]
            if ti == 0:
                wts.append((sV, gst["last_v_o"]))

            def f(e, ti=ti):
                for half in range(2):
                    for hc in range(16):
                        e.matmul(psf(4 + half)[:, :],
                                 xnT[:, hc * GT + ti * 128:
                                     hc * GT + (ti + 1) * 128],
                                 wgs_sb[:, hc * 1024 + half * 512:
                                        hc * 1024 + (half + 1) * 512],
                                 start=(hc == 0), stop=(hc == 15),
                                 skip_group_check=True)
                return None
            pe_g = p.step("tensor", f, waits=wts, inc=sP)
            gst["last_gates_pe"] = pe_g

            def f(e):
                g3v = ps_f32[:, 4 * 512: 4 * 512 + 1024].rearrange(
                    "p (h l) -> p h l", l=NL)
                e.tensor_reduce(out=gat_c[:], in_=g3v, axis=AX.X, op=ALU.max)
                e.drain()
                e.tensor_tensor(out=gat_s[:].rearrange("p (h l) -> p h l", l=NL),
                                in0=g3v,
                                in1=gat_c[:].unsqueeze(2).broadcast_to(
                                    [128, NH, NL]),
                                op=ALU.subtract)
                e.drain()
                return None
            v_gs = p.step("vector", f, waits=[(sP, pe_g)], inc=sV)

            s_ge = p.step("scalar",
                          lambda e: e.activation(
                              out=gat_e[:], in_=gat_s[:], func=AF.Exp),
                          waits=[(sV, v_gs)], inc=sS)

            def f(e):
                e.tensor_reduce(out=gat_c[:],
                                in_=gat_e[:].rearrange("p (h l) -> p h l", l=NL),
                                axis=AX.X, op=ALU.add)
                e.drain()
                e.reciprocal(gat_c[:], gat_c[:])
                e.drain()
                e.tensor_tensor(out=gat_bf[:].rearrange("p (h l) -> p h l", l=NL),
                                in0=gat_e[:].rearrange("p (h l) -> p h l", l=NL),
                                in1=gat_c[:].unsqueeze(2).broadcast_to(
                                    [128, NH, NL]),
                                op=ALU.mult)
                e.drain()
                return None
            v_gbf = p.step("vector", f, waits=[(sS, s_ge)], inc=sV)
            prev_v_gbf = v_gbf

            pe_t = tr8(lambda b: gat_bf[:, b * 128:(b + 1) * 128],
                       8, 128, 0, [(sV, v_gbf)])
            dstv = gatT[:].rearrange("p (i w) -> p i w", w=GT)[
                :, 0:8, ti * 128:(ti + 1) * 128]
            tr8_evac(dstv, 8, 128, 0, pe_t)
        v_gates_done = p.v(sV)
        return v_gates_done

    def emit_M5_pair(g, hp, v_gates_done, prev_v_wtd):
        wts = [(sV, v_gates_done), (sV, prev_v_wtd.get(hp - 1, 0))]

        def f(e, hp=hp):
            for hh in (2 * hp, 2 * hp + 1):
                e.matmul(psf(4 + (hh % 2))[:, :],
                         attn_sb[(hh % 2) * NL:(hh % 2 + 1) * NL,
                                 (hh // 2) * HD:(hh // 2 + 1) * HD],
                         gatT[(hh % 2) * NL:(hh % 2 + 1) * NL,
                              (hh // 2) * GT:(hh // 2 + 1) * GT],
                         start=True, stop=True, skip_group_check=True)
            return None
        pe_w = p.step("tensor", f, waits=wts, inc=sP)

        def f(e, hp=hp):
            e.tensor_copy(wtd[:, 2 * hp * GT:(2 * hp + 2) * GT],
                          ps_f32[:, 4 * 512:6 * 512])
            e.drain()
            return None
        v_wtd_t = p.step("vector", f, waits=[(sP, pe_w)], inc=sV)
        prev_v_wtd[hp] = v_wtd_t

    def emit_B(g, v_gates_done):
        v_ln_done = v_tile_tr[(g, NTI - 1)]

        def f(e):
            e.memset(mx_sb[:], NEGF)
            e.memset(mn_sb[:], POSF)
            e.drain()
            return None
        v_mm = p.step("vector", f, waits=list(gst["prev_bq"]), inc=sV)

        pair_si = {}
        pair_vh = {}
        prev_v_wtd = {}
        v_wtd_done = None
        for k in range(NI):
            jbs = []
            lws = []
            for half in range(2):
                img_row = 2 * k + half
                quarter = img_row // 22
                lw = [(sW1w[quarter], w1_written[quarter])] if g == 0 else []
                if k == 0 and half == 0:
                    lw += [(sP, max(gst["last_gates_pe"], gst["last_peo"]))]
                    if g == 0:
                        lw += prep_quiesce
                w1slots.load("sync",
                             lambda jj, img_row=img_row:
                                 w1b_sb[:, jj * C:(jj + 1) * C],
                             w1qT_hbm[img_row, :, :], extra_waits=lw)
                ju = w1slots.iu + half
                jbs.append(ju % w1slots.n)
                lws.append((w1slots.ld[ju % w1slots.n],
                            16 * (ju // w1slots.n + 1)))

            wts = lws + [(sV, v_ln_done)]
            if k == 0:
                wts += gst["prev_bq"]
            if (k - 2) in pair_si:
                wts.append((sS, pair_si[k - 2]))

            def f(e, jbs=tuple(jbs), k=k):
                for half in range(2):
                    jb = jbs[half]
                    bank = 2 * (k % 2) + half
                    for hc in range(16):
                        e.matmul(psf(bank)[:, :],
                                 w1b_sb[:, jb * C + hc * 128:
                                        jb * C + (hc + 1) * 128],
                                 xqT[:, hc * GT:(hc + 1) * GT],
                                 start=(hc == 0), stop=(hc == 15),
                                 skip_group_check=True)
                return None
            pe_t = p.step("tensor", f, waits=wts, inc=sP)
            w1slots.done("tensor")
            w1slots.done("tensor")
            pe_pair = [pe_t, pe_t]

            sl = k % 2
            swts = [(sP, pe_pair[1]), (sV, pair_vh.get(k - 2, 0))]
            s_si = p.step("scalar",
                          lambda e, sl=sl, k=k: e.activation(
                              out=silu_sb[:, sl * 512:(sl + 1) * 512],
                              in_=psf(2 * (k % 2))[:, :], func=AF.Silu,
                              scale=sw1_all[:, 2 * k:2 * k + 1]),
                          waits=swts, inc=sS)
            s_up = p.step("scalar",
                          lambda e, sl=sl, k=k: e.activation(
                              out=silu_sb[:, (2 + sl) * 512:(3 + sl) * 512],
                              in_=psf(2 * (k % 2) + 1)[:, :], func=AF.Identity,
                              scale=sw1_all[:, 2 * k + 1:2 * k + 2]),
                          waits=[], inc=sS)
            pair_si[k] = s_up

            vwts = [(sS, s_up)]
            if k == 39:
                vwts.append((sOut, 256 * g))

            def f(e, k=k, sl=sl):
                e.tensor_tensor(out=h_fm[:, k * GT:(k + 1) * GT],
                                in0=silu_sb[:, sl * 512:(sl + 1) * 512],
                                in1=silu_sb[:, (2 + sl) * 512:(3 + sl) * 512],
                                op=ALU.mult)
                e.drain()
                return None
            v_h = p.step("vector", f, waits=vwts, inc=sV)
            pair_vh[k] = v_h

            def f(e, k=k):
                e.tensor_tensor(out=mx_sb[:], in0=mx_sb[:],
                                in1=h_fm[:, k * GT:(k + 1) * GT], op=ALU.max)
                e.tensor_tensor(out=mn_sb[:], in0=mn_sb[:],
                                in1=h_fm[:, k * GT:(k + 1) * GT], op=ALU.min)
                e.drain()
                return None
            v_mm = p.step("vector", f, waits=[(sV, v_mm)] if k == 0 else [],
                          inc=sV)

            if 1 <= k <= 8:
                emit_M5_pair(g, k - 1, v_gates_done, prev_v_wtd)
                if k == 8:
                    v_wtd_done = p.v(sV)

        def f(e):
            e.tensor_scalar(out=mn_sb[:], in0=mn_sb[:], scalar1=-1.0,
                            scalar2=None, op0=ALU.mult)
            e.drain()
            e.tensor_tensor(out=mx_sb[:], in0=mx_sb[:], in1=mn_sb[:],
                            op=ALU.max)
            e.drain()
            return None
        v_M = p.step("vector", f, waits=[(sV, v_mm)], inc=sV)

        def f(e):
            for ti in range(NTI):
                e.transpose(psf(4, 128, ti * 128)[:, :],
                            mx_sb[:, ti * 128:(ti + 1) * 128], id_f32[:])
            return None
        pe_Mt = p.step("tensor", f, waits=[(sV, max(v_M, v_wtd_done))],
                       inc=sP)

        def f(e):
            e.tensor_reduce(out=colp[:, 8:12],
                            in_=psf(4)[:, :].rearrange("p (t w) -> p t w", w=128),
                            axis=AX.X, op=ALU.max)
            e.drain()
            e.reciprocal(colp[:, 16:20], colp[:, 8:12])
            e.drain()
            e.tensor_scalar(out=colp[:, 16:20], in0=colp[:, 16:20],
                            scalar1=127.0, scalar2=1e8,
                            op0=ALU.mult, op1=ALU.min)
            e.drain()
            e.reciprocal(colp[:, 12:16], colp[:, 16:20])
            e.drain()
            return None
        v_cols = p.step("vector", f, waits=[(sP, pe_Mt)], inc=sV)

        def f(e):
            for c in range(4):
                e.transpose(psf(5, 128, c * 128)[0:1, :],
                            colp[:, 12 + c:13 + c], id_f32[:])
            for c in range(4):
                e.transpose(psf(4, 128, c * 128)[0:1, :],
                            colp[:, 16 + c:17 + c], id_f32[:])
            return None
        pe_sT = p.step("tensor", f, waits=[(sV, v_cols)], inc=sP)

        def f(e):
            e.tensor_copy(mn_sb[0:1, 0:512], psf(5)[0:1, :])
            e.tensor_copy(mx_sb[0:1, 0:512], psf(4)[0:1, :])
            e.drain()
            return None
        v_sT = p.step("vector", f, waits=[(sP, pe_sT)], inc=sV)

        def f(e):
            for ti in range(NTI):
                e.matmul(psf(4, 128, ti * 128)[:, :], ones_sb[0:1, :],
                         mn_sb[0:1, ti * 128:(ti + 1) * 128],
                         start=True, stop=True, skip_group_check=True)
            for ti in range(NTI):
                e.matmul(psf(5, 128, ti * 128)[:, :], ones_sb[0:1, :],
                         mx_sb[0:1, ti * 128:(ti + 1) * 128],
                         start=True, stop=True, skip_group_check=True)
            return None
        pe_bc = p.step("tensor", f, waits=[(sV, v_sT)], inc=sP)

        def f(e):
            e.tensor_copy(sbc[:, 0:GT], psf(4)[:, :])
            e.tensor_copy(sbc[:, GT:2 * GT], psf(5)[:, :])
            e.drain()
            return None
        v_sbc = p.step("vector", f, waits=[(sP, pe_bc)], inc=sV)

        def f(e, k0=24, k1=NI):
            e._custom_dve(
                RINT,
                out=hq[:, k0 * GT:k1 * GT].rearrange("p (k t) -> p k t", t=GT),
                in0=h_fm[:, k0 * GT:k1 * GT].rearrange("p (k t) -> p k t", t=GT),
                in1=sbc[:, GT:2 * GT].unsqueeze(1).broadcast_to(
                    [128, k1 - k0, GT]),
                s0=MAGIC)
            e.drain()
            return None
        v_hq_p1 = p.step("vector", f, waits=[(sV, v_sbc)], inc=sV)

        def f(e, k0=0, k1=24):
            e._custom_dve(
                RINT,
                out=hq[:, k0 * GT:k1 * GT].rearrange("p (k t) -> p k t", t=GT),
                in0=h_fm[:, k0 * GT:k1 * GT].rearrange("p (k t) -> p k t", t=GT),
                in1=sbc[:, GT:2 * GT].unsqueeze(1).broadcast_to(
                    [128, k1 - k0, GT]),
                s0=MAGIC)
            e.drain()
            return None
        v_hq_done = p.step("vector", f, waits=[], inc=sV)
        b_quiesce = [(sV, v_hq_done), (sG, p.v(sG)), (sS, p.v(sS)),
                     (sP, p.v(sP))]
        qz_p1 = [(sV, v_hq_p1), (sG, p.v(sG)), (sS, p.v(sS)),
                 (sP, p.v(sP))]
        return qz_p1, v_hq_done, b_quiesce, v_wtd_done

    def emit_C(g, v_wtd_done, qz_p1, v_hq_done, b_quiesce, interleave):
        t0 = g * GT
        base_v_o = gst["last_v_o"]
        base_v_r = gst["last_v_r"]
        prev_v_o = {}
        prev_v_r = {}
        pend = []
        last_peo_l = [gst["last_peo"]]

        def emit_phase2(st):
            (osub, jr, tb, v_r) = st
            wts_o = [(sV, max(v_r, prev_v_o.get(osub - 2, base_v_o)))]

            def f(e, jr=jr, tb=tb):
                for ti in range(NTI):
                    e.transpose(psf(tb, 128, ti * 128)[:, :],
                                res_sb[:, jr * 512 + ti * 128:
                                       jr * 512 + (ti + 1) * 128],
                                id_f32[:])
                return None
            pe_o = p.step("tensor", f, waits=wts_o, inc=sP)

            sg = osub % 2
            v_wts = [(sP, pe_o), (sXg, 256 * g + 16 * (osub + 1)),
                     (sOut, 256 * g + (16 * (osub - 1) if osub >= 2 else 0))]

            def f(e, osub=osub, tb=tb, sg=sg):
                src = psf(tb)[:, :].rearrange("p (t w) -> p t w", w=128)
                xsrc = xg_sb[:, sg * 512:(sg + 1) * 512].rearrange(
                    "p (t w) -> p t w", w=128)
                dst = out_sb[:, sg * 512:(sg + 1) * 512].rearrange(
                    "p (t w) -> p t w", w=128)
                e.tensor_tensor(out=dst, in0=src, in1=xsrc, op=ALU.add)
                e.drain()
                return None
            v_o = p.step("vector", f, waits=v_wts, inc=sV)
            prev_v_o[osub] = v_o

            p.step("scalar",
                   lambda e, osub=osub, sg=sg, t0=t0: e.dma_start(
                       out=out_hbm[t0:t0 + GT,
                                   osub * 128:(osub + 1) * 128].rearrange(
                                       "(t q) c -> q t c", q=128),
                       in_=out_sb[:, sg * 512:(sg + 1) * 512].rearrange(
                           "p (t w) -> p t w", w=128)),
                   waits=[(sV, v_o), (sOut, 256 * g + 16 * osub)],
                   inc=[(sOut, 16)])
            return pe_o

        for osub in range(16):
            lw2 = list(qz_p1) if osub == 0 else []
            lwp = list(qz_p1) if osub == 0 else []
            if g == 0 and osub == 0:
                lw2.append((sW2w, w2_written))
                lwp.append((sWpw, wp_written))
            wp_src = wpS_hbm[:, :, osub * 128:(osub + 1) * 128].transpose(
                (1, 0, 2))
            wpslots.load("sync",
                         lambda jj: wpb_sb[:, jj * C:(jj + 1) * C].rearrange(
                             "p (i m) -> p i m", m=128),
                         wp_src, extra_waits=lwp)
            wpw, wpv = wpslots.wait_of_next()
            jp = wpslots.slot()
            w2slots.load("sync",
                         lambda jj: w2b_sb[:, jj * I_FFN:(jj + 1) * I_FFN],
                         w2qT_hbm[osub, :, :], extra_waits=lw2)
            w2w, w2v = w2slots.wait_of_next()
            j2 = w2slots.slot()

            bank_w = osub % 2
            bank_f = 2 + (osub % 2)
            wts_w = [(wpw, wpv),
                     (sV, max(v_wtd_done,
                              prev_v_r.get(osub - 2, base_v_r)))]

            def f(e, jp=jp, bank_w=bank_w):
                for hc in range(16):
                    e.matmul(psf(bank_w)[:, :],
                             wpb_sb[:, jp * C + hc * 128: jp * C + (hc + 1) * 128],
                             wtd[:, hc * GT:(hc + 1) * GT],
                             start=(hc == 0), stop=(hc == 15),
                             skip_group_check=True)
                return None
            pe_wp = p.step("tensor", f, waits=wts_w, inc=sP)
            wpslots.done("tensor")

            wts_f = [(w2w, w2v),
                     (sV, max(v_hq_done, prev_v_r.get(osub - 2, base_v_r)))]

            def f(e, j2=j2, bank_f=bank_f):
                for ic in range(NI):
                    e.matmul(psf(bank_f)[:, :],
                             w2b_sb[:, j2 * I_FFN + ic * 128:
                                    j2 * I_FFN + (ic + 1) * 128],
                             hq[:, ic * GT:(ic + 1) * GT],
                             start=(ic == 0), stop=(ic == NI - 1),
                             skip_group_check=True)
                return None
            pe_f2 = p.step("tensor", f, waits=wts_f, inc=sP)
            w2slots.done("tensor")

            if len(pend) >= 1:
                last_peo_l[0] = emit_phase2(pend.pop(0))

            xg_wts = [(sXg, 256 * g + 16 * osub),
                      (sV, prev_v_o.get(osub - 2, base_v_o))]
            if osub == 0:
                xg_wts += qz_p1 + [(sP, last_peo_l[0])]
            p.step("sync",
                   lambda e, osub=osub, t0=t0: e.dma_start(
                       out=xg_sb[:, (osub % 2) * 512:
                                 (osub % 2 + 1) * 512].rearrange(
                           "p (t w) -> p t w", w=128),
                       in_=x_in[t0:t0 + GT,
                                osub * 128:(osub + 1) * 128].rearrange(
                           "(t q) c -> q t c", q=128)),
                   waits=xg_wts, inc=[(sXg, 16)])

            jr = osub % 2

            def f(e, jr=jr, bank_w=bank_w, bank_f=bank_f, osub=osub):
                e.tensor_scalar(out=res_sb[:, jr * 512:(jr + 1) * 512],
                                in0=psf(bank_f)[:, :],
                                scalar1=sw2_all[:, osub:osub + 1],
                                scalar2=None, op0=ALU.mult)
                e.drain()
                e.tensor_tensor(out=res_sb[:, jr * 512:(jr + 1) * 512],
                                in0=res_sb[:, jr * 512:(jr + 1) * 512],
                                in1=sbc[:, 0:GT], op=ALU.mult)
                e.drain()
                e.tensor_tensor(out=res_sb[:, jr * 512:(jr + 1) * 512],
                                in0=res_sb[:, jr * 512:(jr + 1) * 512],
                                in1=psf(bank_w)[:, :], op=ALU.add)
                e.drain()
                return None
            v_r = p.step("vector", f, waits=[(sP, pe_f2)], inc=sV)
            prev_v_r[osub] = v_r
            pend.append((osub, jr, 4 + (osub % 2), v_r))

            interleave(osub)

        while pend:
            last_peo_l[0] = emit_phase2(pend.pop(0))
        gst["last_peo"] = last_peo_l[0]
        gst["last_v_o"] = p.v(sV)
        gst["last_v_r"] = p.v(sV)

    for ti in range(NTI):
        emit_A_early_tile(0, ti)
        emit_A_early_tile_b(0, ti)
        emit_A_early_tr(0, ti)
    for g in range(NG):
        v_gates_done = emit_A_late(g)
        bret = emit_B(g, v_gates_done)
        v_hq_p1, v_hq_done, b_quiesce, v_wtd_done = bret
        gst["prev_bq"] = b_quiesce

        _pa_at = {1: 0, 4: 1, 7: 2, 10: 3}
        _pb_at = {2: 0, 5: 1, 8: 2, 11: 3}
        _tr_at = {4: 0, 7: 1, 10: 2, 13: 3}

        def ilv(osub, g=g):
            if g + 1 < NG:
                if osub in _tr_at:
                    emit_A_early_tr(g + 1, _tr_at[osub])
                if osub in _pa_at:
                    emit_A_early_tile(g + 1, _pa_at[osub])
                if osub in _pb_at:
                    emit_A_early_tile_b(g + 1, _pb_at[osub])
        emit_C(g, v_wtd_done, v_hq_p1, v_hq_done, b_quiesce, ilv)

    final_waits = [(sOut, p.v(sOut))]
    for sl_ in (xt_slots, w1slots, w2slots, wpslots, wst, wgst, wpst, kvslots):
        final_waits += [(sem, p.v(sem)) for sem in sl_.ld if p.v(sem) > 0]
    p.step("sync", lambda e: None, waits=final_waits)

    with nc.Block() as block:
        p.emit(block)
    nc.finalize()
    return nc


_NC_CACHE = None


def kernel(**ins):
    global _NC_CACHE
    xa = np.ascontiguousarray(ins["L0_attn_x"], dtype=np.float32)
    lat = np.ascontiguousarray(ins["L0_attn_latent_queries"],
                               dtype=np.float32).reshape(NL, NH * HD)
    wk = np.ascontiguousarray(ins["L0_attn_Wk"], dtype=np.float32)
    wv = np.ascontiguousarray(ins["L0_attn_Wv"], dtype=np.float32)
    wg = np.ascontiguousarray(ins["L0_attn_Wg"], dtype=np.float32)
    wp = np.ascontiguousarray(ins["L0_attn_Wp"], dtype=np.float32)
    w1 = np.ascontiguousarray(ins["L0_ffn_w_gate_up"], dtype=np.float32)
    w2 = np.ascontiguousarray(ins["L0_ffn_w_down"], dtype=np.float32)

    if _NC_CACHE is None:
        _NC_CACHE = build_program()
    nc = _NC_CACHE

    in_maps = []
    for c in range(8):
        b = c // 4
        r0 = (c % 4) * TT
        in_maps.append({
            "x": xa[b, r0:r0 + TT, :],
            "xh": xa[b, 0:HT, :],
            "lat": lat,
            "wk": wk, "wv": wv, "wg": wg, "wp": wp,
            "w1": w1, "w2": w2,
        })
    res = run_bass_kernel_spmd(nc, in_maps, core_ids=list(range(8)))
    out = np.empty((2, 8192, C), dtype=np.float32)
    for c in range(8):
        b = c // 4
        r0 = (c % 4) * TT
        out[b, r0:r0 + TT, :] = np.asarray(res.results[c]["out"])
    return out


# version 20
# speedup vs baseline: 1.0705x; 1.0019x over previous
import numpy as np
import ml_dtypes
import concourse.bacc as bacc
import concourse.mybir as mybir
from concourse.bass_utils import run_bass_kernel_spmd

F32 = mybir.dt.float32
BF16 = mybir.dt.bfloat16
ALU = mybir.AluOpType
AF = mybir.ActivationFunctionType
AX = mybir.AxisListType

MAGIC = 12582912.0
SCALE = float(1.0 / np.sqrt(128.0))
EPS_LN = 1e-5
INV127 = float(np.float32(1.0) / np.float32(127.0))
EPS_Q = 1e-08
PI = float(np.pi)
TWO_PI = float(2.0 * np.pi)
LN1E4 = float(np.log(10000.0))
NEGF = -3.0e38
POSF = 3.0e38

C = 2048
TT = 2048
NG = 4
GT = TT // NG
NTI = GT // 128
I_FFN = 5632
NI = I_FFN // 128
NH, HD, NL, HT = 16, 128, 64, 64
N1B = 2 * I_FFN // 128


def _register_rint_op():
    import concourse.dve_ops as dve_ops
    from concourse.dve_op import DveOp
    from concourse.dve_spec import Spec, Src0, Src1, C0, lower
    from concourse.dve_table_gen import dve_ver_for
    for op in dve_ops.OPS:
        if op.name == "RINT_SCALE_TT":
            return op
    name = "RINT_SCALE_TT"
    spec = Spec(body=(Src0 * Src1 + C0) - C0)
    ver = dve_ver_for("TRN2")
    compiled = lower(spec, ver=ver, perf_en=False, name=name)
    op = DveOp(name, spec, subdim=False, uops_sha={ver: compiled.sha(ver)},
               reference=lambda in0, in1, s0, s1, imm2: np.rint(in0 * in1))
    row = len(dve_ops.OPS) + dve_ops._CUSTOM_DVE_ROW_BASE
    assert row <= 31, "no free custom-DVE opcode row"
    dve_ops.OPS.append(op)
    dve_ops._SUB_OPCODE_FOR_NAME[name] = row
    dve_ops.CUSTOM_DVE_SPECS[name] = spec
    dve_ops.CUSTOM_DVE_REFS[name] = op.ref
    return op


RINT = _register_rint_op()


class P:
    def __init__(self, nc):
        self.nc = nc
        self.steps = []
        self.count = {}

    def sem(self, name):
        s = self.nc.alloc_semaphore(name)
        self.count[s] = 0
        return s

    def step(self, eng, fn, waits=(), inc=None):
        incs = []
        if inc is not None:
            if not isinstance(inc, (list, tuple)):
                inc = [(inc, 1)]
            for s, n in inc:
                self.count[s] += n
                incs.append((s, n))

        def run(e, fn=fn, waits=tuple(waits), incs=tuple(incs)):
            for s, v in waits:
                if v > 0:
                    e.wait_ge(s, v)
            last = fn(e)
            if incs:
                if last is None:
                    last = e.drain() if hasattr(e, "drain") else e.nop()
                for s, n in incs:
                    last.then_inc(s, n)
        self.steps.append((eng, run))
        return self.count[incs[0][0]] if incs else None

    def v(self, s):
        return self.count[s]

    def emit(self, block):
        steps = self.steps

        def mk(name):
            def f(eng):
                for e, run in steps:
                    if e == name:
                        run(eng)
            return f
        block.sync(mk("sync"))
        block.scalar(mk("scalar"))
        block.vector(mk("vector"))
        block.tensor(mk("tensor"))
        block.gpsimd(mk("gpsimd"))


class Slots:
    def __init__(self, p, name, n):
        self.p, self.n = p, n
        self.ld = [p.sem(f"{name}_ld{j}") for j in range(n)]
        self.fr = [p.sem(f"{name}_fr{j}") for j in range(n)]
        self.il = 0
        self.iu = 0

    def load(self, queue, dst_fn, src_ap, extra_waits=()):
        j = self.il % self.n
        r = self.il // self.n
        waits = list(extra_waits)
        if r > 0:
            waits.append((self.fr[j], r))
            waits.append((self.ld[j], 16 * r))
        self.p.step(queue,
                    lambda e, j=j, src_ap=src_ap, dst_fn=dst_fn:
                        e.dma_start(out=dst_fn(j), in_=src_ap),
                    waits=waits, inc=[(self.ld[j], 16)])
        self.il += 1
        return j

    def wait_of_next(self):
        j = self.iu % self.n
        return (self.ld[j], 16 * (self.iu // self.n + 1))

    def slot(self):
        return self.iu % self.n

    def done(self, eng):
        j = self.iu % self.n
        self.p.step(eng, lambda e: None, inc=[(self.fr[j], 1)])
        self.iu += 1


def build_program():
    nc = bacc.Bacc("TRN2", target_bir_lowering=False)

    x_in = nc.dram_tensor("x", [TT, C], F32, kind="ExternalInput").ap()
    xh_in = nc.dram_tensor("xh", [HT, C], F32, kind="ExternalInput").ap()
    lat_in = nc.dram_tensor("lat", [NL, C], F32, kind="ExternalInput").ap()
    wk_in = nc.dram_tensor("wk", [C, C], F32, kind="ExternalInput").ap()
    wv_in = nc.dram_tensor("wv", [C, C], F32, kind="ExternalInput").ap()
    wg_in = nc.dram_tensor("wg", [C, NH * NL], F32, kind="ExternalInput").ap()
    wp_in = nc.dram_tensor("wp", [C, C], F32, kind="ExternalInput").ap()
    w1_in = nc.dram_tensor("w1", [2 * I_FFN, C], F32, kind="ExternalInput").ap()
    w2_in = nc.dram_tensor("w2", [C, I_FFN], F32, kind="ExternalInput").ap()
    out_hbm = nc.dram_tensor("out", [TT, C], F32, kind="ExternalOutput").ap()

    w1qT_hbm = nc.dram_tensor("w1qT", [N1B, 128, C], BF16, kind="Internal").ap()
    w2qT_hbm = nc.dram_tensor("w2qT", [16, 128, I_FFN], BF16, kind="Internal").ap()
    wpS_hbm = nc.dram_tensor("wpS", [16, 128, C], BF16, kind="Internal").ap()
    wgS_hbm = nc.dram_tensor("wgS", [C, NH * NL], BF16, kind="Internal").ap()

    id_bf_h = nc.inline_tensor(np.eye(128, dtype=ml_dtypes.bfloat16), "id_bf_h").ap()
    id_f32_h = nc.inline_tensor(np.eye(128, dtype=np.float32), "id_f32_h").ap()
    maskv = np.where(np.arange(HT)[None, :] <= np.arange(NL)[:, None],
                     np.float32(0.0), np.float32(-1000000000.0)).astype(np.float32)
    mask_h = nc.inline_tensor(maskv, "cmask").ap()
    tpart_h = nc.inline_tensor(
        np.broadcast_to(np.arange(HT, dtype=np.float32)[:, None],
                        (HT, HT)).copy(), "tpart_h").ap()
    fidx_h = nc.inline_tensor(
        np.broadcast_to(np.arange(HT, dtype=np.float32)[None, :],
                        (HT, HT)).copy(), "fidx_h").ap()
    ones_h = nc.inline_tensor(np.ones((1, 128), dtype=np.float32), "ones_h").ap()

    p = P(nc)

    id_bf = nc.alloc_sbuf_tensor("id_bf", [128, 128], BF16)
    id_f32 = nc.alloc_sbuf_tensor("id_f32", [128, 128], F32)
    attn_sb = nc.alloc_sbuf_tensor("attn_sb", [128, 8 * HD], BF16)
    xqT = nc.alloc_sbuf_tensor("xqT", [128, 16 * GT], BF16)
    wtd = nc.alloc_sbuf_tensor("wtd", [128, 16 * GT], BF16)
    colp = nc.alloc_sbuf_tensor("colp", [128, 24], F32)
    bn6 = nc.alloc_sbuf_tensor("bn6", [128, 24], F32)
    mv2 = nc.alloc_sbuf_tensor("mv2", [128, 2], F32)
    ones_sb = nc.alloc_sbuf_tensor("ones_sb", [1, 128], F32)
    sw1_all = nc.alloc_sbuf_tensor("sw1_all", [128, 2 * NI], F32)
    sw2_all = nc.alloc_sbuf_tensor("sw2_all", [128, 16], F32)

    ps_f32 = nc.alloc_psum_tensor("ps_f32", [128, 3072], F32)
    ps_bf = nc.alloc_psum_tensor("ps_bf", [128, 2048], BF16)

    def psf(bank, w=512, off=0):
        return ps_f32[:, bank * 512 + off: bank * 512 + off + w]

    def psb(bank, w=1024):
        return ps_bf[:, bank * 1024: bank * 1024 + w]

    sP = p.sem("sP")
    sV = p.sem("sV")
    sS = p.sem("sS")
    sG = p.sem("sG")
    sQs = p.sem("sQs")
    sQid = p.sem("sQid")
    sOut = p.sem("sOut")
    sW1w = [p.sem(f"sW1w{i}") for i in range(4)]
    sW2w = p.sem("sW2w")
    sWpw = p.sem("sWpw")
    sWgw = p.sem("sWgw")
    sXg = p.sem("sXg")
    sWgsL = p.sem("sWgsL")

    psb_guard = {0: (None, 0), 1: (None, 0)}
    wtr_guard = {0: None, 1: None}

    def bcast(col_ap, n):
        return col_ap.broadcast_to([col_ap.shape[0], n])

    ARENA_B = (nc.sbuf_top - nc.sbuf_base - 64) // 32 * 32
    arena_h = nc.sbuf_tensor("arena", [128, ARENA_B], mybir.dt.uint8).__enter__()
    AB = nc.lookup_mloc(arena_h).addr

    def _bytes(shape, dt):
        n = 1
        for d in shape[1:]:
            n *= d
        return n * mybir.dt.size(dt)

    def sb_at(off, name, shape, dt):
        assert off % 32 == 0 and off + _bytes(shape, dt) <= ARENA_B, \
            (name, off, _bytes(shape, dt), ARENA_B)
        return nc.alloc_sbuf_tensor_at(name, shape, dt, offset=AB + off)

    _cur = [0]

    def sbuf(name, shape, dt, side="left"):
        off = (_cur[0] + 31) // 32 * 32
        _cur[0] = off + _bytes(shape, dt)
        return sb_at(off, name, shape, dt)

    def scope_mark():
        return _cur[0]

    def scope_close(mark):
        _cur[0] = mark

    def tr8(src_fn, nblk, n_rows, bank, waits):
        w = list(waits) + [psb_guard[bank]]

        def f(e, nblk=nblk, n_rows=n_rows, bank=bank, src_fn=src_fn):
            for b in range(nblk):
                e.transpose(psb(bank)[0:128, b * 128: b * 128 + n_rows],
                            src_fn(b), id_bf[0:n_rows, 0:n_rows])
            return None
        return p.step("tensor", f, waits=w, inc=sP)

    def tr8_evac(dst_ap, nblk, n_rows, bank, pe_t):
        def f(e, dst_ap=dst_ap, nblk=nblk, n_rows=n_rows, bank=bank):
            srcs = psb(bank)[:].rearrange("p (b w) -> p b w", w=128)[
                :, 0:nblk, 0:n_rows]
            e.tensor_copy(dst_ap, srcs)
            e.drain()
            return None
        t = p.step("vector", f, waits=[(sP, pe_t)], inc=sV)
        psb_guard[bank] = (sV, t)
        return t

    def tr8_evac_s(dst_ap, nblk, n_rows, bank, pe_t):
        def f(e, dst_ap=dst_ap, nblk=nblk, n_rows=n_rows, bank=bank):
            srcs = psb(bank)[:].rearrange("p (b w) -> p b w", w=128)[
                :, 0:nblk, 0:n_rows]
            e.activation(out=dst_ap, in_=srcs, func=AF.Copy)
            return None
        t = p.step("scalar", f, waits=[(sP, pe_t)], inc=sS)
        psb_guard[bank] = (sS, t)
        return t

    p.step("scalar", lambda e: e.dma_start(out=id_bf[:], in_=id_bf_h[:, :]),
           inc=[(sQid, 16)])
    p.step("scalar", lambda e: e.dma_start(out=id_f32[:], in_=id_f32_h[:, :]),
           inc=[(sQid, 16)])
    qs_consts = p.v(sQid)

    m_p0 = scope_mark()
    xh_sb = sbuf("xh_sb", [HT, C], F32)
    lat_sb = sbuf("lat_sb", [NL, C], F32)
    mask_sb = sbuf("mask_sb", [NL, HT], F32)
    xnh = sbuf("xnh", [HT, C], F32)
    xnh_bf = sbuf("xnh_bf", [HT, C], BF16)
    xnhT = sbuf("xnhT", [128, 16 * HT], BF16)
    lat_bf = sbuf("lat_bf", [NL, C], BF16)
    latT = sbuf("latT", [128, 16 * NL], BF16)
    kvf = sbuf("kvf", [128, 2 * C], F32)
    kvb = sbuf("kvb", [128, 2 * C], BF16)
    k_tm = sbuf("k_tm", [HT, C], F32)
    v_tm = sbuf("v_tm", [HT, C], F32)
    k_bf = sbuf("k_bf", [HT, C], BF16)
    v_bf = sbuf("v_bf", [HT, C], BF16)
    kT = sbuf("kT", [128, 16 * HT], BF16)
    rt_t = sbuf("rt_t", [HT, HT], F32)
    rt_i = sbuf("rt_i", [HT, HT], F32)
    rt_k = sbuf("rt_k", [HT, HT], F32)
    cos_t = sbuf("cos_t", [HT, HT], F32)
    sin_t = sbuf("sin_t", [HT, HT], F32)
    re1 = sbuf("re1", [HT, C // 2], F32)
    re2 = sbuf("re2", [HT, C // 2], F32)
    re3 = sbuf("re3", [HT, C // 2], F32)
    sc_sb = sbuf("sc_sb", [NL, 2 * HT], F32)
    w_bf = sbuf("w_bf", [NL, 2 * HT], BF16)
    wT_bf = sbuf("wT_bf", [HT, 2 * NL], BF16)
    wpf = sbuf("wpf", [128, 2 * C], F32)
    wqb = sbuf("wqb", [128, 2 * C], BF16)
    wtr = sbuf("wtr", [128, 2 * C], BF16)
    wpfp = sbuf("wpfp", [128, 2 * C], F32)
    wtrp = sbuf("wtrp", [128, 2 * C], BF16)
    wgf = sbuf("wgf", [128, 2 * NH * NL], F32)
    wgb = sbuf("wgb", [128, 2 * NH * NL], BF16)

    def f(e):
        e.memset(colp[0:1, 23:24], 0.25)
        e.drain()
        return None
    v_seed = p.step("vector", f, inc=sV)

    def f(e):
        for fn in (AF.Copy, AF.Identity, AF.Sqrt, AF.Exp, AF.Sin, AF.Silu):
            e.activation(out=colp[0:1, 22:23], in_=colp[0:1, 23:24], func=fn)
            e.drain()
        return None
    p.step("scalar", f, waits=[(sV, v_seed)], inc=sS)

    p.step("scalar", lambda e: e.dma_start(out=xh_sb[:], in_=xh_in[:, :]),
           inc=[(sQs, 16)])
    p.step("scalar", lambda e: e.dma_start(out=lat_sb[:], in_=lat_in[:, :]),
           inc=[(sQs, 16)])
    p.step("scalar", lambda e: e.dma_start(out=mask_sb[:], in_=mask_h[:, :]),
           inc=[(sQs, 16)])
    qs_head = p.v(sQs)

    wst = Slots(p, "wst", 2)
    wgst = Slots(p, "wgst", 2)
    wpst = Slots(p, "wpst", 2)
    wg_slot_guard = {0: 0, 1: 0}
    wp_slot_guard = {0: 0, 1: 0}
    GW = NH * NL

    def emit_wg_chunk(i):
        wgst.load("sync", lambda j: wgf[:, j * GW: (j + 1) * GW],
                  wg_in[i * 128:(i + 1) * 128, :])
        wsem, wval = wgst.wait_of_next()
        j = wgst.slot()

        def f(e, j=j):
            e.tensor_scalar(out=wgb[:, j * GW:(j + 1) * GW],
                            in0=wgf[:, j * GW:(j + 1) * GW],
                            scalar1=SCALE, scalar2=None, op0=ALU.mult)
            e.drain()
            return None
        v_c = p.step("vector", f,
                     waits=[(wsem, wval), (sWgw, wg_slot_guard[j])], inc=sV)
        wgst.done("vector")
        p.step("gpsimd",
               lambda e, i=i, j=j: e.dma_start(
                   out=wgS_hbm[i * 128:(i + 1) * 128, :],
                   in_=wgb[:, j * GW:(j + 1) * GW]),
               waits=[(sV, v_c), (sWgw, p.v(sWgw))], inc=[(sWgw, 16)])
        wg_slot_guard[j] = p.v(sWgw)

    def emit_wp_chunk(ic):
        wpst.load("sync", lambda j: wpfp[:, j * C:(j + 1) * C],
                  wp_in[ic * 128:(ic + 1) * 128, :])
        wsem, wval = wpst.wait_of_next()
        j = wpst.slot()
        s_c = p.step("scalar",
                     lambda e, j=j: e.activation(
                         out=wtrp[:, j * C:(j + 1) * C],
                         in_=wpfp[:, j * C:(j + 1) * C], func=AF.Copy),
                     waits=[(wsem, wval), (sWpw, wp_slot_guard[j])], inc=sS)
        wpst.done("scalar")
        p.step("gpsimd",
               lambda e, ic=ic, j=j: e.dma_start(
                   out=wpS_hbm[ic, :, :],
                   in_=wtrp[:, j * C:(j + 1) * C]),
               waits=[(sS, s_c), (sWpw, p.v(sWpw))], inc=[(sWpw, 16)])
        wp_slot_guard[j] = p.v(sWpw)


    trslot = [0]
    state = {"tr": {0: 0, 1: 0}}

    def prep_chunk(src_row_ap, W, dst_img_row, done_sem, cpar,
                   b_wpf, b_wqb, b_wtr, scol=None, extra_load_waits=()):
        base = 8 + 4 * cpar
        wst.load("sync", lambda j: b_wpf[:, j * W: j * W + W], src_row_ap,
                 extra_waits=extra_load_waits)
        wsem, wval = wst.wait_of_next()
        j = wst.slot()

        def f(e, j=j, W=W, base=base, cpar=cpar, b_wpf=b_wpf, b_wqb=b_wqb,
              scol=scol):
            e.tensor_reduce(out=colp[:, base:base + 1],
                            in_=b_wpf[:, j * W: j * W + W],
                            axis=AX.X, op=ALU.max, apply_absolute_value=True)
            e.drain()
            e.reciprocal(colp[:, base + 2:base + 3], colp[:, base:base + 1])
            e.drain()
            e.tensor_scalar(out=colp[:, base + 2:base + 3],
                            in0=colp[:, base + 2:base + 3], scalar1=127.0,
                            scalar2=1e8, op0=ALU.mult, op1=ALU.min)
            e.drain()
            e.reciprocal(scol, colp[:, base + 2:base + 3])
            e.drain()
            e._custom_dve(RINT, out=b_wqb[:, cpar * W:cpar * W + W],
                          in0=b_wpf[:, j * W: j * W + W],
                          in1=bcast(colp[:, base + 2:base + 3], W), s0=MAGIC)
            e.drain()
            return None
        v_q = p.step("vector", f,
                     waits=[(wsem, wval), (sP, state["tr"][cpar])], inc=sV)
        wst.done("vector")

        ts = trslot[0]
        trslot[0] ^= 1
        nsub = W // 128
        last_s = None
        pe_t = None
        for lo in range(0, nsub, 8):
            hi = min(lo + 8, nsub)
            bank = (lo // 8) % 2
            extra = [(sV, v_q), (sQid, qs_consts)]
            if wtr_guard[ts] is not None:
                extra.append(wtr_guard[ts])
            pe_t = tr8(lambda b, lo=lo, cpar=cpar, W=W, b_wqb=b_wqb:
                       b_wqb[:, cpar * W + (lo + b) * 128:
                             cpar * W + (lo + b + 1) * 128],
                       hi - lo, 128, bank, extra)
            dst = b_wtr[:, ts * W + lo * 128: ts * W + hi * 128].rearrange(
                "p (b w) -> p b w", w=128)
            last_s = tr8_evac_s(dst, hi - lo, 128, bank, pe_t)
        state["tr"][cpar] = pe_t

        wv_t = p.step("gpsimd",
                      lambda e, ts=ts, W=W, dst_img_row=dst_img_row, b_wtr=b_wtr:
                          e.dma_start(out=dst_img_row,
                                      in_=b_wtr[:, ts * W: ts * W + W]),
                      waits=[(sS, last_s), (done_sem, p.v(done_sem))],
                      inc=[(done_sem, 16)])
        wtr_guard[ts] = (done_sem, wv_t)

    def f(e):
        for q in range(4):
            e.bn_stats(bn6[0:HT, q * 6:(q + 1) * 6],
                       xh_sb[:, q * 512:(q + 1) * 512])
        e.drain()
        e.bn_aggr(mv2[0:HT, :], bn6[0:HT, :])
        e.drain()
        return None
    v_hstats = p.step("vector", f, waits=[(sQs, qs_head)], inc=sV)

    def f(e):
        e.tensor_scalar(out=colp[0:HT, 0:1], in0=mv2[0:HT, 1:2],
                        scalar1=EPS_LN, scalar2=None, op0=ALU.add)
        e.drain()
        return None
    v_heps = p.step("vector", f, waits=[(sV, v_hstats)], inc=sV)
    s_hstd = p.step("scalar",
                    lambda e: e.activation(out=colp[0:HT, 0:1], in_=colp[0:HT, 0:1],
                                           func=AF.Sqrt),
                    waits=[(sV, v_heps)], inc=sS)

    def f(e):
        e.reciprocal(colp[0:HT, 1:2], colp[0:HT, 0:1])
        e.drain()
        e.tensor_scalar(out=colp[0:HT, 2:3], in0=mv2[0:HT, 0:1],
                        scalar1=colp[0:HT, 1:2], scalar2=-1.0,
                        op0=ALU.mult, op1=ALU.mult)
        e.drain()
        return None
    v_hcol = p.step("vector", f, waits=[(sS, s_hstd)], inc=sV)

    s_xnh = p.step("scalar",
                   lambda e: e.activation(out=xnh[:], in_=xh_sb[:], func=AF.Identity,
                                          bias=colp[0:HT, 2:3],
                                          scale=colp[0:HT, 1:2]),
                   waits=[(sV, v_hcol)], inc=sS)

    def f(e):
        e.tensor_copy(xnh_bf[:], xnh[:])
        e.tensor_scalar(out=lat_bf[:], in0=lat_sb[:], scalar1=SCALE,
                        scalar2=None, op0=ALU.mult)
        e.drain()
        return None
    v_hbf = p.step("vector", f, waits=[(sS, s_xnh), (sQs, qs_head)], inc=sV)

    def tr16_into(src_sb, n_rows, dst_sb, dstw, wait_v):
        last = None
        for half in range(2):
            bank = half
            pe_t = tr8(lambda b, half=half, src_sb=src_sb, n_rows=n_rows:
                       src_sb[0:n_rows,
                              (half * 8 + b) * 128:(half * 8 + b + 1) * 128],
                       8, n_rows, bank, [(sV, wait_v), (sQid, qs_consts)])
            dst = dst_sb[:].rearrange("p (i w) -> p i w", w=dstw)[
                :, half * 8:(half + 1) * 8, 0:n_rows]
            last = tr8_evac(dst, 8, n_rows, bank, pe_t)
        return last

    v_xnhT = tr16_into(xnh_bf, HT, xnhT, HT, v_hbf)
    v_latT = tr16_into(lat_bf, NL, latT, NL, v_hbf)

    kvslots = Slots(p, "kv", 2)
    kvb_guard = {0: 0, 1: 0}
    kv_pe = [None]

    def emit_kv_chunk(w_in, hc):
        kvslots.load("sync", lambda j: kvf[:, j * C:(j + 1) * C],
                     w_in[hc * 128:(hc + 1) * 128, :])
        wsem, wval = kvslots.wait_of_next()
        j = kvslots.slot()
        s_c = p.step("scalar",
                     lambda e, j=j: e.activation(
                         out=kvb[:, j * C:(j + 1) * C],
                         in_=kvf[:, j * C:(j + 1) * C], func=AF.Copy),
                     waits=[(wsem, wval), (sP, kvb_guard[j])], inc=sS)
        kvslots.done("scalar")

        def f(e, hc=hc, j=j):
            for oc in range(4):
                e.matmul(psf(oc)[0:HT, :],
                         xnhT[:, hc * HT:(hc + 1) * HT],
                         kvb[:, j * C + oc * 512: j * C + (oc + 1) * 512],
                         start=(hc == 0), stop=(hc == 15),
                         skip_group_check=True)
            return None
        pe_t = p.step("tensor", f, waits=[(sS, s_c), (sV, v_xnhT)], inc=sP)
        kvb_guard[j] = pe_t
        kv_pe[0] = pe_t

    def emit_kv_evac(dst):
        def f(e, dst=dst):
            e.tensor_copy(dst[:], ps_f32[0:HT, 0:2048])
            e.drain()
            return None
        return p.step("vector", f, waits=[(sP, kv_pe[0])], inc=sV)

    w1_order = []
    for k in range(NI):
        w1_order.append(k)
        w1_order.append(NI + k)
    for idx, blk in enumerate(w1_order):
        quarter = idx // 22
        prep_chunk(w1_in[blk * 128:(blk + 1) * 128, :], C,
                   w1qT_hbm[idx, :, :], sW1w[quarter], idx % 2,
                   wpf, wqb, wtr, scol=sw1_all[:, idx:idx + 1])
        if idx % 5 == 1 and idx // 5 < 16:
            emit_wg_chunk(idx // 5)
        if idx % 5 == 3 and idx // 5 < 16:
            emit_wp_chunk(idx // 5)
        if 40 <= idx < 56:
            emit_kv_chunk(wk_in, idx - 40)
        if idx == 56:
            v_k = emit_kv_evac(k_tm)
        if 57 <= idx < 73:
            emit_kv_chunk(wv_in, idx - 57)
        if idx == 73:
            v_v = emit_kv_evac(v_tm)
    w1_written = [p.v(s) for s in sW1w]
    wg_written = p.v(sWgw)
    wp_written = p.v(sWpw)

    p.step("scalar", lambda e: e.dma_start(out=rt_t[:], in_=tpart_h[:, :]),
           inc=[(sQid, 16)])
    p.step("scalar", lambda e: e.dma_start(out=rt_i[:], in_=fidx_h[:, :]),
           inc=[(sQid, 16)])
    p.step("scalar", lambda e: e.dma_start(out=ones_sb[:], in_=ones_h[:, :]),
           inc=[(sQid, 16)])
    qs_iota = p.v(sQid)

    def f(e):
        e.tensor_scalar(out=rt_i[:], in0=rt_i[:],
                        scalar1=float(-2.0 * LN1E4 / 128.0), scalar2=None,
                        op0=ALU.mult)
        e.drain()
        return None
    v_invfs = p.step("vector", f, waits=[(sQid, qs_iota)], inc=sV)
    s_invf = p.step("scalar",
                    lambda e: e.activation(out=rt_i[:], in_=rt_i[:], func=AF.Exp),
                    waits=[(sV, v_invfs)], inc=sS)

    def f(e):
        e.tensor_tensor(out=rt_t[:], in0=rt_t[:], in1=rt_i[:], op=ALU.mult)
        e.memset(rt_k[:], float(1.0 / TWO_PI))
        e.drain()
        e._custom_dve(RINT, out=rt_i[:], in0=rt_t[:], in1=rt_k[:], s0=MAGIC)
        e.drain()
        e.scalar_tensor_tensor(out=sin_t[:], in0=rt_i[:], scalar=-TWO_PI,
                               in1=rt_t[:], op0=ALU.mult, op1=ALU.add)
        e.drain()
        e.tensor_scalar(out=sin_t[:], in0=sin_t[:], scalar1=PI, scalar2=-PI,
                        op0=ALU.min, op1=ALU.max)
        e.tensor_scalar(out=rt_t[:], in0=rt_t[:], scalar1=float(PI / 2),
                        scalar2=None, op0=ALU.add)
        e.drain()
        e._custom_dve(RINT, out=rt_i[:], in0=rt_t[:], in1=rt_k[:], s0=MAGIC)
        e.drain()
        e.scalar_tensor_tensor(out=cos_t[:], in0=rt_i[:], scalar=-TWO_PI,
                               in1=rt_t[:], op0=ALU.mult, op1=ALU.add)
        e.drain()
        e.tensor_scalar(out=cos_t[:], in0=cos_t[:], scalar1=PI, scalar2=-PI,
                        op0=ALU.min, op1=ALU.max)
        e.drain()
        return None
    v_ropearg = p.step("vector", f, waits=[(sS, s_invf)], inc=sV)

    def f(e):
        e.activation(out=sin_t[:], in_=sin_t[:], func=AF.Sin)
        e.drain()
        e.activation(out=cos_t[:], in_=cos_t[:], func=AF.Sin)
        e.drain()
        return None
    s_tab = p.step("scalar", f, waits=[(sV, v_ropearg)], inc=sS)

    def f(e):
        k3 = k_tm[:].rearrange("t (h d2 two) -> t h d2 two", h=NH, two=2)
        ke = k3[:, :, :, 0:1].squeeze(3)
        ko = k3[:, :, :, 1:2].squeeze(3)
        cosb = cos_t[:].unsqueeze(1).broadcast_to([HT, NH, HT])
        sinb = sin_t[:].unsqueeze(1).broadcast_to([HT, NH, HT])
        r1 = re1[:].rearrange("t (h f) -> t h f", h=NH)
        r2 = re2[:].rearrange("t (h f) -> t h f", h=NH)
        r3 = re3[:].rearrange("t (h f) -> t h f", h=NH)
        e.tensor_tensor(out=r1, in0=ke, in1=cosb, op=ALU.mult)
        e.tensor_tensor(out=r2, in0=ko, in1=sinb, op=ALU.mult)
        e.tensor_tensor(out=r3, in0=ke, in1=sinb, op=ALU.mult)
        e.drain()
        e.tensor_sub(re1[:], re1[:], re2[:])
        e.drain()
        e.tensor_tensor(out=r2, in0=ko, in1=cosb, op=ALU.mult)
        e.drain()
        e.tensor_add(re2[:], re2[:], re3[:])
        e.drain()
        e.tensor_copy(ke, r1)
        e.tensor_copy(ko, r2)
        e.drain()
        e.tensor_copy(k_bf[:], k_tm[:])
        e.tensor_copy(v_bf[:], v_tm[:])
        e.drain()
        return None
    v_rope = p.step("vector", f, waits=[(sS, s_tab), (sV, v_v)], inc=sV)

    v_kT = tr16_into(k_bf, HT, kT, HT, v_rope)

    prev_vsc = {}
    prev_vat = {}
    for hh in range(NH):
        par = hh % 2
        bsc = 4 - par
        bat = 5 - 3 * par
        cb = 4 if par == 0 else 17

        def f(e, hh=hh, bsc=bsc):
            e.matmul(psf(bsc)[0:NL, 0:HT],
                     latT[:, hh * NL:(hh + 1) * NL],
                     kT[:, hh * HT:(hh + 1) * HT],
                     start=True, stop=True, skip_group_check=True)
            return None
        pe_sc = p.step("tensor", f,
                       waits=[(sV, v_kT), (sV, v_latT),
                              (sV, prev_vsc.get(hh - 2, 0))],
                       inc=sP)

        def f(e, par=par, bsc=bsc, cb=cb):
            sc = sc_sb[:, par * HT:(par + 1) * HT]
            e.tensor_tensor(out=sc, in0=psf(bsc)[0:NL, 0:HT],
                            in1=mask_sb[:], op=ALU.add)
            e.drain()
            e.tensor_reduce(out=colp[0:NL, cb:cb + 1], in_=sc, axis=AX.X,
                            op=ALU.max)
            e.drain()
            e.tensor_scalar(out=sc, in0=sc, scalar1=colp[0:NL, cb:cb + 1],
                            scalar2=None, op0=ALU.subtract)
            e.drain()
            return None
        v_sc = p.step("vector", f, waits=[(sP, pe_sc)], inc=sV)
        prev_vsc[hh] = v_sc

        s_exp = p.step("scalar",
                       lambda e, par=par, cb=cb: e.activation(
                           out=sc_sb[:, par * HT:(par + 1) * HT],
                           in_=sc_sb[:, par * HT:(par + 1) * HT],
                           func=AF.Exp,
                           accum_out=colp[0:NL, cb + 1:cb + 2]),
                       waits=[(sV, v_sc)], inc=sS)

        def f(e, par=par, cb=cb):
            e.reciprocal(colp[0:NL, cb + 2:cb + 3], colp[0:NL, cb + 1:cb + 2])
            e.drain()
            e.tensor_scalar(out=w_bf[:, par * HT:(par + 1) * HT],
                            in0=sc_sb[:, par * HT:(par + 1) * HT],
                            scalar1=colp[0:NL, cb + 2:cb + 3],
                            scalar2=None, op0=ALU.mult)
            e.drain()
            return None
        v_w = p.step("vector", f, waits=[(sS, s_exp)], inc=sV)

        def f(e, par=par):
            e.transpose(psb(par)[0:HT, 0:NL],
                        w_bf[:, par * HT:(par + 1) * HT], id_bf[0:NL, 0:NL])
            return None
        pe_wT = p.step("tensor", f,
                       waits=[(sV, v_w), psb_guard[par]], inc=sP)

        def f(e, par=par):
            e.tensor_copy(wT_bf[:, par * NL:(par + 1) * NL],
                          psb(par)[0:HT, 0:NL])
            e.drain()
            return None
        v_wT = p.step("vector", f, waits=[(sP, pe_wT)], inc=sV)
        psb_guard[par] = (sV, v_wT)

        def f(e, hh=hh, par=par, bat=bat):
            e.matmul(psf(bat)[0:NL, 0:HD], wT_bf[:, par * NL:(par + 1) * NL],
                     v_bf[:, hh * HD:(hh + 1) * HD],
                     start=True, stop=True, skip_group_check=True)
            return None
        pe_at = p.step("tensor", f,
                       waits=[(sV, v_wT), (sV, prev_vat.get(hh - 2, 0))],
                       inc=sP)

        def f(e, hh=hh, bat=bat):
            e.tensor_copy(attn_sb[(hh % 2) * NL:(hh % 2 + 1) * NL,
                                  (hh // 2) * HD:(hh // 2 + 1) * HD],
                          psf(bat)[0:NL, 0:HD])
            e.drain()
            return None
        v_at = p.step("vector", f, waits=[(sP, pe_at)], inc=sV)
        prev_vat[hh] = v_at
    v_attn_done = p.v(sV)




    scope_close(m_p0)
    m_w2 = scope_mark()
    wpf2 = sbuf("wpf2", [128, 2 * I_FFN], F32)
    wqb2 = sbuf("wqb2", [128, 2 * I_FFN], BF16)
    wtr2 = sbuf("wtr2", [128, 2 * I_FFN], BF16)
    wtr_guard[0] = wtr_guard[1] = (sWpw, wp_written)
    w2_gate = [(sV, p.v(sV)), (sS, p.v(sS)), (sP, p.v(sP)), (sG, p.v(sG))]
    for i in range(16):
        prep_chunk(w2_in[i * 128:(i + 1) * 128, :], I_FFN,
                   w2qT_hbm[i, :, :], sW2w, i % 2,
                   wpf2, wqb2, wtr2, scol=sw2_all[:, i:i + 1],
                   extra_load_waits=(w2_gate if i == 0 else ()))
    w2_written = p.v(sW2w)
    scope_close(m_w2)
    prep_quiesce = ([(sV, p.v(sV)), (sS, p.v(sS)), (sP, p.v(sP)),
                     (sW2w, w2_written), (sWpw, wp_written)] +
                    [(sW1w[q], w1_written[q]) for q in range(4)])

    xt_slots = Slots(p, "xt", 2)
    w1slots = Slots(p, "w1s", 4)
    w2slots = Slots(p, "w2s", 2)
    wpslots = Slots(p, "wps", 2)

    xt_sb = sb_at(0, "xt_sb", [128, 2 * C], F32)
    xn_f = sb_at(16384, "xn_f", [128, C], F32)
    xn_bfile = sb_at(24576, "xn_bf", [128, C], BF16)
    xq_bfile = sb_at(28672, "xq_bf", [128, C], BF16)
    xnT = sb_at(32768, "xnT", [128, 16 * GT], BF16)
    gatT = sb_at(49152, "gatT", [128, 8 * GT], BF16)
    gat_s = sb_at(57344, "gat_s", [128, NH * NL], F32)
    gat_e = sb_at(61440, "gat_e", [128, NH * NL], F32)
    gat_bf = sb_at(65536, "gat_bf", [128, NH * NL], BF16)
    gat_c = sb_at(67584, "gat_c", [128, NH], F32)
    wgs_sb = sb_at(67648, "wgs", [128, 16 * 1024], BF16)
    h_fm = sb_at(0, "h_fm", [128, NI * GT], F32)
    w1b_sb = sb_at(90112, "w1b", [128, 4 * C], BF16)
    silu_sb = sb_at(106496, "silu", [128, 4 * 512], F32)
    mx_sb = sb_at(161792, "mx", [128, GT], F32)
    mn_sb = sb_at(163840, "mn", [128, GT], F32)
    w2b_sb = sb_at(49152, "w2b", [128, 2 * I_FFN], BF16)
    wpb_sb = sb_at(165888, "wpb", [128, 2 * C], BF16)
    out_sb = sb_at(79872, "outb", [128, 2 * 512], F32)
    xg_sb = sb_at(83968, "xg", [128, 2 * 512], F32)
    res_sb = sb_at(88064, "res", [128, 2 * 512], F32)
    sbc = sb_at(114688, "sbc", [128, GT], F32)
    hq = sb_at(116736, "hq", [128, NI * GT], BF16)

    gst = {"last_gates_pe": 0, "last_peo": 0,
           "last_v_o": v_attn_done, "last_v_r": v_attn_done,
           "prev_bq": list(prep_quiesce)}
    v_tile_tr = {}
    v_tile_qx = {}
    tile_jx = {}

    def emit_A_early_tile(g, ti):
        t0 = g * GT
        row = t0 + ti * 128
        gatew = (prep_quiesce if g == 0 else gst["prev_bq"]) if ti == 0 else ()
        xt_slots.load("scalar",
                      lambda jj: xt_sb[:, jj * C:(jj + 1) * C],
                      x_in[row:row + 128, :], extra_waits=gatew)
        wsem, wval = xt_slots.wait_of_next()
        jx = xt_slots.slot()

        def f(e, jx=jx):
            for q in range(4):
                e.bn_stats(bn6[:, q * 6:(q + 1) * 6],
                           xt_sb[:, jx * C + q * 512: jx * C + (q + 1) * 512])
            e.drain()
            e.bn_aggr(mv2[:], bn6[:])
            e.drain()
            return None
        v_st = p.step("vector", f, waits=[(wsem, wval)], inc=sV)

        def f(e):
            e.tensor_scalar(out=colp[:, 0:1], in0=mv2[:, 1:2],
                            scalar1=EPS_LN, scalar2=None, op0=ALU.add)
            e.drain()
            return None
        v_eps = p.step("vector", f, waits=[(sV, v_st)], inc=sV)
        s_sd = p.step("scalar",
                      lambda e: e.activation(out=colp[:, 0:1],
                                             in_=colp[:, 0:1],
                                             func=AF.Sqrt),
                      waits=[(sV, v_eps)], inc=sS)

        def f(e):
            e.reciprocal(colp[:, 1:2], colp[:, 0:1])
            e.drain()
            e.tensor_scalar(out=colp[:, 2:3], in0=mv2[:, 0:1],
                            scalar1=colp[:, 1:2], scalar2=-1.0,
                            op0=ALU.mult, op1=ALU.mult)
            e.drain()
            return None
        v_cl = p.step("vector", f, waits=[(sS, s_sd)], inc=sV)
        tile_jx[(g, ti)] = (jx, v_cl)

    def emit_A_early_tile_b(g, ti):
        (jx, v_cl) = tile_jx[(g, ti)]
        s_xn = p.step("scalar",
                      lambda e, jx=jx: e.activation(
                          out=xn_f[:], in_=xt_sb[:, jx * C:(jx + 1) * C],
                          func=AF.Identity, bias=colp[:, 2:3],
                          scale=colp[:, 1:2]),
                      waits=[(sV, v_cl)], inc=sS)
        xt_slots.done("scalar")

        def f(e):
            e.tensor_reduce(out=colp[:, 3:4], in_=xn_f[:], axis=AX.X,
                            op=ALU.max, apply_absolute_value=True)
            e.drain()
            e.reciprocal(colp[:, 5:6], colp[:, 3:4])
            e.drain()
            e.tensor_scalar(out=colp[:, 5:6], in0=colp[:, 5:6],
                            scalar1=127.0, scalar2=1e8,
                            op0=ALU.mult, op1=ALU.min)
            e.drain()
            e.reciprocal(colp[:, 4:5], colp[:, 5:6])
            e.drain()
            e.tensor_copy(xn_bfile[:], xn_f[:])
            e._custom_dve(RINT, out=xq_bfile[:], in0=xn_f[:],
                          in1=bcast(colp[:, 5:6], C), s0=MAGIC)
            e.drain()
            e.tensor_scalar(out=xq_bfile[:], in0=xq_bfile[:],
                            scalar1=colp[:, 4:5], scalar2=None,
                            op0=ALU.mult)
            e.drain()
            return None
        v_qx = p.step("vector", f, waits=[(sS, s_xn)], inc=sV)
        v_tile_qx[(g, ti)] = v_qx

    def emit_A_early_tr(g, ti):
        v_qx = v_tile_qx[(g, ti)]
        for (src, dst) in ((xn_bfile, xnT), (xq_bfile, xqT)):
            for half in range(2):
                bank = half
                pe_t = tr8(lambda b, src=src, half=half:
                           src[:, (half * 8 + b) * 128:
                               (half * 8 + b + 1) * 128],
                           8, 128, bank, [(sV, v_qx)])
                dstv = dst[:].rearrange("p (i w) -> p i w", w=GT)[
                    :, half * 8:(half + 1) * 8, ti * 128:(ti + 1) * 128]
                tr8_evac(dstv, 8, 128, bank, pe_t)
        v_tile_tr[(g, ti)] = p.v(sV)

    def emit_A_late(g):
        for hcc in range(16):
            lw = []
            if hcc == 0:
                lw = [(sWgsL, 256 * g), (sWgw, wg_written),
                      (sOut, 256 * g), (sV, gst["last_v_o"]),
                      (sP, gst["last_peo"])]
                lw += [(sem, p.v(sem)) for sem in w1slots.ld]
                if g == 0:
                    lw += prep_quiesce
            p.step("gpsimd",
                   lambda e, hcc=hcc: e.dma_start(
                       out=wgs_sb[:, hcc * 1024:(hcc + 1) * 1024],
                       in_=wgS_hbm[hcc * 128:(hcc + 1) * 128, :]),
                   waits=lw, inc=[(sWgsL, 16)])
        wgs_loaded = p.v(sWgsL)

        prev_v_gbf = 0
        for ti in range(NTI):
            wts = [(sWgsL, wgs_loaded), (sV, v_tile_tr[(g, ti)]),
                   (sV, prev_v_gbf)]
            if ti == 0:
                wts.append((sV, gst["last_v_o"]))

            def f(e, ti=ti):
                for half in range(2):
                    for hc in range(16):
                        e.matmul(psf(4 + half)[:, :],
                                 xnT[:, hc * GT + ti * 128:
                                     hc * GT + (ti + 1) * 128],
                                 wgs_sb[:, hc * 1024 + half * 512:
                                        hc * 1024 + (half + 1) * 512],
                                 start=(hc == 0), stop=(hc == 15),
                                 skip_group_check=True)
                return None
            pe_g = p.step("tensor", f, waits=wts, inc=sP)
            gst["last_gates_pe"] = pe_g

            def f(e):
                g3v = ps_f32[:, 4 * 512: 4 * 512 + 1024].rearrange(
                    "p (h l) -> p h l", l=NL)
                e.tensor_reduce(out=gat_c[:], in_=g3v, axis=AX.X, op=ALU.max)
                e.drain()
                e.tensor_tensor(out=gat_s[:].rearrange("p (h l) -> p h l", l=NL),
                                in0=g3v,
                                in1=gat_c[:].unsqueeze(2).broadcast_to(
                                    [128, NH, NL]),
                                op=ALU.subtract)
                e.drain()
                return None
            v_gs = p.step("vector", f, waits=[(sP, pe_g)], inc=sV)

            s_ge = p.step("scalar",
                          lambda e: e.activation(
                              out=gat_e[:], in_=gat_s[:], func=AF.Exp),
                          waits=[(sV, v_gs)], inc=sS)

            def f(e):
                e.tensor_reduce(out=gat_c[:],
                                in_=gat_e[:].rearrange("p (h l) -> p h l", l=NL),
                                axis=AX.X, op=ALU.add)
                e.drain()
                e.reciprocal(gat_c[:], gat_c[:])
                e.drain()
                e.tensor_tensor(out=gat_bf[:].rearrange("p (h l) -> p h l", l=NL),
                                in0=gat_e[:].rearrange("p (h l) -> p h l", l=NL),
                                in1=gat_c[:].unsqueeze(2).broadcast_to(
                                    [128, NH, NL]),
                                op=ALU.mult)
                e.drain()
                return None
            v_gbf = p.step("vector", f, waits=[(sS, s_ge)], inc=sV)
            prev_v_gbf = v_gbf

            pe_t = tr8(lambda b: gat_bf[:, b * 128:(b + 1) * 128],
                       8, 128, 0, [(sV, v_gbf)])
            dstv = gatT[:].rearrange("p (i w) -> p i w", w=GT)[
                :, 0:8, ti * 128:(ti + 1) * 128]
            tr8_evac(dstv, 8, 128, 0, pe_t)
        v_gates_done = p.v(sV)
        return v_gates_done

    def emit_M5_pair(g, hp, v_gates_done, prev_v_wtd):
        wts = [(sV, v_gates_done), (sV, prev_v_wtd.get(hp - 1, 0))]

        def f(e, hp=hp):
            for hh in (2 * hp, 2 * hp + 1):
                e.matmul(psf(4 + (hh % 2))[:, :],
                         attn_sb[(hh % 2) * NL:(hh % 2 + 1) * NL,
                                 (hh // 2) * HD:(hh // 2 + 1) * HD],
                         gatT[(hh % 2) * NL:(hh % 2 + 1) * NL,
                              (hh // 2) * GT:(hh // 2 + 1) * GT],
                         start=True, stop=True, skip_group_check=True)
            return None
        pe_w = p.step("tensor", f, waits=wts, inc=sP)

        def f(e, hp=hp):
            e.tensor_copy(wtd[:, 2 * hp * GT:(2 * hp + 2) * GT],
                          ps_f32[:, 4 * 512:6 * 512])
            e.drain()
            return None
        v_wtd_t = p.step("vector", f, waits=[(sP, pe_w)], inc=sV)
        prev_v_wtd[hp] = v_wtd_t

    def emit_B(g, v_gates_done):
        v_ln_done = v_tile_tr[(g, NTI - 1)]

        def f(e):
            e.memset(mx_sb[:], NEGF)
            e.memset(mn_sb[:], POSF)
            e.drain()
            return None
        v_mm = p.step("vector", f, waits=list(gst["prev_bq"]), inc=sV)

        pair_si = {}
        pair_vh = {}
        prev_v_wtd = {}
        v_wtd_done = None
        for k in range(NI):
            jbs = []
            lws = []
            for half in range(2):
                img_row = 2 * k + half
                quarter = img_row // 22
                lw = [(sW1w[quarter], w1_written[quarter])] if g == 0 else []
                if k == 0 and half == 0:
                    lw += [(sP, max(gst["last_gates_pe"], gst["last_peo"]))]
                    if g == 0:
                        lw += prep_quiesce
                w1slots.load("sync",
                             lambda jj, img_row=img_row:
                                 w1b_sb[:, jj * C:(jj + 1) * C],
                             w1qT_hbm[img_row, :, :], extra_waits=lw)
                ju = w1slots.iu + half
                jbs.append(ju % w1slots.n)
                lws.append((w1slots.ld[ju % w1slots.n],
                            16 * (ju // w1slots.n + 1)))

            wts = lws + [(sV, v_ln_done)]
            if k == 0:
                wts += gst["prev_bq"]
            if (k - 2) in pair_si:
                wts.append((sS, pair_si[k - 2]))

            def f(e, jbs=tuple(jbs), k=k):
                for half in range(2):
                    jb = jbs[half]
                    bank = 2 * (k % 2) + half
                    for hc in range(16):
                        e.matmul(psf(bank)[:, :],
                                 w1b_sb[:, jb * C + hc * 128:
                                        jb * C + (hc + 1) * 128],
                                 xqT[:, hc * GT:(hc + 1) * GT],
                                 start=(hc == 0), stop=(hc == 15),
                                 skip_group_check=True)
                return None
            pe_t = p.step("tensor", f, waits=wts, inc=sP)
            w1slots.done("tensor")
            w1slots.done("tensor")
            pe_pair = [pe_t, pe_t]

            sl = k % 2
            swts = [(sP, pe_pair[1]), (sV, pair_vh.get(k - 2, 0))]
            s_si = p.step("scalar",
                          lambda e, sl=sl, k=k: e.activation(
                              out=silu_sb[:, sl * 512:(sl + 1) * 512],
                              in_=psf(2 * (k % 2))[:, :], func=AF.Silu,
                              scale=sw1_all[:, 2 * k:2 * k + 1]),
                          waits=swts, inc=sS)
            s_up = p.step("scalar",
                          lambda e, sl=sl, k=k: e.activation(
                              out=silu_sb[:, (2 + sl) * 512:(3 + sl) * 512],
                              in_=psf(2 * (k % 2) + 1)[:, :], func=AF.Identity,
                              scale=sw1_all[:, 2 * k + 1:2 * k + 2]),
                          waits=[], inc=sS)
            pair_si[k] = s_up

            vwts = [(sS, s_up)]
            if k == 39:
                vwts.append((sOut, 256 * g))

            def f(e, k=k, sl=sl):
                e.tensor_tensor(out=h_fm[:, k * GT:(k + 1) * GT],
                                in0=silu_sb[:, sl * 512:(sl + 1) * 512],
                                in1=silu_sb[:, (2 + sl) * 512:(3 + sl) * 512],
                                op=ALU.mult)
                e.drain()
                return None
            v_h = p.step("vector", f, waits=vwts, inc=sV)
            pair_vh[k] = v_h

            def f(e, k=k):
                e.tensor_tensor(out=mx_sb[:], in0=mx_sb[:],
                                in1=h_fm[:, k * GT:(k + 1) * GT], op=ALU.max)
                e.tensor_tensor(out=mn_sb[:], in0=mn_sb[:],
                                in1=h_fm[:, k * GT:(k + 1) * GT], op=ALU.min)
                e.drain()
                return None
            v_mm = p.step("vector", f, waits=[(sV, v_mm)] if k == 0 else [],
                          inc=sV)

            if 1 <= k <= 8:
                emit_M5_pair(g, k - 1, v_gates_done, prev_v_wtd)
                if k == 8:
                    v_wtd_done = p.v(sV)

        def f(e):
            e.tensor_scalar(out=mn_sb[:], in0=mn_sb[:], scalar1=-1.0,
                            scalar2=None, op0=ALU.mult)
            e.drain()
            e.tensor_tensor(out=mx_sb[:], in0=mx_sb[:], in1=mn_sb[:],
                            op=ALU.max)
            e.drain()
            return None
        v_M = p.step("vector", f, waits=[(sV, v_mm)], inc=sV)

        def f(e):
            for ti in range(NTI):
                e.transpose(psf(4, 128, ti * 128)[:, :],
                            mx_sb[:, ti * 128:(ti + 1) * 128], id_f32[:])
            return None
        pe_Mt = p.step("tensor", f, waits=[(sV, max(v_M, v_wtd_done))],
                       inc=sP)

        def f(e):
            e.tensor_reduce(out=colp[:, 8:12],
                            in_=psf(4)[:, :].rearrange("p (t w) -> p t w", w=128),
                            axis=AX.X, op=ALU.max)
            e.drain()
            e.reciprocal(colp[:, 16:20], colp[:, 8:12])
            e.drain()
            e.tensor_scalar(out=colp[:, 16:20], in0=colp[:, 16:20],
                            scalar1=127.0, scalar2=1e8,
                            op0=ALU.mult, op1=ALU.min)
            e.drain()
            e.reciprocal(colp[:, 12:16], colp[:, 16:20])
            e.drain()
            return None
        v_cols = p.step("vector", f, waits=[(sP, pe_Mt)], inc=sV)

        def f(e):
            for c in range(4):
                e.transpose(psf(5, 128, c * 128)[0:1, :],
                            colp[:, 12 + c:13 + c], id_f32[:])
            for c in range(4):
                e.transpose(psf(4, 128, c * 128)[0:1, :],
                            colp[:, 16 + c:17 + c], id_f32[:])
            return None
        pe_sT = p.step("tensor", f, waits=[(sV, v_cols)], inc=sP)

        def f(e):
            e.tensor_copy(mn_sb[0:1, 0:512], psf(5)[0:1, :])
            e.tensor_copy(mx_sb[0:1, 0:512], psf(4)[0:1, :])
            e.drain()
            return None
        v_sT = p.step("vector", f, waits=[(sP, pe_sT)], inc=sV)

        def f(e):
            for ti in range(NTI):
                e.matmul(psf(4, 128, ti * 128)[:, :], ones_sb[0:1, :],
                         mn_sb[0:1, ti * 128:(ti + 1) * 128],
                         start=True, stop=True, skip_group_check=True)
            for ti in range(NTI):
                e.matmul(psf(5, 128, ti * 128)[:, :], ones_sb[0:1, :],
                         mx_sb[0:1, ti * 128:(ti + 1) * 128],
                         start=True, stop=True, skip_group_check=True)
            return None
        pe_bc = p.step("tensor", f, waits=[(sV, v_sT)], inc=sP)

        def f(e):
            e.tensor_copy(sbc[:, 0:GT], psf(4)[:, :])
            e.tensor_copy(mn_sb[:], psf(5)[:, :])
            e.drain()
            return None
        v_sbc = p.step("vector", f, waits=[(sP, pe_bc)], inc=sV)

        def f(e, k0=24, k1=NI):
            e._custom_dve(
                RINT,
                out=hq[:, k0 * GT:k1 * GT].rearrange("p (k t) -> p k t", t=GT),
                in0=h_fm[:, k0 * GT:k1 * GT].rearrange("p (k t) -> p k t", t=GT),
                in1=mn_sb[:].unsqueeze(1).broadcast_to(
                    [128, k1 - k0, GT]),
                s0=MAGIC)
            e.drain()
            return None
        v_hq_p1 = p.step("vector", f, waits=[(sV, v_sbc)], inc=sV)

        def f(e, k0=0, k1=24):
            e._custom_dve(
                RINT,
                out=hq[:, k0 * GT:k1 * GT].rearrange("p (k t) -> p k t", t=GT),
                in0=h_fm[:, k0 * GT:k1 * GT].rearrange("p (k t) -> p k t", t=GT),
                in1=mn_sb[:].unsqueeze(1).broadcast_to(
                    [128, k1 - k0, GT]),
                s0=MAGIC)
            e.drain()
            return None
        v_hq_done = p.step("vector", f, waits=[], inc=sV)
        b_quiesce = [(sV, v_hq_done), (sG, p.v(sG)), (sS, p.v(sS)),
                     (sP, p.v(sP))]
        qz_p1 = [(sV, v_hq_p1), (sG, p.v(sG)), (sS, p.v(sS)),
                 (sP, p.v(sP))]
        s_b01_free = pair_si[NI - 2]
        return qz_p1, v_hq_done, b_quiesce, v_wtd_done, s_b01_free

    def emit_C(g, v_wtd_done, qz_p1, v_hq_done, b_quiesce, s_b01_free,
               interleave):
        t0 = g * GT
        base_v_o = gst["last_v_o"]
        base_v_r = gst["last_v_r"]
        prev_v_o = {}
        prev_v_r = {}
        pend = []
        last_peo_l = [gst["last_peo"]]

        def emit_phase2(st):
            (osub, jr, tb, v_r) = st
            wts_o = [(sV, max(v_r, prev_v_o.get(osub - 2, base_v_o)))]

            def f(e, jr=jr, tb=tb):
                for ti in range(NTI):
                    e.transpose(psf(tb, 128, ti * 128)[:, :],
                                res_sb[:, jr * 512 + ti * 128:
                                       jr * 512 + (ti + 1) * 128],
                                id_f32[:])
                return None
            pe_o = p.step("tensor", f, waits=wts_o, inc=sP)

            sg = osub % 2
            v_wts = [(sP, pe_o), (sXg, 256 * g + 16 * (osub + 1)),
                     (sOut, 256 * g + (16 * (osub - 1) if osub >= 2 else 0))]

            def f(e, osub=osub, tb=tb, sg=sg):
                src = psf(tb)[:, :].rearrange("p (t w) -> p t w", w=128)
                xsrc = xg_sb[:, sg * 512:(sg + 1) * 512].rearrange(
                    "p (t w) -> p t w", w=128)
                dst = out_sb[:, sg * 512:(sg + 1) * 512].rearrange(
                    "p (t w) -> p t w", w=128)
                e.tensor_tensor(out=dst, in0=src, in1=xsrc, op=ALU.add)
                e.drain()
                return None
            v_o = p.step("vector", f, waits=v_wts, inc=sV)
            prev_v_o[osub] = v_o

            p.step("scalar",
                   lambda e, osub=osub, sg=sg, t0=t0: e.dma_start(
                       out=out_hbm[t0:t0 + GT,
                                   osub * 128:(osub + 1) * 128].rearrange(
                                       "(t q) c -> q t c", q=128),
                       in_=out_sb[:, sg * 512:(sg + 1) * 512].rearrange(
                           "p (t w) -> p t w", w=128)),
                   waits=[(sV, v_o), (sOut, 256 * g + 16 * osub)],
                   inc=[(sOut, 16)])
            return pe_o

        for osub in range(16):
            lw2 = list(qz_p1) if osub == 0 else []
            lwp = [(sP, gst["last_peo"])] if osub == 0 else []
            if g == 0 and osub == 0:
                lw2.append((sW2w, w2_written))
                lwp.append((sWpw, wp_written))
            wp_src = wpS_hbm[:, :, osub * 128:(osub + 1) * 128].transpose(
                (1, 0, 2))
            wpslots.load("sync",
                         lambda jj: wpb_sb[:, jj * C:(jj + 1) * C].rearrange(
                             "p (i m) -> p i m", m=128),
                         wp_src, extra_waits=lwp)
            wpw, wpv = wpslots.wait_of_next()
            jp = wpslots.slot()
            w2slots.load("sync",
                         lambda jj: w2b_sb[:, jj * I_FFN:(jj + 1) * I_FFN],
                         w2qT_hbm[osub, :, :], extra_waits=lw2)
            w2w, w2v = w2slots.wait_of_next()
            j2 = w2slots.slot()

            bank_w = osub % 2
            bank_f = 2 + (osub % 2)
            wts_w = [(wpw, wpv),
                     (sV, max(v_wtd_done,
                              prev_v_r.get(osub - 2, base_v_r)))]
            if osub < 2:
                wts_w.append((sS, s_b01_free))

            def f(e, jp=jp, bank_w=bank_w):
                for hc in range(16):
                    e.matmul(psf(bank_w)[:, :],
                             wpb_sb[:, jp * C + hc * 128: jp * C + (hc + 1) * 128],
                             wtd[:, hc * GT:(hc + 1) * GT],
                             start=(hc == 0), stop=(hc == 15),
                             skip_group_check=True)
                return None
            pe_wp = p.step("tensor", f, waits=wts_w, inc=sP)
            wpslots.done("tensor")

            wts_f = [(w2w, w2v),
                     (sV, max(v_hq_done, prev_v_r.get(osub - 2, base_v_r)))]

            def f(e, j2=j2, bank_f=bank_f):
                for ic in range(NI):
                    e.matmul(psf(bank_f)[:, :],
                             w2b_sb[:, j2 * I_FFN + ic * 128:
                                    j2 * I_FFN + (ic + 1) * 128],
                             hq[:, ic * GT:(ic + 1) * GT],
                             start=(ic == 0), stop=(ic == NI - 1),
                             skip_group_check=True)
                return None
            pe_f2 = p.step("tensor", f, waits=wts_f, inc=sP)
            w2slots.done("tensor")

            if len(pend) >= 1:
                last_peo_l[0] = emit_phase2(pend.pop(0))

            xg_wts = [(sXg, 256 * g + 16 * osub),
                      (sV, prev_v_o.get(osub - 2, base_v_o))]
            if osub == 0:
                xg_wts += qz_p1 + [(sP, last_peo_l[0])]
            p.step("sync",
                   lambda e, osub=osub, t0=t0: e.dma_start(
                       out=xg_sb[:, (osub % 2) * 512:
                                 (osub % 2 + 1) * 512].rearrange(
                           "p (t w) -> p t w", w=128),
                       in_=x_in[t0:t0 + GT,
                                osub * 128:(osub + 1) * 128].rearrange(
                           "(t q) c -> q t c", q=128)),
                   waits=xg_wts, inc=[(sXg, 16)])

            jr = osub % 2

            def f(e, jr=jr, bank_w=bank_w, bank_f=bank_f, osub=osub):
                e.tensor_scalar(out=res_sb[:, jr * 512:(jr + 1) * 512],
                                in0=psf(bank_f)[:, :],
                                scalar1=sw2_all[:, osub:osub + 1],
                                scalar2=None, op0=ALU.mult)
                e.drain()
                e.tensor_tensor(out=res_sb[:, jr * 512:(jr + 1) * 512],
                                in0=res_sb[:, jr * 512:(jr + 1) * 512],
                                in1=sbc[:, 0:GT], op=ALU.mult)
                e.drain()
                e.tensor_tensor(out=res_sb[:, jr * 512:(jr + 1) * 512],
                                in0=res_sb[:, jr * 512:(jr + 1) * 512],
                                in1=psf(bank_w)[:, :], op=ALU.add)
                e.drain()
                return None
            v_r = p.step("vector", f, waits=[(sP, pe_f2)], inc=sV)
            prev_v_r[osub] = v_r
            pend.append((osub, jr, 4 + (osub % 2), v_r))

            interleave(osub)

        while pend:
            last_peo_l[0] = emit_phase2(pend.pop(0))
        gst["last_peo"] = last_peo_l[0]
        gst["last_v_o"] = p.v(sV)
        gst["last_v_r"] = p.v(sV)

    for ti in range(NTI):
        emit_A_early_tile(0, ti)
        emit_A_early_tile_b(0, ti)
        emit_A_early_tr(0, ti)
    for g in range(NG):
        v_gates_done = emit_A_late(g)
        bret = emit_B(g, v_gates_done)
        v_hq_p1, v_hq_done, b_quiesce, v_wtd_done, s_b01_free = bret
        gst["prev_bq"] = b_quiesce

        _pa_at = {1: 0, 4: 1, 7: 2, 10: 3}
        _pb_at = {2: 0, 5: 1, 8: 2, 11: 3}
        _tr_at = {4: 0, 7: 1, 10: 2, 13: 3}

        def ilv(osub, g=g):
            if g + 1 < NG:
                if osub in _tr_at:
                    emit_A_early_tr(g + 1, _tr_at[osub])
                if osub in _pa_at:
                    emit_A_early_tile(g + 1, _pa_at[osub])
                if osub in _pb_at:
                    emit_A_early_tile_b(g + 1, _pb_at[osub])
        emit_C(g, v_wtd_done, v_hq_p1, v_hq_done, b_quiesce, s_b01_free,
               ilv)

    final_waits = [(sOut, p.v(sOut))]
    for sl_ in (xt_slots, w1slots, w2slots, wpslots, wst, wgst, wpst, kvslots):
        final_waits += [(sem, p.v(sem)) for sem in sl_.ld if p.v(sem) > 0]
    p.step("sync", lambda e: None, waits=final_waits)

    with nc.Block() as block:
        p.emit(block)
    nc.finalize()
    return nc


_NC_CACHE = None


def kernel(**ins):
    global _NC_CACHE
    xa = np.ascontiguousarray(ins["L0_attn_x"], dtype=np.float32)
    lat = np.ascontiguousarray(ins["L0_attn_latent_queries"],
                               dtype=np.float32).reshape(NL, NH * HD)
    wk = np.ascontiguousarray(ins["L0_attn_Wk"], dtype=np.float32)
    wv = np.ascontiguousarray(ins["L0_attn_Wv"], dtype=np.float32)
    wg = np.ascontiguousarray(ins["L0_attn_Wg"], dtype=np.float32)
    wp = np.ascontiguousarray(ins["L0_attn_Wp"], dtype=np.float32)
    w1 = np.ascontiguousarray(ins["L0_ffn_w_gate_up"], dtype=np.float32)
    w2 = np.ascontiguousarray(ins["L0_ffn_w_down"], dtype=np.float32)

    if _NC_CACHE is None:
        _NC_CACHE = build_program()
    nc = _NC_CACHE

    in_maps = []
    for c in range(8):
        b = c // 4
        r0 = (c % 4) * TT
        in_maps.append({
            "x": xa[b, r0:r0 + TT, :],
            "xh": xa[b, 0:HT, :],
            "lat": lat,
            "wk": wk, "wv": wv, "wg": wg, "wp": wp,
            "w1": w1, "w2": w2,
        })
    res = run_bass_kernel_spmd(nc, in_maps, core_ids=list(range(8)))
    out = np.empty((2, 8192, C), dtype=np.float32)
    for c in range(8):
        b = c // 4
        r0 = (c % 4) * TT
        out[b, r0:r0 + TT, :] = np.asarray(res.results[c]["out"])
    return out
